# version 17
# baseline (speedup 1.0000x reference)
import numpy as np
import concourse.bass as bass
import concourse.mybir as mybir
from contextlib import ExitStack

F32 = mybir.dt.float32
BF16 = mybir.dt.bfloat16
AF = mybir.ActivationFunctionType
ALU = mybir.AluOpType
AX = mybir.AxisListType

D = 1024
HID = 4096


def bcast_row(ap, nparts):
    dims = [list(d) for d in ap.ap]
    return bass.AP(ap.tensor, ap.offset, [dims[0], [0, nparts]] + dims[1:])


def host_consts():
    import ml_dtypes
    B = ml_dtypes.bfloat16
    idn = np.zeros((128, 192), np.float32)
    for i in range(128):
        idn[i, i] = 1.0
    for i in range(64):
        idn[64 + i, 128 + i] = 1.0
    mask = np.zeros((128, 512), np.float32)
    for p in range(128):
        t8 = p // 16
        for o in range(4):
            mask[p, o * 128 + t8 * 16: o * 128 + t8 * 16 + 16] = 1.0
    bones = np.zeros((128, 8), np.float32)
    for p in range(128):
        bones[p, p // 16] = 1.0
    return {"idn": idn.astype(B), "mask5": mask.astype(B), "bones": bones.astype(B)}


class Ctr:
    def __init__(self, es, nc, name):
        self.sem = es.enter_context(nc.semaphore(name))
        self.n = 0

    def inc(self, instr, k=1):
        instr.then_inc(self.sem, k)
        self.n += k
        return self.n


def build(T=4096, p1mt=512, p2mt=512, mt_per_hg=2, dbg=False):
    n1 = T // p1mt
    n2 = T // p2mt
    nst1 = p1mt // 128
    nst2 = p2mt // 128
    chtok = min(256, p1mt)
    nch_a = p1mt // chtok
    noct = chtok // 8
    npair = chtok // 2
    nw = noct // 8
    n_hg = max(1, n2 // mt_per_hg)
    mt_hg = n2 // n_hg

    nc = bass.Bass("TRN2", target_bir_lowering=False)

    def dram(nm, sh, dt, kind=None):
        t = nc.dram_tensor(nm, sh, dt, kind=kind) if kind else nc.dram_tensor(nm, sh, dt)
        return t.ap()

    x_in = dram("x", [T, D], F32, "ExternalInput")
    wq_in = dram("wq", [D, D], F32, "ExternalInput")
    bq_in = dram("bq", [D], F32, "ExternalInput")
    wdn_in = dram("wdown", [D, 256], F32, "ExternalInput")
    bdn_in = dram("bdown", [256], F32, "ExternalInput")
    wup_in = dram("wup", [256, 2048], F32, "ExternalInput")
    bup_in = dram("bup", [2048], F32, "ExternalInput")
    wo_in = dram("wout", [D, D], F32, "ExternalInput")
    bo_in = dram("bout", [D], F32, "ExternalInput")
    lns_in = dram("ln_scale", [D], F32, "ExternalInput")
    lnb_in = dram("ln_bias", [D], F32, "ExternalInput")
    awt_in = dram("attn_wT", [D, D], F32, "ExternalInput")
    ab_in = dram("attn_b", [D], F32, "ExternalInput")
    nwt_in = dram("next_wT", [D, D], F32, "ExternalInput")
    nb_in = dram("next_b", [D], F32, "ExternalInput")
    w1_in = [dram("w1a", [D, HID], F32, "ExternalInput"), dram("w1b", [D, HID], F32, "ExternalInput")]
    b1_in = [dram("b1a", [HID], F32, "ExternalInput"), dram("b1b", [HID], F32, "ExternalInput")]
    w2_in = [dram("w2a", [HID, D], F32, "ExternalInput"), dram("w2b", [HID, D], F32, "ExternalInput")]
    b2_in = [dram("b2a", [D], F32, "ExternalInput"), dram("b2b", [D], F32, "ExternalInput")]
    c_in = dram("cb_", [D], F32, "ExternalInput")
    wm_in = dram("wmod", [D, 6 * D], F32, "ExternalInput")
    idn_in = dram("idn", [128, 192], BF16, "ExternalInput")
    msk_in = dram("mask5", [128, 512], BF16, "ExternalInput")
    bon_in = dram("bones", [128, 8], BF16, "ExternalInput")
    x2s = dram("x2s", [T, D], BF16)
    y_out = dram("y", [T, D], F32, "ExternalOutput")
    dbg_t = {}
    if dbg:
        for nm, sh in [("d_h1t", [128, 8 * p1mt]), ("d_q", [128, 8 * p1mt]), ("d_kv", [128, 16 * p1mt]),
                       ("d_ao", [128, 8 * p1mt]), ("d_xat", [128, 8 * p2mt]), ("d_eo", [128, 8 * p2mt]),
                       ("d_eob", [128, 8 * p2mt]), ("d_fb", [128, 8 * p2mt]), ("d_hq", [128, 8 * p2mt]),
                       ("d_x2", [T, D])]:
            dbg_t[nm] = dram(nm, sh, F32, "ExternalOutput")

    es = ExitStack()
    S = lambda nm, sh, dt: es.enter_context(nc.sbuf_tensor(nm, sh, dt))
    P = lambda nm, sh, dt: es.enter_context(nc.psum_tensor(nm, sh, dt))

    IDN = S("IDN", [128, 192], BF16)
    MSK = S("MSK", [128, 512], BF16)
    BON = S("BON", [128, 8], BF16)
    EYEF = S("EYEF", [128, 128], F32)
    AWT = S("AWT", [128, 8 * D], BF16)
    NWT = S("NWT", [128, 8 * D], BF16)
    COLS = S("COLS", [128, 144], F32)
    B1C = S("B1C", [128, 64], F32)
    B2C = S("B2C", [128, 16], F32)
    DGA = S("DGA", [128, 8 * 128], BF16)
    DGB = S("DGB", [128, 8 * 128], BF16)
    STB = S("STB", [128, D], BF16)
    BTB = S("BTB", [128, D], BF16)
    SCR = S("SCR", [128, D], BF16)
    SCR2 = S("SCR2", [128, D], BF16)
    STAT = S("STAT", [128, 4 * max(nst1, nst2)], F32)
    STATB = S("STATB", [128, 4 * max(nst1, nst2)], F32)
    TPP = P("TPP", [128, 1024], BF16)
    MMP = [P("MMP0", [128, 512], F32), P("MMP1", [128, 512], F32)]
    X2P = P("X2P", [128, 512], F32)
    es_p1w = ExitStack()
    S1w = lambda nm, sh, dt: es_p1w.enter_context(nc.sbuf_tensor(nm, sh, dt))
    WQ = S1w("WQ", [128, 8 * D], BF16)
    WDN = S1w("WDN", [128, 8 * 256], BF16)
    WUP = S1w("WUP", [128, 2 * 2048], BF16)
    WO = S1w("WO", [128, 8 * D], BF16)
    es_pro = ExitStack()
    Spro = lambda nm, sh, dt: es_pro.enter_context(nc.sbuf_tensor(nm, sh, dt))
    Ppro = lambda nm, sh, dt: es_pro.enter_context(nc.psum_tensor(nm, sh, dt))
    WMK = Spro("WMK", [128, 2 * 6144], BF16)
    SCB = Spro("SCB", [128, 8], BF16)
    BAB = Spro("BAB", [128, 16], BF16)
    MODC = Spro("MODC", [128, 48], F32)
    MODR = Spro("MODR", [8, 256], F32)
    MODP = Ppro("MODP", [128, 8 * 48], F32)
    CRP = Ppro("CRP", [128, 24], F32)
    MRP = Ppro("MRP", [8, 256], F32)

    pe = Ctr(es, nc, "pe")
    dv = Ctr(es, nc, "dv")
    ac = Ctr(es, nc, "ac")
    gp_ = Ctr(es, nc, "gp")
    scon = Ctr(es, nc, "scon")
    sspill = Ctr(es, nc, "sspill")
    sswapp = [Ctr(es, nc, "sswap0"), Ctr(es, nc, "sswap1")]
    sx2l = [Ctr(es, nc, "sx2l0"), Ctr(es, nc, "sx2l1"), Ctr(es, nc, "sx2l2"), Ctr(es, nc, "sx2l3")]
    sout = Ctr(es, nc, "sout")
    gcols = Ctr(es, nc, "gcols")
    gwm = Ctr(es, nc, "gwm")
    gwk = [Ctr(es, nc, "gwk0"), Ctr(es, nc, "gwk1")]
    gx = [Ctr(es, nc, "gx0"), Ctr(es, nc, "gx1")]
    gmx = Ctr(es, nc, "gmx")
    gzf = Ctr(es, nc, "gzf")
    gstb = Ctr(es, nc, "gstb")
    gw1 = Ctr(es, nc, "gw1")
    gw2 = Ctr(es, nc, "gw2")
    gdbg = Ctr(es, nc, "gdbg")

    block = es.enter_context(nc.Block())
    on = lambda name: getattr(block, name)
    col = lambda i, n=1: COLS[:, i:i + n]

    @on("sync")
    def _(e):
        scon.inc(e.dma_start(out=IDN[:], in_=idn_in[:]), 16)
        scon.inc(e.dma_start(out=MSK[:], in_=msk_in[:]), 16)
        scon.inc(e.dma_start(out=BON[:], in_=bon_in[:]), 16)
    ev_consts = scon.n

    def col_load(e, dst, src_vec, nchunks):
        src = bass.AP(src_vec.tensor, src_vec.offset, [[1, 128], [128, nchunks], [1, 1]])
        gcols.inc(e.dma_start(out=dst.unsqueeze(2), in_=src, allow_slow_non_contiguous=True), 16)

    @on("gpsimd")
    def _(e):
        col_load(e, col(0, 8), bq_in, 8)
        col_load(e, col(8, 2), bdn_in, 2)
        col_load(e, col(10, 16), bup_in, 16)
        col_load(e, col(26, 8), bo_in, 8)
        col_load(e, col(34, 8), ab_in, 8)
        col_load(e, col(42, 8), nb_in, 8)
        col_load(e, col(50, 8), lns_in, 8)
        col_load(e, col(58, 8), lnb_in, 8)
        col_load(e, col(114, 8), c_in, 8)
        col_load(e, B1C[:, 0:32], b1_in[0], 32)
        col_load(e, B1C[:, 32:64], b1_in[1], 32)
        col_load(e, B2C[:, 0:8], b2_in[0], 8)
        col_load(e, B2C[:, 8:16], b2_in[1], 8)
    ev_cols = gcols.n

    @on("gpsimd")
    def _(e):
        for kc in range(8):
            gwm.inc(e.dma_start(out=WQ[:, kc * D:(kc + 1) * D], in_=wq_in[128 * kc:128 * (kc + 1), :]), 16)
        for kc in range(8):
            gwm.inc(e.dma_start(out=WDN[:, kc * 256:(kc + 1) * 256], in_=wdn_in[128 * kc:128 * (kc + 1), :]), 16)
        for kc in range(2):
            gwm.inc(e.dma_start(out=WUP[:, kc * 2048:(kc + 1) * 2048], in_=wup_in[128 * kc:128 * (kc + 1), :]), 16)
        for kc in range(8):
            gwm.inc(e.dma_start(out=WO[:, kc * D:(kc + 1) * D], in_=wo_in[128 * kc:128 * (kc + 1), :]), 16)
        for kc in range(8):
            gwm.inc(e.dma_start(out=AWT[:, kc * D:(kc + 1) * D], in_=awt_in[128 * kc:128 * (kc + 1), :]), 16)
        for kc in range(8):
            gwm.inc(e.dma_start(out=NWT[:, kc * D:(kc + 1) * D], in_=nwt_in[128 * kc:128 * (kc + 1), :]), 16)
    ev_wmain = gwm.n

    @on("scalar")
    def _(e):
        e.wait_ge(gcols.sem, ev_cols)
        ac.inc(e.activation(col(122, 8), col(114, 8), AF.Silu))
    ev_silu = ac.n

    @on("vector")
    def _(e):
        e.wait_ge(ac.sem, ev_silu)
        e.wait_ge(scon.sem, ev_consts)
        dv.inc(e.tensor_copy(SCB[:], col(122, 8)))
        dv.inc(e.tensor_copy(EYEF[:], IDN[:, 0:128]))
    ev_scb = dv.n

    ev_wmk, ev_modk = [], []
    for kc in range(8):
        @on("gpsimd")
        def _(e, kc=kc):
            if kc >= 2:
                e.wait_ge(pe.sem, ev_modk[kc - 2])
            gwk[kc % 2].inc(e.dma_start(out=WMK[:, (kc % 2) * 6144:(kc % 2 + 1) * 6144],
                               in_=wm_in[128 * kc:128 * (kc + 1), :]), 16)
        ev_wmk.append(gwk[kc % 2].n)

        @on("tensor")
        def _(e, kc=kc):
            if kc == 0:
                e.wait_ge(dv.sem, ev_scb)
            e.wait_ge(gwk[kc % 2].sem, ev_wmk[kc])
            for m in range(48):
                mm = e.matmul(MODP[:, kc * 48 + m:kc * 48 + m + 1],
                              WMK[:, (kc % 2) * 6144 + 128 * m:(kc % 2) * 6144 + 128 * (m + 1)],
                              SCB[:, kc:kc + 1], start=True, stop=True)
            pe.inc(mm)
        ev_modk.append(pe.n)
    ev_mod = pe.n

    @on("vector")
    def _(e):
        e.wait_ge(pe.sem, ev_mod)
        step = lambda i: (dv.inc(i), e.wait_ge(dv.sem, dv.n))
        step(e.tensor_copy(MODC[:], MODP[:, 0:48]))
        for kc in range(1, 8):
            step(e.tensor_add(MODC[:], MODC[:], MODP[:, kc * 48:(kc + 1) * 48]))
        dv.inc(e.engine_nop())
    ev_modc_dv = dv.n

    @on("vector")
    def _(e):
        e.wait_ge(dv.sem, ev_modc_dv)
        step = lambda i: (dv.inc(i), e.wait_ge(dv.sem, dv.n))
        e.tensor_copy(col(66, 8), MODC[:, 0:8])
        e.tensor_scalar_add(col(74, 8), MODC[:, 8:16], 1.0)
        e.tensor_copy(col(90, 8), MODC[:, 24:32])
        e.tensor_scalar_add(col(98, 8), MODC[:, 32:40], 1.0)
        step(e.tensor_scalar(col(130, 8), MODC[:, 16:24], 0.0, None, ALU.is_ge))
        step(e.tensor_scalar(col(130, 8), col(130, 8), 2.0, -1.0, ALU.mult, ALU.add))
        dv.inc(e.engine_nop())
    ev_gsgn = dv.n

    @on("scalar")
    def _(e):
        e.wait_ge(dv.sem, ev_gsgn)
        ac.inc(e.activation(col(82, 8), MODC[:, 16:24], AF.Abs))
    ev_gab = ac.n

    @on("vector")
    def _(e):
        e.wait_ge(ac.sem, ev_gab)
        step = lambda i: (dv.inc(i), e.wait_ge(dv.sem, dv.n))
        step(e.tensor_scalar_max(col(82, 8), col(82, 8), 0.25))
        step(e.tensor_mul(col(82, 8), col(82, 8), col(130, 8)))
        step(e.tensor_scalar(col(130, 8), MODC[:, 40:48], 0.0, None, ALU.is_ge))
        step(e.tensor_scalar(col(130, 8), col(130, 8), 2.0, -1.0, ALU.mult, ALU.add))
    ev_gbs = dv.n

    @on("scalar")
    def _(e):
        e.wait_ge(dv.sem, ev_gbs)
        ac.inc(e.activation(col(106, 8), MODC[:, 40:48], AF.Abs))
    ev_gbab = ac.n

    @on("vector")
    def _(e):
        e.wait_ge(ac.sem, ev_gbab)
        step = lambda i: (dv.inc(i), e.wait_ge(dv.sem, dv.n))
        step(e.tensor_scalar_max(col(106, 8), col(106, 8), 0.25))
        step(e.tensor_mul(col(106, 8), col(106, 8), col(130, 8)))
        e.wait_ge(gwm.sem, ev_wmain)
        e.tensor_copy(BAB[:, 0:8], col(66, 8))
        dv.inc(e.tensor_copy(BAB[:, 8:16], col(58, 8)))
    ev_bab = dv.n

    @on("tensor")
    def _(e):
        e.wait_ge(dv.sem, ev_bab)
        for dc in range(8):
            for kc in range(8):
                mm = e.matmul(CRP[:, dc:dc + 1], WQ[:, kc * D + dc * 128:kc * D + dc * 128 + 128],
                              BAB[:, kc:kc + 1], start=(kc == 0), stop=(kc == 7))
        for dc in range(2):
            for kc in range(8):
                mm = e.matmul(CRP[:, 8 + dc:9 + dc], WDN[:, kc * 256 + dc * 128:kc * 256 + dc * 128 + 128],
                              BAB[:, kc:kc + 1], start=(kc == 0), stop=(kc == 7))
        for dc in range(8):
            for kc in range(8):
                mm = e.matmul(CRP[:, 16 + dc:17 + dc], AWT[:, kc * D + dc * 128:kc * D + dc * 128 + 128],
                              BAB[:, 8 + kc:9 + kc], start=(kc == 0), stop=(kc == 7))
        pe.inc(mm)
    ev_corr = pe.n

    @on("tensor")
    def _(e):
        pe.inc(e.transpose(MRP[0:8, 0:128], col(90, 8), EYEF[:]))
        pe.inc(e.transpose(MRP[0:8, 128:256], col(98, 8), EYEF[:]))
    ev_mrp = pe.n

    @on("vector")
    def _(e):
        e.wait_ge(pe.sem, ev_corr)
        step = lambda i: (dv.inc(i), e.wait_ge(dv.sem, dv.n))
        step(e.tensor_add(col(0, 8), col(0, 8), CRP[:, 0:8]))
        step(e.tensor_scalar_mul(col(0, 8), col(0, 8), 0.125))
        step(e.tensor_add(col(8, 2), col(8, 2), CRP[:, 8:10]))
        step(e.tensor_add(col(34, 8), col(34, 8), CRP[:, 16:24]))
        for kc in range(8):
            step(e.tensor_scalar(col(131, 1), col(74 + kc, 1), 0.125, None, ALU.mult))
            step(e.tensor_scalar(WQ[:, kc * D:(kc + 1) * D], WQ[:, kc * D:(kc + 1) * D], col(131, 1), None, ALU.mult))
            step(e.tensor_scalar(WDN[:, kc * 256:(kc + 1) * 256], WDN[:, kc * 256:(kc + 1) * 256],
                                 col(74 + kc, 1), None, ALU.mult))
            step(e.tensor_scalar(AWT[:, kc * D:(kc + 1) * D], AWT[:, kc * D:(kc + 1) * D],
                                 col(50 + kc, 1), None, ALU.mult))
        for c in range(8):
            e.tensor_scalar(DGA[:, c * 128:(c + 1) * 128], IDN[:, 0:128], col(82 + c, 1), None, ALU.mult)
            e.tensor_scalar(DGB[:, c * 128:(c + 1) * 128], IDN[:, 0:128], col(106 + c, 1), None, ALU.mult)
        dv.inc(e.engine_nop())
    ev_prescale = dv.n

    @on("scalar")
    def _(e):
        e.wait_ge(pe.sem, ev_mrp)
        ac.inc(e.copy(MODR[:, :], MRP[:, :]))
    ev_modr = ac.n

    @on("gpsimd")
    def _(e):
        e.wait_ge(ac.sem, ev_modr)
        for c in range(8):
            srcB = bass.AP(MODR[0:1, 0:1].tensor, c * 256, [[256, 1], [0, 128], [1, 128]])
            gstb.inc(e.dma_start(out=BTB[:, c * 128:(c + 1) * 128], in_=srcB), 16)
            srcS = bass.AP(MODR[0:1, 0:1].tensor, c * 256 + 128, [[256, 1], [0, 128], [1, 128]])
            gstb.inc(e.dma_start(out=STB[:, c * 128:(c + 1) * 128], in_=srcS), 16)
    ev_stb = gstb.n

    def ln_stats(e, src, n, ST=None):
        ST = ST if ST is not None else STAT
        for st in range(n):
            e.tensor_scalar(SCR[:], src[:, st * D:(st + 1) * D], 1.0, 0.0, ALU.mult, ALU.add,
                            accum_out=ST[:, st:st + 1])
            s = e.scalar_tensor_tensor(SCR2[:], src[:, st * D:(st + 1) * D], 1.0,
                                       src[:, st * D:(st + 1) * D], ALU.mult, ALU.mult,
                                       accum_out=ST[:, n + st:n + st + 1])
            dv.inc(s)
            e.wait_ge(dv.sem, dv.n)
        step = lambda i: (dv.inc(i), e.wait_ge(dv.sem, dv.n))
        step(e.tensor_scalar_mul(ST[:, 0:n], ST[:, 0:n], 1.0 / D))
        step(e.tensor_scalar_mul(ST[:, n:2 * n], ST[:, n:2 * n], 1.0 / D))
        step(e.tensor_mul(ST[:, 2 * n:3 * n], ST[:, 0:n], ST[:, 0:n]))
        step(e.tensor_sub(ST[:, n:2 * n], ST[:, n:2 * n], ST[:, 2 * n:3 * n]))
        step(e.tensor_scalar_add(ST[:, n:2 * n], ST[:, n:2 * n], 1e-5))
        step(e.reciprocal(ST[:, n:2 * n], ST[:, n:2 * n]))

    def ln_rest(n, ST=None):
        ST = ST if ST is not None else STAT
        @on("scalar")
        def _(e):
            e.wait_ge(dv.sem, dv.n)
            a = e.activation(ST[:, 3 * n:4 * n], ST[:, n:2 * n], AF.Ln)
            ac.inc(a)
            e.wait_ge(ac.sem, ac.n)
            ac.inc(e.activation(ST[:, 2 * n:3 * n], ST[:, 3 * n:4 * n], AF.Exp, scale=0.5))
        @on("vector")
        def _(e):
            e.wait_ge(ac.sem, ac.n)
            dv.inc(e.scalar_tensor_tensor(ST[:, 3 * n:4 * n], ST[:, 0:n], -1.0,
                                          ST[:, 2 * n:3 * n], ALU.mult, ALU.mult))
        return dv.n

    es_pro.close()
    es_p1 = ExitStack()
    S1 = lambda nm, sh, dt: es_p1.enter_context(nc.sbuf_tensor(nm, sh, dt))
    P1 = lambda nm, sh, dt: es_p1.enter_context(nc.psum_tensor(nm, sh, dt))
    XB = S1("XB", [128, 2 * nst1 * D], BF16)
    Z1 = S1("Z1", [128, nst1 * D], BF16)
    XN1T = S1("XN1T", [128, 8 * p1mt], BF16)
    QFp = [S1("QF0", [128, 8 * p1mt], BF16), S1("QF1", [128, 8 * p1mt], BF16)]
    QSp = [S1("QS0", [128, 8 * p1mt], BF16), S1("QS1", [128, 8 * p1mt], BF16)]
    KVp = [S1("KV0", [128, 16 * p1mt], BF16), S1("KV1", [128, 16 * p1mt], BF16)]
    KHIp = [S1("KHI0", [128, 16 * p1mt], BF16), S1("KHI1", [128, 16 * p1mt], BF16)]
    LATT = S1("LATT", [128, 2 * p1mt], BF16)
    LHS = S1("LHS", [128, 32 * npair], BF16)
    RHSQ = S1("RHSQ", [128, 16 * npair], BF16)
    EB = S1("EB", [128, 2 * 16 * noct], BF16)
    BD = S1("BD", [128, 128 * noct], BF16)
    VOC = S1("VOC", [128, 64 * noct], BF16)
    VSTG = S1("VSTG", [128, 128 * noct], BF16)
    ZTS = S1("ZTS", [16, 8 * noct], BF16)
    ZFM = S1("ZFM", [128, 8 * chtok], BF16)
    MXC = S1("MXC", [128, 4], F32)
    MXR = S1("MXR", [1, 128], F32)
    MXBIAS = MXC[:, 1:2]
    AOUT = S1("AOUT", [128, 8 * p1mt], BF16)
    Y1B = S1("Y1B", [128, 8 * p1mt], BF16)
    X2B = S1("X2B", [128, nst1 * D], BF16)
    SCP = [P1("SCP0", [128, max(16 * noct, 512)], F32), P1("SCP1", [128, max(16 * noct, 512)], F32)]
    VOP = P1("VOP", [128, 512], BF16)
    ZTP = P1("ZTP", [16, 8 * noct], F32)

    ev_x = [0] * n1
    ev_x2 = [0] * n1
    ev_spill = [0] * n1
    evm = {}
    pv = dict(scores_pe=[0, 0], exp_ac=[0, 0], ebcons=[(0, 0, 0)] * 2,
              bd_vcons_pe=0, voc_pe=0, zts_gq=0, ztp_ac=0, zfm_dv=0, o2_dv=0,
              mx_gq=0, mxb_ac=0, x2b_sy=0, mx_dv=0)

    def stage_a(mt):
        par = mt % 2
        sl = par * nst1 * D
        QF, QS, KV, KHI = QFp[par], QSp[par], KVp[par], KHIp[par]

        @on("gpsimd")
        def _(e):
            if mt == 0:
                e.wait_ge(pe.sem, ev_mod)
                e.wait_ge(gstb.sem, ev_stb)
            if mt >= 2:
                e.wait_ge(dv.sem, ev_x2[mt - 2])
            gx[par].inc(e.dma_start(out=XB[:, sl:sl + nst1 * D].rearrange("p (s f) -> p s f", s=nst1),
                                    in_=x_in.rearrange("(m s p) f -> m p s f", s=nst1, p=128)[mt]), 16)
        ev_x[mt] = (par, gx[par].n)

        @on("vector")
        def _(e):
            e.wait_ge(gx[par].sem, ev_x[mt][1])
            if mt >= 1:
                e.wait_ge(ac.sem, evm[("z1", mt - 1)])
            ln_stats(e, XB[:, sl:sl + nst1 * D], nst1)
        ln_rest(nst1)
        ev_nmr = dv.n

        @on("scalar")
        def _(e):
            e.wait_ge(dv.sem, ev_nmr)
            if mt >= 1:
                e.wait_ge(pe.sem, evm[("z1busy", mt - 1)])
            for st in range(nst1):
                a = e.activation(Z1[:, st * D:(st + 1) * D], XB[:, sl + st * D:sl + (st + 1) * D],
                                 AF.Identity, bias=STAT[:, 3 * nst1 + st:3 * nst1 + st + 1],
                                 scale=STAT[:, 2 * nst1 + st:2 * nst1 + st + 1])
            ac.inc(a)
        ev_z1 = ac.n
        evm[("z1", mt)] = ac.n

        for st in range(nst1):
            @on("tensor")
            def _(e, st=st):
                e.wait_ge(ac.sem, ev_z1 if st == 0 else ac.n)
                for c in range(8):
                    t = e.transpose(TPP[:, c * 128:(c + 1) * 128],
                                    Z1[:, st * D + c * 128:st * D + (c + 1) * 128], IDN[:, 0:128])
                pe.inc(t)

            @on("scalar")
            def _(e, st=st):
                e.wait_ge(pe.sem, pe.n)
                ac.inc(e.copy(XN1T[:].rearrange("p (c t) -> p c t", c=8)[:, :, st * 128:(st + 1) * 128],
                              TPP[:].rearrange("p (c j) -> p c j", c=8)))
        evm[("z1busy", mt)] = pe.n
        ev_xn1t = ac.n

        ev_qev = [ac.n + dc + 1 for dc in range(8)]
        @on("tensor")
        def _(e, ev=ev_xn1t, dvb=dv.n):
            e.wait_ge(ac.sem, ev)
            e.wait_ge(dv.sem, ev_prescale)
            e.wait_ge(dv.sem, dvb)
            for dc in range(8):
                if dc >= 2:
                    e.wait_ge(ac.sem, ev_qev[dc - 2])
                for kc in range(8):
                    mm = e.matmul(MMP[dc % 2][:, 0:p1mt], WQ[:, kc * D + dc * 128:kc * D + dc * 128 + 128],
                                  XN1T[:, kc * p1mt:(kc + 1) * p1mt], start=(kc == 0), stop=(kc == 7))
                pe.inc(mm)
        base_q = pe.n - 8

        @on("scalar")
        def _(e, base=base_q):
            if mt >= 2:
                e.wait_ge(dv.sem, evm[("qfree_dv", mt - 2)])
            for dc in range(8):
                e.wait_ge(pe.sem, base + dc + 1)
                a = e.activation(QF[:, dc * p1mt:(dc + 1) * p1mt], MMP[dc % 2][:, 0:p1mt],
                                 AF.Identity, bias=col(dc, 1))
                ac.inc(a)
        assert ac.n == ev_qev[-1]
        ev_q = ac.n

        @on("tensor")
        def _(e, ev=ev_q, dvb=dv.n):
            e.wait_ge(ac.sem, ev)
            e.wait_ge(dv.sem, dvb)
            for dc in range(2):
                for kc in range(8):
                    mm = e.matmul(MMP[dc % 2][:, 0:p1mt], WDN[:, kc * 256 + dc * 128:kc * 256 + dc * 128 + 128],
                                  XN1T[:, kc * p1mt:(kc + 1) * p1mt], start=(kc == 0), stop=(kc == 7))
                pe.inc(mm)
        base_lat = pe.n - 2

        @on("scalar")
        def _(e, base=base_lat):
            for dc in range(2):
                e.wait_ge(pe.sem, base + dc + 1)
                a = e.activation(LATT[:, dc * p1mt:(dc + 1) * p1mt], MMP[dc % 2][:, 0:p1mt],
                                 AF.Identity, bias=col(8 + dc, 1))
                ac.inc(a)
        ev_lat = ac.n

        ev_kvev = [ac.n + gc + 1 for gc in range(16)]
        @on("tensor")
        def _(e, ev=ev_lat, dvb=dv.n):
            e.wait_ge(ac.sem, ev)
            e.wait_ge(dv.sem, dvb)
            for gc in range(16):
                if gc >= 2:
                    e.wait_ge(ac.sem, ev_kvev[gc - 2])
                for kc in range(2):
                    mm = e.matmul(MMP[gc % 2][:, 0:p1mt], WUP[:, kc * 2048 + gc * 128:kc * 2048 + gc * 128 + 128],
                                  LATT[:, kc * p1mt:(kc + 1) * p1mt], start=(kc == 0), stop=(kc == 1))
                pe.inc(mm)
        base_kv = pe.n - 16

        @on("scalar")
        def _(e, base=base_kv):
            if mt >= 2:
                e.wait_ge(dv.sem, evm[("kvfree_dv", mt - 2)])
            for gc in range(16):
                e.wait_ge(pe.sem, base + gc + 1)
                a = e.activation(KV[:, gc * p1mt:(gc + 1) * p1mt], MMP[gc % 2][:, 0:p1mt],
                                 AF.Identity, bias=col(10 + gc, 1))
                ac.inc(a)
        assert ac.n == ev_kvev[-1]
        ev_kv = ac.n
        evm[("kv", mt)] = ac.n
        evm[("q", mt)] = ev_q

        @on("sync")
        def _(e, evq=ev_q, evk=ev_kv):
            e.wait_ge(ac.sem, evq)
            if mt >= 2:
                e.wait_ge(dv.sem, evm[("qfree_dv", mt - 2)])
            sswapp[par].inc(e.dma_start(out=QS[0:64, :], in_=QF[64:128, :]), 16)
            sswapp[par].inc(e.dma_start(out=QS[64:128, :], in_=QF[0:64, :]), 16)
            e.wait_ge(ac.sem, evk)
            sswapp[par].inc(e.dma_start(out=KHI[64:128, :], in_=KV[0:64, :]), 16)
        evm[("sw", mt)] = sswapp[par].n

    def stage_b(mt):
        par = mt % 2
        sl = par * nst1 * D
        QF, QS, KV, KHI = QFp[par], QSp[par], KVp[par], KHIp[par]
        ev_sw = evm[("sw", mt)]
        ev_kv = evm[("kv", mt)]

        for ch in range(nch_a):
            t0 = ch * chtok
            bsl = 0
            rsl = 0
            esl = (ch % 2) * 16 * noct
            chi = mt * nch_a + ch
            cpar = chi % 2

            @on("vector")
            def _(e, t0=t0, chi=chi, cpar=cpar):
                if chi == 0:
                    e.wait_ge(scon.sem, ev_consts)
                    e.wait_ge(pe.sem, ev_mod)
                    e.wait_ge(gstb.sem, ev_stb)
                    dv.inc(e.memset(LHS[:], 0.0))
                    e.wait_ge(dv.sem, dv.n)
                e.wait_ge(sswapp[par].sem, ev_sw)
                if chi >= 1:
                    e.wait_ge(pe.sem, pv["scores_pe"][(chi - 1) % 2])
                e.tensor_copy(LHS[0:64, bsl:bsl + 32 * npair].rearrange("p (x g) -> p x g", g=32)[:, :, 0:16],
                              KV[0:64, :].rearrange("p (g t) -> p t g", g=16)[:, t0:t0 + chtok:2, :])
                e.tensor_copy(LHS[64:128, bsl:bsl + 32 * npair].rearrange("p (x g) -> p x g", g=32)[:, :, 16:32],
                              KHI[64:128, :].rearrange("p (g t) -> p t g", g=16)[:, t0 + 1:t0 + chtok:2, :])
                e.tensor_copy(RHSQ[0:64, rsl:rsl + 16 * npair].rearrange("p (x h) -> p x h", h=16)[:, :, 0:16:2],
                              QF[0:64, :].rearrange("p (c t) -> p t c", c=8)[:, t0:t0 + chtok:2, :])
                e.tensor_copy(RHSQ[0:64, rsl:rsl + 16 * npair].rearrange("p (x h) -> p x h", h=16)[:, :, 1:16:2],
                              QS[0:64, :].rearrange("p (c t) -> p t c", c=8)[:, t0:t0 + chtok:2, :])
                e.tensor_copy(RHSQ[64:128, rsl:rsl + 16 * npair].rearrange("p (x h) -> p x h", h=16)[:, :, 0:16:2],
                              QS[64:128, :].rearrange("p (c t) -> p t c", c=8)[:, t0 + 1:t0 + chtok:2, :])
                dv.inc(e.tensor_copy(RHSQ[64:128, rsl:rsl + 16 * npair].rearrange("p (x h) -> p x h", h=16)[:, :, 1:16:2],
                                     QF[64:128, :].rearrange("p (c t) -> p t c", c=8)[:, t0 + 1:t0 + chtok:2, :]))
            ev_build = dv.n

            @on("tensor")
            def _(e, chi=chi, cpar=cpar, ev=ev_build):
                e.wait_ge(dv.sem, ev)
                if chi >= 2:
                    e.wait_ge(ac.sem, pv["exp_ac"][cpar])
                    e.wait_ge(dv.sem, pv["mx_dv"])
                for o in range(noct):
                    for j in range(4):
                        pr = 4 * o + j
                        mm = e.matmul(SCP[ch % 2][32 * j:32 * j + 32, 16 * o:16 * o + 16],
                                      LHS[:, bsl + 32 * pr:bsl + 32 * pr + 32],
                                      RHSQ[:, rsl + 16 * pr:rsl + 16 * pr + 16],
                                      start=True, stop=True, tile_position=(0, 32 * j))
                pe.inc(mm)
            ev_sc = pe.n
            pv["scores_pe"][cpar] = ev_sc

            @on("vector")
            def _(e, ch=ch, ev=ev_sc):
                e.wait_ge(pe.sem, ev)
                e.wait_ge(gmx.sem, pv["mx_gq"])
                dv.inc(e.reduce_max(MXC[:, 0:1], SCP[ch % 2][:, 0:16 * noct], axis=AX.X))
            ev_mx = dv.n
            pv["mx_dv"] = ev_mx

            @on("scalar")
            def _(e, ev=ev_mx):
                e.wait_ge(dv.sem, ev)
                gmx.inc(e.dma_start(out=MXR[0:1, 0:128], in_=MXC[:, 0:1]), 16)
            ev_mxr = gmx.n

            @on("vector")
            def _(e, ev=ev_mxr):
                e.wait_ge(gmx.sem, ev)
                dv.inc(e.tensor_reduce(MXR[0:1, 0:1], MXR[0:1, 0:128], AX.X, ALU.max, negate=True))
            ev_mneg = dv.n

            @on("scalar")
            def _(e, ev=ev_mneg):
                e.wait_ge(dv.sem, ev)
                e.wait_ge(ac.sem, pv["mxb_ac"])
                gmx.inc(e.dma_start(out=MXBIAS, in_=bcast_row(MXR[0:1, 0:1], 128)), 16)
            ev_nmx = gmx.n
            pv["mx_gq"] = gmx.n

            @on("scalar")
            def _(e, ch=ch, esl=esl, chi=chi, cpar=cpar, ev=ev_nmx):
                e.wait_ge(gmx.sem, ev)
                if chi >= 2:
                    zp, bdv, bgp = pv["ebcons"][cpar]
                    e.wait_ge(pe.sem, zp)
                    e.wait_ge(dv.sem, bdv)
                ac.inc(e.activation(EB[:, esl:esl + 16 * noct], SCP[ch % 2][:, 0:16 * noct], AF.Exp, bias=MXBIAS))
            ev_eb = ac.n
            pv["exp_ac"][cpar] = ev_eb
            pv["mxb_ac"] = ev_eb

            @on("vector")
            def _(e, esl=esl, chi=chi, ev=ev_eb):
                e.wait_ge(ac.sem, ev)
                if chi >= 1:
                    e.wait_ge(pe.sem, pv["bd_vcons_pe"]) 
                for par in range(2):
                    bb = e.tensor_tensor(
                        BD[:].rearrange("p (o a t h) -> p o a t h", o=noct, a=2, t=8)[:, :, par],
                        EB[:, esl:esl + 16 * noct].rearrange("p (o h) -> p o h", o=noct)[:, :, par::2].unsqueeze(2).broadcast_to([128, noct, 8, 8]),
                        MSK[:, 0:128].rearrange("p (t h) -> p t h", t=8)[:, :, par::2].unsqueeze(1).broadcast_to([128, noct, 8, 8]),
                        ALU.mult)
                dv.inc(bb)
            ev_bd_dv = dv.n
            ev_bd_ac = dv.n

            @on("vector")
            def _(e, t0=t0, chi=chi, ev=ev_kv):
                e.wait_ge(ac.sem, ev)
                if chi >= 1:
                    e.wait_ge(pe.sem, pv["voc_pe"])
                dv.inc(e.tensor_copy(
                    VSTG[64:128, :].rearrange("p (x g) -> p x g", g=16),
                    KV[64:128, :].rearrange("p (g t) -> p t g", g=16)[:, t0:t0 + chtok, :]))
            ev_vs_dv = dv.n
            ev_vs_gp = gp_.n

            ev_vev = []
            for w in range(nw):
                @on("tensor")
                def _(e, w=w, evd=ev_vs_dv):
                    if w == 0:
                        e.wait_ge(dv.sem, evd)
                    else:
                        e.wait_ge(ac.sem, ev_vev[w - 1])
                    for o in range(8 * w, 8 * w + 8):
                        t = e.transpose(VOP[:, (o % 8) * 64:(o % 8) * 64 + 64],
                                        VSTG[64:128, o * 128:(o + 1) * 128],
                                        IDN[64:128, 128:192])
                    pe.inc(t)

                @on("scalar")
                def _(e, w=w):
                    e.wait_ge(pe.sem, pe.n)
                    ac.inc(e.copy(VOC[:, w * 512:(w + 1) * 512], VOP[:]))
                ev_vev.append(ac.n)
            ev_voc = ac.n

            @on("tensor")
            def _(e, esl=esl, chi=chi, evb=ev_eb):
                e.wait_ge(ac.sem, evb)
                if chi >= 1:
                    e.wait_ge(ac.sem, pv["ztp_ac"])
                for o in range(noct):
                    mm = e.matmul(ZTP[:, 8 * o:8 * o + 8], EB[:, esl + 16 * o:esl + 16 * o + 16], BON[:],
                                  start=True, stop=True)
                pe.inc(mm)
            ev_zmm = pe.n

            @on("scalar")
            def _(e, chi=chi, ev=ev_zmm):
                e.wait_ge(pe.sem, ev)
                if chi >= 1:
                    e.wait_ge(gzf.sem, pv["zts_gq"])
                ac.inc(e.copy(ZTS[:], ZTP[:]))
            ev_zts = ac.n
            pv["ztp_ac"] = ev_zts

            @on("vector")
            def _(e, ev=ev_zts):
                e.wait_ge(ac.sem, ev)
                with nc.allow_low_precision(reason="softmax denominator reciprocal in bf16; tolerance-checked"):
                    dv.inc(e.reciprocal(ZTS[:], ZTS[:]))
            ev_zrec = dv.n

            @on("gpsimd")
            def _(e, chi=chi, ev=ev_zrec):
                e.wait_ge(dv.sem, ev)
                if chi >= 1:
                    e.wait_ge(dv.sem, pv["zfm_dv"])
                zp = 8 * noct
                for cc in range(8):
                    s2 = bass.AP(ZTS[0:1, 0:1].tensor, 2 * cc * zp, [[zp, 2], [0, 64], [1, zp]])
                    gzf.inc(e.dma_start(out=ZFM[:, cc * chtok:(cc + 1) * chtok], in_=s2), 16)
            ev_zfm = gzf.n
            pv["zts_gq"] = gzf.n

            ev_o2w = []
            for w in range(nw):
                @on("tensor")
                def _(e, w=w, chi=chi, evd=ev_bd_ac, evv=ev_voc):
                    O2P = X2P
                    if w == 0:
                        e.wait_ge(dv.sem, evd)
                        e.wait_ge(ac.sem, evv)
                        if chi >= 1:
                            e.wait_ge(dv.sem, pv["o2_dv"])
                    else:
                        e.wait_ge(dv.sem, ev_o2w[w - 1])
                    for o in range(8 * w, 8 * w + 8):
                        off = (o % 8) * 64
                        e.matmul(O2P[0:64, off:off + 64], VOC[:, o * 64:o * 64 + 64],
                                 BD[:, o * 128:o * 128 + 64],
                                 start=True, stop=True)
                        mm = e.matmul(O2P[64:128, off:off + 64], VOC[:, o * 64:o * 64 + 64],
                                      BD[:, o * 128 + 64:o * 128 + 128],
                                      start=True, stop=True)
                    pe.inc(mm)

                @on("vector")
                def _(e, t0=t0, w=w, chi=chi, evz=ev_zfm):
                    O2P = X2P
                    e.wait_ge(gzf.sem, evz)
                    e.wait_ge(pe.sem, pe.n)
                    for half in range(2):
                        a = e.scalar_tensor_tensor(
                            AOUT[half * 64:(half + 1) * 64, :].rearrange("p (c t) -> p c t", c=8)[:, :, t0 + w * 64:t0 + (w + 1) * 64],
                            O2P[half * 64:(half + 1) * 64, 0:512].rearrange("p (x h) -> p h x", h=8),
                            1.0,
                            ZFM[half * 64:(half + 1) * 64, :].rearrange("p (c t) -> p c t", c=8)[:, :, w * 64:(w + 1) * 64],
                            ALU.mult, ALU.mult)
                    dv.inc(a)
                ev_o2w.append(dv.n)
            pv["zfm_dv"] = dv.n
            pv["voc_pe"] = pe.n
            pv["bd_vcons_pe"] = pe.n
            pv["ebcons"][cpar] = (ev_zmm, ev_bd_dv, 0)
        ev_aout = dv.n
        evm[("qfree_dv", mt)] = dv.n
        evm[("kvfree_dv", mt)] = dv.n

        ev_y1ev = [dv.n + dc + 1 for dc in range(8)]
        @on("tensor")
        def _(e, ev=ev_aout, acb=ac.n):
            e.wait_ge(dv.sem, ev)
            e.wait_ge(ac.sem, acb)
            for dc in range(8):
                if dc >= 2:
                    e.wait_ge(dv.sem, ev_y1ev[dc - 2])
                for kc in range(8):
                    mm = e.matmul(MMP[dc % 2][:, 0:p1mt], WO[:, kc * D + dc * 128:kc * D + dc * 128 + 128],
                                  AOUT[:, kc * p1mt:(kc + 1) * p1mt], start=(kc == 0), stop=(kc == 7))
                pe.inc(mm)
        base_wo = pe.n - 8

        @on("vector")
        def _(e, base=base_wo):
            for dc in range(8):
                e.wait_ge(pe.sem, base + dc + 1)
                a = e.tensor_scalar(Y1B[:, dc * p1mt:(dc + 1) * p1mt], MMP[dc % 2][:, 0:p1mt],
                                    col(26 + dc, 1), None, ALU.add)
                dv.inc(a)
        assert dv.n == ev_y1ev[-1]
        ev_y1 = dv.n

        ev_x2w = []
        for st in range(nst1):
            for hb in range(2):
                @on("tensor")
                def _(e, st=st, hb=hb, ev=ev_y1):
                    if st == 0 and hb == 0:
                        e.wait_ge(dv.sem, ev)
                    else:
                        e.wait_ge(dv.sem, ev_x2w[-1])
                    for c in range(4 * hb, 4 * hb + 4):
                        mm = e.matmul(X2P[:, (c % 4) * 128:(c % 4) * 128 + 128],
                                      Y1B[:, c * p1mt + st * 128:c * p1mt + st * 128 + 128],
                                      DGA[:, c * 128:(c + 1) * 128], start=True, stop=True)
                    pe.inc(mm)

                @on("vector")
                def _(e, st=st, hb=hb):
                    e.wait_ge(pe.sem, pe.n)
                    if mt >= 1 and st == 0 and hb == 0:
                        e.wait_ge(sspill.sem, pv["x2b_sy"])
                    a = e.scalar_tensor_tensor(X2B[:, st * D + hb * 512:st * D + (hb + 1) * 512],
                                               X2P[:], 1.0,
                                               XB[:, sl + st * D + hb * 512:sl + st * D + (hb + 1) * 512],
                                               ALU.mult, ALU.add)
                    dv.inc(a)
                ev_x2w.append(dv.n)
        ev_x2[mt] = dv.n
        pv["o2_dv"] = dv.n

        @on("sync")
        def _(e, ev=ev_x2[mt]):
            e.wait_ge(dv.sem, ev)
            sspill.inc(e.dma_start(out=x2s.rearrange("(m s p) f -> m p s f", s=nst1, p=128)[mt],
                                   in_=X2B[:].rearrange("p (s f) -> p s f", s=nst1)), 16)
        ev_spill[mt] = sspill.n
        pv["x2b_sy"] = sspill.n

    if n1 >= 2:
        stage_a(0)
        stage_a(1)
        for m in range(2, n1):
            stage_b(m - 2)
            stage_a(m)
        stage_b(n1 - 2)
        stage_b(n1 - 1)
    else:
        stage_a(0)
        stage_b(0)

    ev_p1done = sspill.n
    if dbg:
        @on("gpsimd")
        def _(e):
            e.wait_ge(sspill.sem, ev_p1done)
            gdbg.inc(e.dma_start(out=dbg_t["d_h1t"][:], in_=XN1T[:]), 16)
            gdbg.inc(e.dma_start(out=dbg_t["d_q"][:], in_=QF[:]), 16)
            gdbg.inc(e.dma_start(out=dbg_t["d_kv"][:], in_=KV[:]), 16)
            gdbg.inc(e.dma_start(out=dbg_t["d_ao"][:], in_=AOUT[:]), 16)
            gdbg.inc(e.dma_start(out=dbg_t["d_x2"].rearrange("(m s p) f -> m p s f", s=nst1, p=128)[n1 - 1],
                               in_=X2B[:].rearrange("p (s f) -> p s f", s=nst1)), 16)

    ev_p1done = sspill.n
    ev_p1done_gq = gdbg.n
    es_p1.close()
    es_p1w.close()
    es_p2 = ExitStack()
    S2 = lambda nm, sh, dt: es_p2.enter_context(nc.sbuf_tensor(nm, sh, dt))
    X2L = S2("X2L", [128, mt_hg * nst2 * D], BF16)
    Z2P = [S2("Z2a", [128, nst2 * D], BF16), S2("Z2b", [128, nst2 * D], BF16)]
    H2P = [S2("H2a", [128, nst2 * D], BF16), S2("H2b", [128, nst2 * D], BF16)]
    XN3T = S2("XN3T", [128, 8 * p2mt], BF16)
    X2T = S2("X2T", [128, 8 * p2mt], BF16)
    XAT = S2("XAT", [128, mt_hg * 8 * p2mt], BF16)
    W1S = S2("W1S", [128, 8 * 1024], BF16)
    W2S = S2("W2S", [128, 8 * 1024], BF16)
    HQ = S2("HQ", [128, 8 * p2mt], BF16)
    EO = S2("EO", [128, mt_hg * 8 * p2mt], F32)
    EOB = S2("EOB", [128, mt_hg * 8 * p2mt], BF16)
    FB = Z2P[0]
    OUTB = S2("OUTB", [128, D], F32)
    P2_ = lambda nm, sh, dt: es_p2.enter_context(nc.psum_tensor(nm, sh, dt))
    MM2 = [MMP[0], MMP[1], P2_("MM2a", [128, 512], F32), P2_("MM2b", [128, 512], F32)]

    p2 = dict(x2l_dv=0, w1_pe=0, w2_pe=0, outb_sy=0, z2_dv=0, h2_ac=0,
              eo_last={}, eob_pe={}, statfree_ac=0)

    for hg in range(n_hg):
        eidx = (hg * mt_hg * p2mt * 2) // T
        ev_xat = [0] * mt_hg
        if "ch_ev" not in p2:
            p2["ch_ev"] = {}
        ch_ev = p2["ch_ev"]

        def pro_chain(mtg, hg=hg):
            mtglob = hg * mt_hg + mtg
            xsl = mtg * nst2 * D
            par = mtg % 2
            ST = STAT if par == 0 else STATB
            Z2 = Z2P[par]
            H2 = H2P[par]

            @on("sync")
            def _(e):
                if hg == 0 and mtg == 0:
                    e.wait_ge(sspill.sem, ev_p1done)
                    if dbg:
                        e.wait_ge(gdbg.sem, ev_p1done_gq)
                if hg >= 1:
                    e.wait_ge(dv.sem, p2["x2l_dv"])
                sx2l[mtg % 4].inc(e.dma_start(out=X2L[:, xsl:xsl + nst2 * D].rearrange("p (s f) -> p s f", s=nst2),
                                              in_=x2s.rearrange("(m s p) f -> m p s f", s=nst2, p=128)[mtglob]), 16)
            ev_x2l = sx2l[mtg % 4].n

            @on("vector")
            def _(e):
                e.wait_ge(sx2l[mtg % 4].sem, ev_x2l)
                e.wait_ge(gstb.sem, ev_stb)
                if ("st_free", par) in ch_ev:
                    e.wait_ge(ac.sem, ch_ev[("st_free", par)])
                ln_stats(e, X2L[:, xsl:xsl + nst2 * D], nst2, ST)
            ln_rest(nst2, ST)

            @on("scalar")
            def _(e):
                e.wait_ge(dv.sem, dv.n)
                if ("z2_free", par) in ch_ev:
                    e.wait_ge(pe.sem, ch_ev[("z2_free", par)])
                for st in range(nst2):
                    a = e.activation(Z2[:, st * D:(st + 1) * D], X2L[:, xsl + st * D:xsl + (st + 1) * D],
                                     AF.Identity, bias=ST[:, 3 * nst2 + st:3 * nst2 + st + 1],
                                     scale=ST[:, 2 * nst2 + st:2 * nst2 + st + 1])
                ac.inc(a)
            ev_z2 = ac.n

            @on("vector")
            def _(e):
                e.wait_ge(ac.sem, ev_z2)
                if ("h2_free", par) in ch_ev:
                    e.wait_ge(ac.sem, ch_ev[("h2_free", par)])
                for st in range(nst2):
                    h = e.scalar_tensor_tensor(H2[:, st * D:(st + 1) * D], Z2[:, st * D:(st + 1) * D],
                                               1.0, STB[:], ALU.mult, ALU.mult)
                dv.inc(h)
                e.wait_ge(dv.sem, dv.n)
                for st in range(nst2):
                    h = e.tensor_add(H2[:, st * D:(st + 1) * D], H2[:, st * D:(st + 1) * D], BTB[:])
                dv.inc(h)
                e.wait_ge(dv.sem, dv.n)
                ln_stats(e, H2[:, 0:nst2 * D], nst2, ST)
            ln_rest(nst2, ST)
            ev_nmr3 = dv.n

            @on("scalar")
            def _(e):
                e.wait_ge(dv.sem, ev_nmr3)
                for st in range(nst2):
                    a = e.activation(Z2[:, st * D:(st + 1) * D], H2[:, st * D:(st + 1) * D],
                                     AF.Identity, bias=ST[:, 3 * nst2 + st:3 * nst2 + st + 1],
                                     scale=ST[:, 2 * nst2 + st:2 * nst2 + st + 1])
                ac.inc(a)
            ch_ev[("xn3", mtg)] = ac.n
            ch_ev[("st_free", par)] = ac.n
            ch_ev[("h2_free", par)] = ac.n

        def pro_pe(mtg, hg=hg):
            xsl = mtg * nst2 * D
            par = mtg % 2
            XN3 = Z2P[par]
            ev_xn3 = ch_ev[("xn3", mtg)]
            for st in range(nst2):
                @on("tensor")
                def _(e, st=st):
                    e.wait_ge(ac.sem, ev_xn3 if st == 0 else ac.n)
                    for c in range(8):
                        t = e.transpose(TPP[:, c * 128:(c + 1) * 128],
                                        XN3[:, st * D + c * 128:st * D + (c + 1) * 128], IDN[:, 0:128])
                    pe.inc(t)

                @on("scalar")
                def _(e, st=st):
                    e.wait_ge(pe.sem, pe.n)
                    ac.inc(e.copy(XN3T[:].rearrange("p (c t) -> p c t", c=8)[:, :, st * 128:(st + 1) * 128],
                                  TPP[:].rearrange("p (c j) -> p c j", c=8)))
            ch_ev[("z2_free", par)] = pe.n
            for st in range(nst2):
                @on("tensor")
                def _(e, st=st):
                    e.wait_ge(ac.sem, ac.n)
                    for c in range(8):
                        t = e.transpose(TPP[:, c * 128:(c + 1) * 128],
                                        X2L[:, xsl + st * D + c * 128:xsl + st * D + (c + 1) * 128], IDN[:, 0:128])
                    pe.inc(t)

                @on("scalar")
                def _(e, st=st):
                    e.wait_ge(pe.sem, pe.n)
                    if mtg >= 1 and st == 0:
                        e.wait_ge(dv.sem, ev_xat[mtg - 1])
                    ac.inc(e.copy(X2T[:].rearrange("p (c t) -> p c t", c=8)[:, :, st * 128:(st + 1) * 128],
                                  TPP[:].rearrange("p (c j) -> p c j", c=8)))
            ev_x2t = ac.n

            ev_xaev = [dv.n + dc + 1 for dc in range(8)]
            @on("tensor")
            def _(e, ev=ev_x2t, dvb=dv.n):
                e.wait_ge(ac.sem, ev)
                e.wait_ge(dv.sem, dvb)
                for dc in range(8):
                    if dc >= 4:
                        e.wait_ge(dv.sem, ev_xaev[dc - 4])
                    for kc in range(8):
                        mm = e.matmul(MM2[dc % 4][:, 0:p2mt], AWT[:, kc * D + dc * 128:kc * D + dc * 128 + 128],
                                      XN3T[:, kc * p2mt:(kc + 1) * p2mt], start=(kc == 0), stop=(kc == 7))
                    pe.inc(mm)
            base_xa = pe.n - 8

            @on("vector")
            def _(e, mtg=mtg, base=base_xa):
                for dc in range(8):
                    e.wait_ge(pe.sem, base + dc + 1)
                    a = e.scalar_tensor_tensor(XAT[:, mtg * 8 * p2mt + dc * p2mt:mtg * 8 * p2mt + (dc + 1) * p2mt],
                                               MM2[dc % 4][:, 0:p2mt], col(34 + dc, 1),
                                               X2T[:, dc * p2mt:(dc + 1) * p2mt], ALU.add, ALU.add)
                    dv.inc(a)
            assert dv.n == ev_xaev[-1], ("xaev", dv.n, ev_xaev[-1])
            ev_xat[mtg] = dv.n

        sched = []
        for m in range(mt_hg):
            sched.append(("c", m))
        order = []
        if mt_hg >= 2:
            order = [("c", 0), ("c", 1)]
            for m in range(2, mt_hg):
                order.append(("p", m - 2))
                order.append(("c", m))
            order.append(("p", mt_hg - 2))
            order.append(("p", mt_hg - 1))
        else:
            order = [("c", 0), ("p", 0)]
        for kind, m in order:
            if kind == "c":
                pro_chain(m)
            else:
                pro_pe(m)

        for wq in range(4):
            @on("gpsimd")
            def _(e, wq=wq, eidx=eidx):
                e.wait_ge(sspill.sem, ev_p1done)
                if dbg:
                    e.wait_ge(gdbg.sem, ev_p1done_gq)
                e.wait_ge(pe.sem, p2["w1_pe"])
                for kc in range(8):
                    gw1.inc(e.dma_start(out=W1S[:, kc * 1024:(kc + 1) * 1024],
                                       in_=w1_in[eidx][128 * kc:128 * (kc + 1), 1024 * wq:1024 * (wq + 1)]), 16)
            ev_w1l = gw1.n

            @on("gpsimd")
            def _(e, wq=wq, eidx=eidx):
                e.wait_ge(pe.sem, p2["w2_pe"])
                for kc in range(8):
                    gw2.inc(e.dma_start(out=W2S[:, kc * 1024:(kc + 1) * 1024],
                                       in_=w2_in[eidx][1024 * wq + 128 * kc:1024 * wq + 128 * (kc + 1), :]), 16)
            ev_w2l = gw2.n

            for mtg in range(mt_hg):
                ev_hqev = [ac.n + hc + 1 for hc in range(8)]
                @on("tensor")
                def _(e, wq=wq, mtg=mtg, ev=ev_w1l, dvb=dv.n):
                    e.wait_ge(gw1.sem, ev)
                    e.wait_ge(dv.sem, ev_xat[mtg])
                    e.wait_ge(dv.sem, dvb)
                    for hc in range(8):
                        if hc >= 4:
                            e.wait_ge(ac.sem, ev_hqev[hc - 4])
                        for kc in range(8):
                            mm = e.matmul(MM2[hc % 4][:, 0:p2mt], W1S[:, kc * 1024 + hc * 128:kc * 1024 + hc * 128 + 128],
                                          XAT[:, mtg * 8 * p2mt + kc * p2mt:mtg * 8 * p2mt + (kc + 1) * p2mt],
                                          start=(kc == 0), stop=(kc == 7))
                        pe.inc(mm)
                base_h = pe.n - 8
                p2["w1_pe"] = pe.n if (mtg == mt_hg - 1) else p2["w1_pe"]

                @on("scalar")
                def _(e, wq=wq, mtg=mtg, eidx=eidx, base=base_h):
                    for hc in range(8):
                        e.wait_ge(pe.sem, base + hc + 1)
                        a = e.activation(HQ[:, hc * p2mt:(hc + 1) * p2mt],
                                         MM2[hc % 4][:, 0:p2mt], AF.Gelu,
                                         bias=B1C[:, eidx * 32 + wq * 8 + hc:eidx * 32 + wq * 8 + hc + 1])
                        ac.inc(a)
                assert ac.n == ev_hqev[-1], ("hqev", ac.n, ev_hqev[-1])
                ev_hq = ac.n

                ev_eoev = [dv.n + dc + 1 for dc in range(8)]
                @on("tensor")
                def _(e, wq=wq, mtg=mtg, ev2=ev_w2l, evh=ev_hq, dvb=dv.n):
                    e.wait_ge(gw2.sem, ev2)
                    e.wait_ge(ac.sem, evh)
                    e.wait_ge(dv.sem, dvb)
                    for dc in range(8):
                        if dc >= 4:
                            e.wait_ge(dv.sem, ev_eoev[dc - 4])
                        for kc in range(8):
                            mm = e.matmul(MM2[dc % 4][:, 0:p2mt], W2S[:, kc * 1024 + dc * 128:kc * 1024 + dc * 128 + 128],
                                          HQ[:, kc * p2mt:(kc + 1) * p2mt],
                                          start=(kc == 0), stop=(kc == 7))
                        pe.inc(mm)
                base_eo = pe.n - 8
                p2["w2_pe"] = pe.n if (mtg == mt_hg - 1) else p2["w2_pe"]

                @on("vector")
                def _(e, wq=wq, mtg=mtg, eidx=eidx, base=base_eo):
                    for dc in range(8):
                        e.wait_ge(pe.sem, base + dc + 1)
                        eo_sl = EO[:, mtg * 8 * p2mt + dc * p2mt:mtg * 8 * p2mt + (dc + 1) * p2mt]
                        key = (mtg, dc)
                        if wq == 0:
                            if key in p2["eo_last"]:
                                e.wait_ge(dv.sem, p2["eo_last"][key])
                            a = e.tensor_copy(eo_sl, MM2[dc % 4][:, 0:p2mt])
                        elif wq < 3:
                            e.wait_ge(dv.sem, p2["eo_last"][key])
                            a = e.tensor_add(eo_sl, eo_sl, MM2[dc % 4][:, 0:p2mt])
                        else:
                            e.wait_ge(dv.sem, p2["eo_last"][key])
                            if key in p2["eob_pe"]:
                                e.wait_ge(pe.sem, p2["eob_pe"][key])
                            eob_sl = EOB[:, mtg * 8 * p2mt + dc * p2mt:mtg * 8 * p2mt + (dc + 1) * p2mt]
                            a = e.scalar_tensor_tensor(eob_sl, MM2[dc % 4][:, 0:p2mt],
                                                       B2C[:, eidx * 8 + dc:eidx * 8 + dc + 1],
                                                       eo_sl, ALU.add, ALU.add)
                        dv.inc(a)
                        p2["eo_last"][key] = dv.n
                    assert dv.n == ev_eoev[-1], ("eoev", dv.n, ev_eoev[-1])

        for mtg in range(mt_hg):
            mtglob = hg * mt_hg + mtg
            xsl = mtg * nst2 * D
            ev_fbev = [ac.n + dc + 1 for dc in range(8)]
            @on("tensor")
            def _(e, mtg=mtg, dvb=dv.n, acb=ac.n):
                e.wait_ge(dv.sem, dvb)
                e.wait_ge(ac.sem, acb)
                for dcc in range(8):
                    e.wait_ge(dv.sem, p2["eo_last"][(mtg, dcc)])
                for dc in range(8):
                    if dc >= 2:
                        e.wait_ge(ac.sem, ev_fbev[dc - 2])
                    for kc in range(8):
                        mm = e.matmul(MMP[dc % 2][:, 0:p2mt], NWT[:, kc * D + dc * 128:kc * D + dc * 128 + 128],
                                      EOB[:, mtg * 8 * p2mt + kc * p2mt:mtg * 8 * p2mt + (kc + 1) * p2mt],
                                      start=(kc == 0), stop=(kc == 7))
                    pe.inc(mm)
            base_f = pe.n - 8
            for dcc in range(8):
                p2["eob_pe"][(mtg, dcc)] = pe.n

            @on("scalar")
            def _(e, base=base_f):
                for dc in range(8):
                    e.wait_ge(pe.sem, base + dc + 1)
                    a = e.activation(FB[:, dc * p2mt:(dc + 1) * p2mt], MMP[dc % 2][:, 0:p2mt],
                                     AF.Gelu, bias=col(42 + dc, 1))
                    ac.inc(a)
            assert ac.n == ev_fbev[-1], ("fbev", ac.n, ev_fbev[-1])
            ev_fb = ac.n

            ev_ow = []
            for st in range(nst2):
                for hb in range(2):
                    @on("tensor")
                    def _(e, st=st, hb=hb, ev=ev_fb):
                        if st == 0 and hb == 0:
                            e.wait_ge(ac.sem, ev)
                        else:
                            e.wait_ge(dv.sem, ev_ow[-1])
                        for c in range(4 * hb, 4 * hb + 4):
                            mm = e.matmul(X2P[:, (c % 4) * 128:(c % 4) * 128 + 128],
                                          FB[:, c * p2mt + st * 128:c * p2mt + st * 128 + 128],
                                          DGB[:, c * 128:(c + 1) * 128], start=True, stop=True)
                        pe.inc(mm)

                    @on("vector")
                    def _(e, st=st, hb=hb, xsl=xsl):
                        e.wait_ge(pe.sem, pe.n)
                        if p2["outb_sy"]:
                            e.wait_ge(sout.sem, p2["outb_sy"])
                        a = e.scalar_tensor_tensor(OUTB[:, hb * 512:(hb + 1) * 512],
                                                   X2P[:], 1.0,
                                                   X2L[:, xsl + st * D + hb * 512:xsl + st * D + (hb + 1) * 512],
                                                   ALU.mult, ALU.add)
                        dv.inc(a)
                    ev_ow.append(dv.n)

                @on("sync")
                def _(e, mtglob=mtglob, st=st, ev=dv.n):
                    e.wait_ge(dv.sem, ev)
                    sout.inc(e.dma_start(
                        out=y_out.rearrange("(m s p) f -> m s p f", s=nst2, p=128)[mtglob, st],
                        in_=OUTB[:]), 16)
                p2["outb_sy"] = sout.n
            ev_out = dv.n
        p2["x2l_dv"] = dv.n

        if dbg and hg == 0:
            @on("gpsimd")
            def _(e):
                e.wait_ge(dv.sem, dv.n)
                gdbg.inc(e.dma_start(out=dbg_t["d_xat"][:], in_=XAT[:, 0:8 * p2mt]), 16)
                gdbg.inc(e.dma_start(out=dbg_t["d_eo"][:], in_=EO[:, 0:8 * p2mt]), 16)
                gdbg.inc(e.dma_start(out=dbg_t["d_eob"][:], in_=EOB[:, 0:8 * p2mt]), 16)
                gdbg.inc(e.dma_start(out=dbg_t["d_fb"][:], in_=FB[:]), 16)
                gdbg.inc(e.dma_start(out=dbg_t["d_hq"][:], in_=HQ[:]), 16)

    @on("sync")
    def _(e):
        e.wait_ge(sout.sem, sout.n)
        e.wait_ge(sspill.sem, sspill.n)
    @on("gpsimd")
    def _(e):
        e.wait_ge(gw1.sem, gw1.n)
        e.wait_ge(gw2.sem, gw2.n)
        e.wait_ge(gdbg.sem, gdbg.n)
        e.wait_ge(gp_.sem, gp_.n)
    @on("vector")
    def _(e):
        e.wait_ge(sout.sem, sout.n)

    es_p2.close()
    es.close()
    return nc


_NC_CACHE = {}


def _get_nc():
    key = "real"
    if key not in _NC_CACHE:
        _NC_CACHE[key] = build(T=4096, p1mt=256, p2mt=256, mt_per_hg=4, dbg=False)
    return _NC_CACHE[key]


def kernel(**inputs):
    from concourse.bass_utils import run_bass_kernel_spmd

    consts = host_consts()
    nc = _get_nc()

    awt = np.ascontiguousarray(inputs["L0_ffn_attn_w"].T)
    nwt = np.ascontiguousarray(inputs["L0_ffn_next_w"].T)
    x = np.ascontiguousarray(inputs["L0_attn_x"])
    w1 = inputs["L0_ffn_w1"]
    b1 = inputs["L0_ffn_b1"]
    w2 = inputs["L0_ffn_w2"]
    b2 = inputs["L0_ffn_b2"]

    shared = {
        "wq": np.ascontiguousarray(inputs["L0_attn_Wq"]),
        "bq": np.ascontiguousarray(inputs["L0_attn_bq"]),
        "wdown": np.ascontiguousarray(inputs["L0_attn_Wdown"]),
        "bdown": np.ascontiguousarray(inputs["L0_attn_bdown"]),
        "wup": np.ascontiguousarray(inputs["L0_attn_Wup"]),
        "bup": np.ascontiguousarray(inputs["L0_attn_bup"]),
        "wout": np.ascontiguousarray(inputs["L0_attn_Wout"]),
        "bout": np.ascontiguousarray(inputs["L0_attn_bout"]),
        "ln_scale": np.ascontiguousarray(inputs["L0_ffn_ln_scale"]),
        "ln_bias": np.ascontiguousarray(inputs["L0_ffn_ln_bias"]),
        "attn_wT": awt,
        "attn_b": np.ascontiguousarray(inputs["L0_ffn_attn_b"]),
        "next_wT": nwt,
        "next_b": np.ascontiguousarray(inputs["L0_ffn_next_b"]),
        "wmod": np.ascontiguousarray(inputs["L0_wmod"]),
    }
    shared.update(consts)

    in_maps = []
    for cid in range(8):
        b, half = cid // 2, cid % 2
        e0 = 4 * (b % 2) + 2 * half
        m = dict(shared)
        m["x"] = np.ascontiguousarray(x[b, half * 4096:(half + 1) * 4096, :])
        m["cb_"] = np.ascontiguousarray(inputs["L0_c"][b])
        m["w1a"] = np.ascontiguousarray(w1[e0])
        m["w1b"] = np.ascontiguousarray(w1[e0 + 1])
        m["b1a"] = np.ascontiguousarray(b1[e0])
        m["b1b"] = np.ascontiguousarray(b1[e0 + 1])
        m["w2a"] = np.ascontiguousarray(w2[e0])
        m["w2b"] = np.ascontiguousarray(w2[e0 + 1])
        m["b2a"] = np.ascontiguousarray(b2[e0])
        m["b2b"] = np.ascontiguousarray(b2[e0 + 1])
        in_maps.append(m)

    res = run_bass_kernel_spmd(nc, in_maps, core_ids=list(range(8)))
    out = np.empty((4, 8192, 1024), np.float32)
    for cid in range(8):
        b, half = cid // 2, cid % 2
        out[b, half * 4096:(half + 1) * 4096, :] = res.results[cid]["y"]
    return out


# version 18
# speedup vs baseline: 1.0700x; 1.0700x over previous
import numpy as np
import concourse.bass as bass
import concourse.mybir as mybir
from contextlib import ExitStack

F32 = mybir.dt.float32
BF16 = mybir.dt.bfloat16
AF = mybir.ActivationFunctionType
ALU = mybir.AluOpType
AX = mybir.AxisListType

D = 1024
HID = 4096


def bcast_row(ap, nparts):
    dims = [list(d) for d in ap.ap]
    return bass.AP(ap.tensor, ap.offset, [dims[0], [0, nparts]] + dims[1:])


def host_consts():
    import ml_dtypes
    B = ml_dtypes.bfloat16
    idn = np.zeros((128, 192), np.float32)
    for i in range(128):
        idn[i, i] = 1.0
    for i in range(64):
        idn[64 + i, 128 + i] = 1.0
    mask = np.zeros((128, 512), np.float32)
    for p in range(128):
        t8 = p // 16
        for o in range(4):
            mask[p, o * 128 + t8 * 16: o * 128 + t8 * 16 + 16] = 1.0
    bones = np.zeros((128, 8), np.float32)
    for p in range(128):
        bones[p, p // 16] = 1.0
    return {"idn": idn.astype(B), "mask5": mask.astype(B), "bones": bones.astype(B)}


class Ctr:
    def __init__(self, es, nc, name):
        self.sem = es.enter_context(nc.semaphore(name))
        self.n = 0

    def inc(self, instr, k=1):
        instr.then_inc(self.sem, k)
        self.n += k
        return self.n


def build(T=4096, p1mt=512, p2mt=512, mt_per_hg=2, dbg=False):
    n1 = T // p1mt
    n2 = T // p2mt
    nst1 = p1mt // 128
    nst2 = p2mt // 128
    chtok = min(256, p1mt)
    nch_a = p1mt // chtok
    noct = chtok // 8
    npair = chtok // 2
    nw = noct // 8
    n_hg = max(1, n2 // mt_per_hg)
    mt_hg = n2 // n_hg

    nc = bass.Bass("TRN2", target_bir_lowering=False)

    def dram(nm, sh, dt, kind=None):
        t = nc.dram_tensor(nm, sh, dt, kind=kind) if kind else nc.dram_tensor(nm, sh, dt)
        return t.ap()

    x_in = dram("x", [T, D], F32, "ExternalInput")
    wq_in = dram("wq", [D, D], F32, "ExternalInput")
    bq_in = dram("bq", [D], F32, "ExternalInput")
    wdn_in = dram("wdown", [D, 256], F32, "ExternalInput")
    bdn_in = dram("bdown", [256], F32, "ExternalInput")
    wup_in = dram("wup", [256, 2048], F32, "ExternalInput")
    bup_in = dram("bup", [2048], F32, "ExternalInput")
    wo_in = dram("wout", [D, D], F32, "ExternalInput")
    bo_in = dram("bout", [D], F32, "ExternalInput")
    lns_in = dram("ln_scale", [D], F32, "ExternalInput")
    lnb_in = dram("ln_bias", [D], F32, "ExternalInput")
    awt_in = dram("attn_wT", [D, D], F32, "ExternalInput")
    ab_in = dram("attn_b", [D], F32, "ExternalInput")
    nwt_in = dram("next_wT", [D, D], F32, "ExternalInput")
    nb_in = dram("next_b", [D], F32, "ExternalInput")
    w1_in = [dram("w1a", [D, HID], F32, "ExternalInput"), dram("w1b", [D, HID], F32, "ExternalInput")]
    b1_in = [dram("b1a", [HID], F32, "ExternalInput"), dram("b1b", [HID], F32, "ExternalInput")]
    w2_in = [dram("w2a", [HID, D], F32, "ExternalInput"), dram("w2b", [HID, D], F32, "ExternalInput")]
    b2_in = [dram("b2a", [D], F32, "ExternalInput"), dram("b2b", [D], F32, "ExternalInput")]
    c_in = dram("cb_", [D], F32, "ExternalInput")
    wm_in = dram("wmod", [D, 6 * D], F32, "ExternalInput")
    idn_in = dram("idn", [128, 192], BF16, "ExternalInput")
    msk_in = dram("mask5", [128, 512], BF16, "ExternalInput")
    bon_in = dram("bones", [128, 8], BF16, "ExternalInput")
    x2s = dram("x2s", [T, D], BF16)
    y_out = dram("y", [T, D], F32, "ExternalOutput")
    dbg_t = {}
    if dbg:
        for nm, sh in [("d_h1t", [128, 8 * p1mt]), ("d_q", [128, 8 * p1mt]), ("d_kv", [128, 16 * p1mt]),
                       ("d_ao", [128, 8 * p1mt]), ("d_xat", [128, 8 * p2mt]), ("d_eo", [128, 8 * p2mt]),
                       ("d_eob", [128, 8 * p2mt]), ("d_fb", [128, 8 * p2mt]), ("d_hq", [128, 8 * p2mt]),
                       ("d_x2", [T, D])]:
            dbg_t[nm] = dram(nm, sh, F32, "ExternalOutput")

    es = ExitStack()
    S = lambda nm, sh, dt: es.enter_context(nc.sbuf_tensor(nm, sh, dt))
    P = lambda nm, sh, dt: es.enter_context(nc.psum_tensor(nm, sh, dt))

    IDN = S("IDN", [128, 192], BF16)
    MSK = S("MSK", [128, 512], BF16)
    BON = S("BON", [128, 8], BF16)
    EYEF = S("EYEF", [128, 128], F32)
    AWT = S("AWT", [128, 8 * D], BF16)
    NWT = S("NWT", [128, 8 * D], BF16)
    COLS = S("COLS", [128, 144], F32)
    B1C = S("B1C", [128, 64], F32)
    B2C = S("B2C", [128, 16], F32)
    DGA = S("DGA", [128, 8 * 128], BF16)
    DGB = S("DGB", [128, 8 * 128], BF16)
    STB = S("STB", [128, D], BF16)
    BTB = S("BTB", [128, D], BF16)
    SCR = S("SCR", [128, D], BF16)
    SCR2 = S("SCR2", [128, D], BF16)
    STAT = S("STAT", [128, 4 * max(nst1, nst2)], F32)
    STATB = S("STATB", [128, 4 * max(nst1, nst2)], F32)
    TPP = P("TPP", [128, 1024], BF16)
    MMP = [P("MMP0", [128, 512], F32), P("MMP1", [128, 512], F32)]
    X2P = P("X2P", [128, 512], F32)
    es_p1w = ExitStack()
    S1w = lambda nm, sh, dt: es_p1w.enter_context(nc.sbuf_tensor(nm, sh, dt))
    WQ = S1w("WQ", [128, 8 * D], BF16)
    WDN = S1w("WDN", [128, 8 * 256], BF16)
    WUP = S1w("WUP", [128, 2 * 2048], BF16)
    WO = S1w("WO", [128, 8 * D], BF16)
    es_pro = ExitStack()
    Spro = lambda nm, sh, dt: es_pro.enter_context(nc.sbuf_tensor(nm, sh, dt))
    Ppro = lambda nm, sh, dt: es_pro.enter_context(nc.psum_tensor(nm, sh, dt))
    WMK = Spro("WMK", [128, 2 * 6144], BF16)
    SCB = Spro("SCB", [128, 8], BF16)
    BAB = Spro("BAB", [128, 16], BF16)
    MODC = Spro("MODC", [128, 48], F32)
    MODR = Spro("MODR", [8, 256], F32)
    MODP = Ppro("MODP", [128, 8 * 48], F32)
    CRP = Ppro("CRP", [128, 24], F32)
    MRP = Ppro("MRP", [8, 256], F32)

    pe = Ctr(es, nc, "pe")
    dv = Ctr(es, nc, "dv")
    ac = Ctr(es, nc, "ac")
    gp_ = Ctr(es, nc, "gp")
    scon = Ctr(es, nc, "scon")
    sspill = Ctr(es, nc, "sspill")
    sswapp = [Ctr(es, nc, "sswap0"), Ctr(es, nc, "sswap1")]
    sx2l = [Ctr(es, nc, "sx2l0"), Ctr(es, nc, "sx2l1"), Ctr(es, nc, "sx2l2"), Ctr(es, nc, "sx2l3")]
    sout = Ctr(es, nc, "sout")
    gcols = Ctr(es, nc, "gcols")
    gwm = Ctr(es, nc, "gwm")
    gwk = [Ctr(es, nc, "gwk0"), Ctr(es, nc, "gwk1")]
    gx = [Ctr(es, nc, "gx0"), Ctr(es, nc, "gx1")]
    gmx = Ctr(es, nc, "gmx")
    gzf = Ctr(es, nc, "gzf")
    gstb = Ctr(es, nc, "gstb")
    gw1 = Ctr(es, nc, "gw1")
    gw2 = Ctr(es, nc, "gw2")
    gdbg = Ctr(es, nc, "gdbg")

    block = es.enter_context(nc.Block())
    on = lambda name: getattr(block, name)
    col = lambda i, n=1: COLS[:, i:i + n]

    @on("sync")
    def _(e):
        scon.inc(e.dma_start(out=IDN[:], in_=idn_in[:]), 16)
        scon.inc(e.dma_start(out=MSK[:], in_=msk_in[:]), 16)
        scon.inc(e.dma_start(out=BON[:], in_=bon_in[:]), 16)
    ev_consts = scon.n

    def col_load(e, dst, src_vec, nchunks):
        src = bass.AP(src_vec.tensor, src_vec.offset, [[1, 128], [128, nchunks], [1, 1]])
        gcols.inc(e.dma_start(out=dst.unsqueeze(2), in_=src, allow_slow_non_contiguous=True), 16)

    @on("gpsimd")
    def _(e):
        col_load(e, col(0, 8), bq_in, 8)
        col_load(e, col(8, 2), bdn_in, 2)
        col_load(e, col(10, 16), bup_in, 16)
        col_load(e, col(26, 8), bo_in, 8)
        col_load(e, col(34, 8), ab_in, 8)
        col_load(e, col(42, 8), nb_in, 8)
        col_load(e, col(50, 8), lns_in, 8)
        col_load(e, col(58, 8), lnb_in, 8)
        col_load(e, col(114, 8), c_in, 8)
        col_load(e, B1C[:, 0:32], b1_in[0], 32)
        col_load(e, B1C[:, 32:64], b1_in[1], 32)
        col_load(e, B2C[:, 0:8], b2_in[0], 8)
        col_load(e, B2C[:, 8:16], b2_in[1], 8)
    ev_cols = gcols.n

    @on("gpsimd")
    def _(e):
        for kc in range(8):
            gwm.inc(e.dma_start(out=WQ[:, kc * D:(kc + 1) * D], in_=wq_in[128 * kc:128 * (kc + 1), :]), 16)
        for kc in range(8):
            gwm.inc(e.dma_start(out=WDN[:, kc * 256:(kc + 1) * 256], in_=wdn_in[128 * kc:128 * (kc + 1), :]), 16)
        for kc in range(2):
            gwm.inc(e.dma_start(out=WUP[:, kc * 2048:(kc + 1) * 2048], in_=wup_in[128 * kc:128 * (kc + 1), :]), 16)
        for kc in range(8):
            gwm.inc(e.dma_start(out=WO[:, kc * D:(kc + 1) * D], in_=wo_in[128 * kc:128 * (kc + 1), :]), 16)
        for kc in range(8):
            gwm.inc(e.dma_start(out=AWT[:, kc * D:(kc + 1) * D], in_=awt_in[128 * kc:128 * (kc + 1), :]), 16)
        for kc in range(8):
            gwm.inc(e.dma_start(out=NWT[:, kc * D:(kc + 1) * D], in_=nwt_in[128 * kc:128 * (kc + 1), :]), 16)
    ev_wmain = gwm.n

    @on("scalar")
    def _(e):
        e.wait_ge(gcols.sem, ev_cols)
        ac.inc(e.activation(col(122, 8), col(114, 8), AF.Silu))
    ev_silu = ac.n

    @on("vector")
    def _(e):
        e.wait_ge(ac.sem, ev_silu)
        e.wait_ge(scon.sem, ev_consts)
        dv.inc(e.tensor_copy(SCB[:], col(122, 8)))
        dv.inc(e.tensor_copy(EYEF[:], IDN[:, 0:128]))
    ev_scb = dv.n

    ev_wmk, ev_modk = [], []
    for kc in range(8):
        @on("gpsimd")
        def _(e, kc=kc):
            if kc >= 2:
                e.wait_ge(pe.sem, ev_modk[kc - 2])
            gwk[kc % 2].inc(e.dma_start(out=WMK[:, (kc % 2) * 6144:(kc % 2 + 1) * 6144],
                               in_=wm_in[128 * kc:128 * (kc + 1), :]), 16)
        ev_wmk.append(gwk[kc % 2].n)

        @on("tensor")
        def _(e, kc=kc):
            if kc == 0:
                e.wait_ge(dv.sem, ev_scb)
            e.wait_ge(gwk[kc % 2].sem, ev_wmk[kc])
            for m in range(48):
                mm = e.matmul(MODP[:, kc * 48 + m:kc * 48 + m + 1],
                              WMK[:, (kc % 2) * 6144 + 128 * m:(kc % 2) * 6144 + 128 * (m + 1)],
                              SCB[:, kc:kc + 1], start=True, stop=True)
            pe.inc(mm)
        ev_modk.append(pe.n)
    ev_mod = pe.n

    @on("vector")
    def _(e):
        e.wait_ge(pe.sem, ev_mod)
        step = lambda i: (dv.inc(i), e.wait_ge(dv.sem, dv.n))
        step(e.tensor_copy(MODC[:], MODP[:, 0:48]))
        for kc in range(1, 8):
            step(e.tensor_add(MODC[:], MODC[:], MODP[:, kc * 48:(kc + 1) * 48]))
        dv.inc(e.engine_nop())
    ev_modc_dv = dv.n

    @on("vector")
    def _(e):
        e.wait_ge(dv.sem, ev_modc_dv)
        step = lambda i: (dv.inc(i), e.wait_ge(dv.sem, dv.n))
        e.tensor_copy(col(66, 8), MODC[:, 0:8])
        e.tensor_scalar_add(col(74, 8), MODC[:, 8:16], 1.0)
        e.tensor_copy(col(90, 8), MODC[:, 24:32])
        e.tensor_scalar_add(col(98, 8), MODC[:, 32:40], 1.0)
        step(e.tensor_scalar(col(130, 8), MODC[:, 16:24], 0.0, None, ALU.is_ge))
        step(e.tensor_scalar(col(130, 8), col(130, 8), 2.0, -1.0, ALU.mult, ALU.add))
        dv.inc(e.engine_nop())
    ev_gsgn = dv.n

    @on("scalar")
    def _(e):
        e.wait_ge(dv.sem, ev_gsgn)
        ac.inc(e.activation(col(82, 8), MODC[:, 16:24], AF.Abs))
    ev_gab = ac.n

    @on("vector")
    def _(e):
        e.wait_ge(ac.sem, ev_gab)
        step = lambda i: (dv.inc(i), e.wait_ge(dv.sem, dv.n))
        step(e.tensor_scalar_max(col(82, 8), col(82, 8), 0.25))
        step(e.tensor_mul(col(82, 8), col(82, 8), col(130, 8)))
        step(e.tensor_scalar(col(130, 8), MODC[:, 40:48], 0.0, None, ALU.is_ge))
        step(e.tensor_scalar(col(130, 8), col(130, 8), 2.0, -1.0, ALU.mult, ALU.add))
    ev_gbs = dv.n

    @on("scalar")
    def _(e):
        e.wait_ge(dv.sem, ev_gbs)
        ac.inc(e.activation(col(106, 8), MODC[:, 40:48], AF.Abs))
    ev_gbab = ac.n

    @on("vector")
    def _(e):
        e.wait_ge(ac.sem, ev_gbab)
        step = lambda i: (dv.inc(i), e.wait_ge(dv.sem, dv.n))
        step(e.tensor_scalar_max(col(106, 8), col(106, 8), 0.25))
        step(e.tensor_mul(col(106, 8), col(106, 8), col(130, 8)))
        e.wait_ge(gwm.sem, ev_wmain)
        e.tensor_copy(BAB[:, 0:8], col(66, 8))
        dv.inc(e.tensor_copy(BAB[:, 8:16], col(58, 8)))
    ev_bab = dv.n

    @on("tensor")
    def _(e):
        e.wait_ge(dv.sem, ev_bab)
        for dc in range(8):
            for kc in range(8):
                mm = e.matmul(CRP[:, dc:dc + 1], WQ[:, kc * D + dc * 128:kc * D + dc * 128 + 128],
                              BAB[:, kc:kc + 1], start=(kc == 0), stop=(kc == 7))
        for dc in range(2):
            for kc in range(8):
                mm = e.matmul(CRP[:, 8 + dc:9 + dc], WDN[:, kc * 256 + dc * 128:kc * 256 + dc * 128 + 128],
                              BAB[:, kc:kc + 1], start=(kc == 0), stop=(kc == 7))
        for dc in range(8):
            for kc in range(8):
                mm = e.matmul(CRP[:, 16 + dc:17 + dc], AWT[:, kc * D + dc * 128:kc * D + dc * 128 + 128],
                              BAB[:, 8 + kc:9 + kc], start=(kc == 0), stop=(kc == 7))
        pe.inc(mm)
    ev_corr = pe.n

    @on("tensor")
    def _(e):
        pe.inc(e.transpose(MRP[0:8, 0:128], col(90, 8), EYEF[:]))
        pe.inc(e.transpose(MRP[0:8, 128:256], col(98, 8), EYEF[:]))
    ev_mrp = pe.n

    @on("vector")
    def _(e):
        e.wait_ge(pe.sem, ev_corr)
        step = lambda i: (dv.inc(i), e.wait_ge(dv.sem, dv.n))
        step(e.tensor_add(col(0, 8), col(0, 8), CRP[:, 0:8]))
        step(e.tensor_scalar_mul(col(0, 8), col(0, 8), 0.125))
        step(e.tensor_add(col(8, 2), col(8, 2), CRP[:, 8:10]))
        step(e.tensor_add(col(34, 8), col(34, 8), CRP[:, 16:24]))
        for kc in range(8):
            step(e.tensor_scalar(col(131, 1), col(74 + kc, 1), 0.125, None, ALU.mult))
            step(e.tensor_scalar(WQ[:, kc * D:(kc + 1) * D], WQ[:, kc * D:(kc + 1) * D], col(131, 1), None, ALU.mult))
            step(e.tensor_scalar(WDN[:, kc * 256:(kc + 1) * 256], WDN[:, kc * 256:(kc + 1) * 256],
                                 col(74 + kc, 1), None, ALU.mult))
            step(e.tensor_scalar(AWT[:, kc * D:(kc + 1) * D], AWT[:, kc * D:(kc + 1) * D],
                                 col(50 + kc, 1), None, ALU.mult))
        for c in range(8):
            e.tensor_scalar(DGA[:, c * 128:(c + 1) * 128], IDN[:, 0:128], col(82 + c, 1), None, ALU.mult)
            e.tensor_scalar(DGB[:, c * 128:(c + 1) * 128], IDN[:, 0:128], col(106 + c, 1), None, ALU.mult)
        dv.inc(e.engine_nop())
    ev_prescale = dv.n

    @on("scalar")
    def _(e):
        e.wait_ge(pe.sem, ev_mrp)
        ac.inc(e.copy(MODR[:, :], MRP[:, :]))
    ev_modr = ac.n

    @on("gpsimd")
    def _(e):
        e.wait_ge(ac.sem, ev_modr)
        for c in range(8):
            srcB = bass.AP(MODR[0:1, 0:1].tensor, c * 256, [[256, 1], [0, 128], [1, 128]])
            gstb.inc(e.dma_start(out=BTB[:, c * 128:(c + 1) * 128], in_=srcB), 16)
            srcS = bass.AP(MODR[0:1, 0:1].tensor, c * 256 + 128, [[256, 1], [0, 128], [1, 128]])
            gstb.inc(e.dma_start(out=STB[:, c * 128:(c + 1) * 128], in_=srcS), 16)
    ev_stb = gstb.n

    def ln_stats(e, src, n, ST=None):
        ST = ST if ST is not None else STAT
        for st in range(n):
            e.tensor_scalar(SCR[:], src[:, st * D:(st + 1) * D], 1.0, 0.0, ALU.mult, ALU.add,
                            accum_out=ST[:, st:st + 1])
            s = e.scalar_tensor_tensor(SCR2[:], src[:, st * D:(st + 1) * D], 1.0,
                                       src[:, st * D:(st + 1) * D], ALU.mult, ALU.mult,
                                       accum_out=ST[:, n + st:n + st + 1])
            dv.inc(s)
            e.wait_ge(dv.sem, dv.n)
        step = lambda i: (dv.inc(i), e.wait_ge(dv.sem, dv.n))
        step(e.tensor_scalar_mul(ST[:, 0:n], ST[:, 0:n], 1.0 / D))
        step(e.tensor_scalar_mul(ST[:, n:2 * n], ST[:, n:2 * n], 1.0 / D))
        step(e.tensor_mul(ST[:, 2 * n:3 * n], ST[:, 0:n], ST[:, 0:n]))
        step(e.tensor_sub(ST[:, n:2 * n], ST[:, n:2 * n], ST[:, 2 * n:3 * n]))
        step(e.tensor_scalar_add(ST[:, n:2 * n], ST[:, n:2 * n], 1e-5))
        step(e.reciprocal(ST[:, n:2 * n], ST[:, n:2 * n]))

    def ln_rest(n, ST=None):
        ST = ST if ST is not None else STAT
        @on("scalar")
        def _(e):
            e.wait_ge(dv.sem, dv.n)
            a = e.activation(ST[:, 3 * n:4 * n], ST[:, n:2 * n], AF.Ln)
            ac.inc(a)
            e.wait_ge(ac.sem, ac.n)
            ac.inc(e.activation(ST[:, 2 * n:3 * n], ST[:, 3 * n:4 * n], AF.Exp, scale=0.5))
        @on("vector")
        def _(e):
            e.wait_ge(ac.sem, ac.n)
            dv.inc(e.scalar_tensor_tensor(ST[:, 3 * n:4 * n], ST[:, 0:n], -1.0,
                                          ST[:, 2 * n:3 * n], ALU.mult, ALU.mult))
        return dv.n

    es_pro.close()
    es_p1 = ExitStack()
    S1 = lambda nm, sh, dt: es_p1.enter_context(nc.sbuf_tensor(nm, sh, dt))
    P1 = lambda nm, sh, dt: es_p1.enter_context(nc.psum_tensor(nm, sh, dt))
    XB = S1("XB", [128, 2 * nst1 * D], BF16)
    Z1 = S1("Z1", [128, nst1 * D], BF16)
    XN1T = S1("XN1T", [128, 8 * p1mt], BF16)
    QFp = [S1("QF0", [128, 8 * p1mt], BF16), S1("QF1", [128, 8 * p1mt], BF16)]
    QSp = [S1("QS0", [128, 8 * p1mt], BF16), S1("QS1", [128, 8 * p1mt], BF16)]
    KVp = [S1("KV0", [128, 16 * p1mt], BF16), S1("KV1", [128, 16 * p1mt], BF16)]
    KHIp = [S1("KHI0", [128, 16 * p1mt], BF16), S1("KHI1", [128, 16 * p1mt], BF16)]
    LATT = S1("LATT", [128, 2 * p1mt], BF16)
    LHS = S1("LHS", [128, 32 * npair], BF16)
    RHSQ = S1("RHSQ", [128, 16 * npair], BF16)
    EB = S1("EB", [128, 2 * 16 * noct], BF16)
    BD = S1("BD", [128, 128 * noct], BF16)
    VOC = S1("VOC", [128, 64 * noct], BF16)
    VSTG = S1("VSTG", [128, 128 * noct], BF16)
    ZTS = S1("ZTS", [16, 8 * noct], BF16)
    ZFM = S1("ZFM", [128, 8 * chtok], BF16)
    MXC = S1("MXC", [128, 4], F32)
    MXR = S1("MXR", [1, 128], F32)
    MXBIAS = MXC[:, 1:2]
    AOUT = S1("AOUT", [128, 8 * p1mt], BF16)
    Y1B = S1("Y1B", [128, 8 * p1mt], BF16)
    X2B = S1("X2B", [128, nst1 * D], BF16)
    SCP = [P1("SCP0", [128, max(16 * noct, 512)], F32), P1("SCP1", [128, max(16 * noct, 512)], F32)]
    VOP = P1("VOP", [128, 512], BF16)
    ZTP = P1("ZTP", [16, 8 * noct], F32)

    ev_x = [0] * n1
    ev_x2 = [0] * n1
    ev_spill = [0] * n1
    evm = {}
    pv = dict(scores_pe=[0, 0], exp_ac=[0, 0], ebcons=[(0, 0, 0)] * 2,
              bd_vcons_pe=0, voc_pe=0, zts_gq=0, ztp_ac=0, zfm_dv=0, o2_dv=0,
              mx_gq=0, mxb_ac=0, x2b_sy=0, mx_dv=0)

    def stage_a(mt):
        par = mt % 2
        sl = par * nst1 * D
        QF, QS, KV, KHI = QFp[par], QSp[par], KVp[par], KHIp[par]

        @on("gpsimd")
        def _(e):
            if mt == 0:
                e.wait_ge(pe.sem, ev_mod)
                e.wait_ge(gstb.sem, ev_stb)
            if mt >= 2:
                e.wait_ge(dv.sem, ev_x2[mt - 2])
            gx[par].inc(e.dma_start(out=XB[:, sl:sl + nst1 * D].rearrange("p (s f) -> p s f", s=nst1),
                                    in_=x_in.rearrange("(m s p) f -> m p s f", s=nst1, p=128)[mt]), 16)
        ev_x[mt] = (par, gx[par].n)

        @on("vector")
        def _(e):
            e.wait_ge(gx[par].sem, ev_x[mt][1])
            if mt >= 1:
                e.wait_ge(ac.sem, evm[("z1", mt - 1)])
            ln_stats(e, XB[:, sl:sl + nst1 * D], nst1)
        ln_rest(nst1)
        ev_nmr = dv.n

        @on("scalar")
        def _(e):
            e.wait_ge(dv.sem, ev_nmr)
            if mt >= 1:
                e.wait_ge(pe.sem, evm[("z1busy", mt - 1)])
            for st in range(nst1):
                a = e.activation(Z1[:, st * D:(st + 1) * D], XB[:, sl + st * D:sl + (st + 1) * D],
                                 AF.Identity, bias=STAT[:, 3 * nst1 + st:3 * nst1 + st + 1],
                                 scale=STAT[:, 2 * nst1 + st:2 * nst1 + st + 1])
            ac.inc(a)
        ev_z1 = ac.n
        evm[("z1", mt)] = ac.n

        for st in range(nst1):
            @on("tensor")
            def _(e, st=st):
                e.wait_ge(ac.sem, ev_z1 if st == 0 else ac.n)
                for c in range(8):
                    t = e.transpose(TPP[:, c * 128:(c + 1) * 128],
                                    Z1[:, st * D + c * 128:st * D + (c + 1) * 128], IDN[:, 0:128])
                pe.inc(t)

            @on("scalar")
            def _(e, st=st):
                e.wait_ge(pe.sem, pe.n)
                ac.inc(e.copy(XN1T[:].rearrange("p (c t) -> p c t", c=8)[:, :, st * 128:(st + 1) * 128],
                              TPP[:].rearrange("p (c j) -> p c j", c=8)))
        evm[("z1busy", mt)] = pe.n
        ev_xn1t = ac.n

        ev_qev = [ac.n + dc + 1 for dc in range(8)]
        @on("tensor")
        def _(e, ev=ev_xn1t, dvb=dv.n):
            e.wait_ge(ac.sem, ev)
            e.wait_ge(dv.sem, ev_prescale)
            e.wait_ge(dv.sem, dvb)
            for dc in range(8):
                if dc >= 2:
                    e.wait_ge(ac.sem, ev_qev[dc - 2])
                for kc in range(8):
                    mm = e.matmul(MMP[dc % 2][:, 0:p1mt], WQ[:, kc * D + dc * 128:kc * D + dc * 128 + 128],
                                  XN1T[:, kc * p1mt:(kc + 1) * p1mt], start=(kc == 0), stop=(kc == 7))
                pe.inc(mm)
        base_q = pe.n - 8

        @on("scalar")
        def _(e, base=base_q):
            if mt >= 2:
                e.wait_ge(dv.sem, evm[("qfree_dv", mt - 2)])
            for dc in range(8):
                e.wait_ge(pe.sem, base + dc + 1)
                a = e.activation(QF[:, dc * p1mt:(dc + 1) * p1mt], MMP[dc % 2][:, 0:p1mt],
                                 AF.Identity, bias=col(dc, 1))
                ac.inc(a)
        assert ac.n == ev_qev[-1]
        ev_q = ac.n

        @on("tensor")
        def _(e, ev=ev_q, dvb=dv.n):
            e.wait_ge(ac.sem, ev)
            e.wait_ge(dv.sem, dvb)
            for dc in range(2):
                for kc in range(8):
                    mm = e.matmul(MMP[dc % 2][:, 0:p1mt], WDN[:, kc * 256 + dc * 128:kc * 256 + dc * 128 + 128],
                                  XN1T[:, kc * p1mt:(kc + 1) * p1mt], start=(kc == 0), stop=(kc == 7))
                pe.inc(mm)
        base_lat = pe.n - 2

        @on("scalar")
        def _(e, base=base_lat):
            for dc in range(2):
                e.wait_ge(pe.sem, base + dc + 1)
                a = e.activation(LATT[:, dc * p1mt:(dc + 1) * p1mt], MMP[dc % 2][:, 0:p1mt],
                                 AF.Identity, bias=col(8 + dc, 1))
                ac.inc(a)
        ev_lat = ac.n

        ev_kvev = [ac.n + gc + 1 for gc in range(16)]
        @on("tensor")
        def _(e, ev=ev_lat, dvb=dv.n):
            e.wait_ge(ac.sem, ev)
            e.wait_ge(dv.sem, dvb)
            for gc in range(16):
                if gc >= 2:
                    e.wait_ge(ac.sem, ev_kvev[gc - 2])
                for kc in range(2):
                    mm = e.matmul(MMP[gc % 2][:, 0:p1mt], WUP[:, kc * 2048 + gc * 128:kc * 2048 + gc * 128 + 128],
                                  LATT[:, kc * p1mt:(kc + 1) * p1mt], start=(kc == 0), stop=(kc == 1))
                pe.inc(mm)
        base_kv = pe.n - 16

        @on("scalar")
        def _(e, base=base_kv):
            if mt >= 2:
                e.wait_ge(dv.sem, evm[("kvfree_dv", mt - 2)])
            for gc in range(16):
                e.wait_ge(pe.sem, base + gc + 1)
                a = e.activation(KV[:, gc * p1mt:(gc + 1) * p1mt], MMP[gc % 2][:, 0:p1mt],
                                 AF.Identity, bias=col(10 + gc, 1))
                ac.inc(a)
        assert ac.n == ev_kvev[-1]
        ev_kv = ac.n
        evm[("kv", mt)] = ac.n
        evm[("q", mt)] = ev_q

        @on("sync")
        def _(e, evq=ev_q, evk=ev_kv):
            e.wait_ge(ac.sem, evq)
            if mt >= 2:
                e.wait_ge(dv.sem, evm[("qfree_dv", mt - 2)])
            sswapp[par].inc(e.dma_start(out=QS[0:64, :], in_=QF[64:128, :]), 16)
            sswapp[par].inc(e.dma_start(out=QS[64:128, :], in_=QF[0:64, :]), 16)
            e.wait_ge(ac.sem, evk)
            sswapp[par].inc(e.dma_start(out=KHI[64:128, :], in_=KV[0:64, :]), 16)
        evm[("sw", mt)] = sswapp[par].n

    def stage_b(mt):
        par = mt % 2
        sl = par * nst1 * D
        QF, QS, KV, KHI = QFp[par], QSp[par], KVp[par], KHIp[par]
        ev_sw = evm[("sw", mt)]
        ev_kv = evm[("kv", mt)]

        for ch in range(nch_a):
            t0 = ch * chtok
            bsl = 0
            rsl = 0
            esl = (ch % 2) * 16 * noct
            chi = mt * nch_a + ch
            cpar = chi % 2

            @on("vector")
            def _(e, t0=t0, chi=chi, ev=ev_kv):
                e.wait_ge(ac.sem, ev)
                if chi >= 1:
                    e.wait_ge(pe.sem, pv["voc_pe"])
                dv.inc(e.tensor_copy(
                    VSTG[64:128, :].rearrange("p (x g) -> p x g", g=16),
                    KV[64:128, :].rearrange("p (g t) -> p t g", g=16)[:, t0:t0 + chtok, :]))
            ev_vs_dv = dv.n

            @on("vector")
            def _(e, t0=t0, chi=chi, cpar=cpar):
                if chi == 0:
                    e.wait_ge(scon.sem, ev_consts)
                    e.wait_ge(pe.sem, ev_mod)
                    e.wait_ge(gstb.sem, ev_stb)
                    dv.inc(e.memset(LHS[:], 0.0))
                    e.wait_ge(dv.sem, dv.n)
                e.wait_ge(sswapp[par].sem, ev_sw)
                if chi >= 1:
                    e.wait_ge(pe.sem, pv["scores_pe"][(chi - 1) % 2])
                e.tensor_copy(LHS[0:64, bsl:bsl + 32 * npair].rearrange("p (x g) -> p x g", g=32)[:, :, 0:16],
                              KV[0:64, :].rearrange("p (g t) -> p t g", g=16)[:, t0:t0 + chtok:2, :])
                e.tensor_copy(LHS[64:128, bsl:bsl + 32 * npair].rearrange("p (x g) -> p x g", g=32)[:, :, 16:32],
                              KHI[64:128, :].rearrange("p (g t) -> p t g", g=16)[:, t0 + 1:t0 + chtok:2, :])
                e.tensor_copy(RHSQ[0:64, rsl:rsl + 16 * npair].rearrange("p (x h) -> p x h", h=16)[:, :, 0:16:2],
                              QF[0:64, :].rearrange("p (c t) -> p t c", c=8)[:, t0:t0 + chtok:2, :])
                e.tensor_copy(RHSQ[0:64, rsl:rsl + 16 * npair].rearrange("p (x h) -> p x h", h=16)[:, :, 1:16:2],
                              QS[0:64, :].rearrange("p (c t) -> p t c", c=8)[:, t0:t0 + chtok:2, :])
                e.tensor_copy(RHSQ[64:128, rsl:rsl + 16 * npair].rearrange("p (x h) -> p x h", h=16)[:, :, 0:16:2],
                              QS[64:128, :].rearrange("p (c t) -> p t c", c=8)[:, t0 + 1:t0 + chtok:2, :])
                dv.inc(e.tensor_copy(RHSQ[64:128, rsl:rsl + 16 * npair].rearrange("p (x h) -> p x h", h=16)[:, :, 1:16:2],
                                     QF[64:128, :].rearrange("p (c t) -> p t c", c=8)[:, t0 + 1:t0 + chtok:2, :]))
            ev_build = dv.n

            @on("tensor")
            def _(e, chi=chi, cpar=cpar, ev=ev_build):
                e.wait_ge(dv.sem, ev)
                if chi >= 2:
                    e.wait_ge(ac.sem, pv["exp_ac"][cpar])
                    e.wait_ge(dv.sem, pv["mx_dv"])
                for o in range(noct):
                    for j in range(4):
                        pr = 4 * o + j
                        mm = e.matmul(SCP[ch % 2][32 * j:32 * j + 32, 16 * o:16 * o + 16],
                                      LHS[:, bsl + 32 * pr:bsl + 32 * pr + 32],
                                      RHSQ[:, rsl + 16 * pr:rsl + 16 * pr + 16],
                                      start=True, stop=True, tile_position=(0, 32 * j))
                pe.inc(mm)
            ev_sc = pe.n
            pv["scores_pe"][cpar] = ev_sc

            @on("vector")
            def _(e, ch=ch, ev=ev_sc):
                e.wait_ge(pe.sem, ev)
                e.wait_ge(gmx.sem, pv["mx_gq"])
                dv.inc(e.reduce_max(MXC[:, 0:1], SCP[ch % 2][:, 0:16 * noct], axis=AX.X))
            ev_mx = dv.n
            pv["mx_dv"] = ev_mx

            @on("scalar")
            def _(e, ev=ev_mx):
                e.wait_ge(dv.sem, ev)
                gmx.inc(e.dma_start(out=MXR[0:1, 0:128], in_=MXC[:, 0:1]), 16)
            ev_mxr = gmx.n

            @on("vector")
            def _(e, ev=ev_mxr):
                e.wait_ge(gmx.sem, ev)
                dv.inc(e.tensor_reduce(MXR[0:1, 0:1], MXR[0:1, 0:128], AX.X, ALU.max, negate=True))
            ev_mneg = dv.n

            @on("scalar")
            def _(e, ev=ev_mneg):
                e.wait_ge(dv.sem, ev)
                e.wait_ge(ac.sem, pv["mxb_ac"])
                gmx.inc(e.dma_start(out=MXBIAS, in_=bcast_row(MXR[0:1, 0:1], 128)), 16)
            ev_nmx = gmx.n
            pv["mx_gq"] = gmx.n

            @on("scalar")
            def _(e, ch=ch, esl=esl, chi=chi, cpar=cpar, ev=ev_nmx):
                e.wait_ge(gmx.sem, ev)
                if chi >= 2:
                    zp, bdv, bgp = pv["ebcons"][cpar]
                    e.wait_ge(pe.sem, zp)
                    e.wait_ge(dv.sem, bdv)
                ac.inc(e.activation(EB[:, esl:esl + 16 * noct], SCP[ch % 2][:, 0:16 * noct], AF.Exp, bias=MXBIAS))
            ev_eb = ac.n
            pv["exp_ac"][cpar] = ev_eb
            pv["mxb_ac"] = ev_eb

            @on("vector")
            def _(e, esl=esl, chi=chi, ev=ev_eb):
                e.wait_ge(ac.sem, ev)
                if chi >= 1:
                    e.wait_ge(pe.sem, pv["bd_vcons_pe"]) 
                for par in range(2):
                    bb = e.tensor_tensor(
                        BD[:].rearrange("p (o a t h) -> p o a t h", o=noct, a=2, t=8)[:, :, par],
                        EB[:, esl:esl + 16 * noct].rearrange("p (o h) -> p o h", o=noct)[:, :, par::2].unsqueeze(2).broadcast_to([128, noct, 8, 8]),
                        MSK[:, 0:128].rearrange("p (t h) -> p t h", t=8)[:, :, par::2].unsqueeze(1).broadcast_to([128, noct, 8, 8]),
                        ALU.mult)
                dv.inc(bb)
            ev_bd_dv = dv.n
            ev_bd_ac = dv.n

            @on("tensor")
            def _(e, esl=esl, chi=chi, evb=ev_eb):
                e.wait_ge(ac.sem, evb)
                if chi >= 1:
                    e.wait_ge(ac.sem, pv["ztp_ac"])
                for o in range(noct):
                    mm = e.matmul(ZTP[:, 8 * o:8 * o + 8], EB[:, esl + 16 * o:esl + 16 * o + 16], BON[:],
                                  start=True, stop=True)
                pe.inc(mm)
            ev_zmm = pe.n

            @on("scalar")
            def _(e, chi=chi, ev=ev_zmm):
                e.wait_ge(pe.sem, ev)
                if chi >= 1:
                    e.wait_ge(gzf.sem, pv["zts_gq"])
                ac.inc(e.copy(ZTS[:], ZTP[:]))
            ev_zts = ac.n
            pv["ztp_ac"] = ev_zts

            @on("vector")
            def _(e, ev=ev_zts):
                e.wait_ge(ac.sem, ev)
                with nc.allow_low_precision(reason="softmax denominator reciprocal in bf16; tolerance-checked"):
                    dv.inc(e.reciprocal(ZTS[:], ZTS[:]))
            ev_zrec = dv.n

            @on("gpsimd")
            def _(e, chi=chi, ev=ev_zrec):
                e.wait_ge(dv.sem, ev)
                if chi >= 1:
                    e.wait_ge(dv.sem, pv["zfm_dv"])
                zp = 8 * noct
                for cc in range(8):
                    s2 = bass.AP(ZTS[0:1, 0:1].tensor, 2 * cc * zp, [[zp, 2], [0, 64], [1, zp]])
                    gzf.inc(e.dma_start(out=ZFM[:, cc * chtok:(cc + 1) * chtok], in_=s2), 16)
            ev_zfm = gzf.n
            pv["zts_gq"] = gzf.n

            ev_vev = []
            for w in range(nw):
                @on("tensor")
                def _(e, w=w, evd=ev_vs_dv):
                    if w == 0:
                        e.wait_ge(dv.sem, evd)
                    else:
                        e.wait_ge(ac.sem, ev_vev[w - 1])
                    for o in range(8 * w, 8 * w + 8):
                        t = e.transpose(VOP[:, (o % 8) * 64:(o % 8) * 64 + 64],
                                        VSTG[64:128, o * 128:(o + 1) * 128],
                                        IDN[64:128, 128:192])
                    pe.inc(t)

                @on("scalar")
                def _(e, w=w):
                    e.wait_ge(pe.sem, pe.n)
                    ac.inc(e.copy(VOC[:, w * 512:(w + 1) * 512], VOP[:]))
                ev_vev.append(ac.n)

            ev_o2w = []
            for w in range(nw):
                @on("tensor")
                def _(e, w=w, chi=chi, evd=ev_bd_ac):
                    O2P = X2P
                    e.wait_ge(ac.sem, ev_vev[w])
                    if w == 0:
                        e.wait_ge(dv.sem, evd)
                        if chi >= 1:
                            e.wait_ge(dv.sem, pv["o2_dv"])
                    else:
                        e.wait_ge(dv.sem, ev_o2w[w - 1])
                    for o in range(8 * w, 8 * w + 8):
                        off = (o % 8) * 64
                        e.matmul(O2P[0:64, off:off + 64], VOC[:, o * 64:o * 64 + 64],
                                 BD[:, o * 128:o * 128 + 64],
                                 start=True, stop=True)
                        mm = e.matmul(O2P[64:128, off:off + 64], VOC[:, o * 64:o * 64 + 64],
                                      BD[:, o * 128 + 64:o * 128 + 128],
                                      start=True, stop=True)
                    pe.inc(mm)

                @on("vector")
                def _(e, t0=t0, w=w, chi=chi, evz=ev_zfm):
                    O2P = X2P
                    e.wait_ge(gzf.sem, evz)
                    e.wait_ge(pe.sem, pe.n)
                    for half in range(2):
                        a = e.scalar_tensor_tensor(
                            AOUT[half * 64:(half + 1) * 64, :].rearrange("p (c t) -> p c t", c=8)[:, :, t0 + w * 64:t0 + (w + 1) * 64],
                            O2P[half * 64:(half + 1) * 64, 0:512].rearrange("p (x h) -> p h x", h=8),
                            1.0,
                            ZFM[half * 64:(half + 1) * 64, :].rearrange("p (c t) -> p c t", c=8)[:, :, w * 64:(w + 1) * 64],
                            ALU.mult, ALU.mult)
                    dv.inc(a)
                ev_o2w.append(dv.n)
            pv["zfm_dv"] = dv.n
            pv["voc_pe"] = pe.n
            pv["bd_vcons_pe"] = pe.n
            pv["ebcons"][cpar] = (ev_zmm, ev_bd_dv, 0)
        ev_aout = dv.n
        evm[("qfree_dv", mt)] = dv.n
        evm[("kvfree_dv", mt)] = dv.n

        ev_y1ev = [dv.n + dc + 1 for dc in range(8)]
        @on("tensor")
        def _(e, ev=ev_aout, acb=ac.n):
            e.wait_ge(dv.sem, ev)
            e.wait_ge(ac.sem, acb)
            for dc in range(8):
                if dc >= 2:
                    e.wait_ge(dv.sem, ev_y1ev[dc - 2])
                for kc in range(8):
                    mm = e.matmul(MMP[dc % 2][:, 0:p1mt], WO[:, kc * D + dc * 128:kc * D + dc * 128 + 128],
                                  AOUT[:, kc * p1mt:(kc + 1) * p1mt], start=(kc == 0), stop=(kc == 7))
                pe.inc(mm)
        base_wo = pe.n - 8

        @on("vector")
        def _(e, base=base_wo):
            for dc in range(8):
                e.wait_ge(pe.sem, base + dc + 1)
                a = e.tensor_scalar(Y1B[:, dc * p1mt:(dc + 1) * p1mt], MMP[dc % 2][:, 0:p1mt],
                                    col(26 + dc, 1), None, ALU.add)
                dv.inc(a)
        assert dv.n == ev_y1ev[-1]
        ev_y1 = dv.n

        ev_x2w = []
        for st in range(nst1):
            for hb in range(2):
                @on("tensor")
                def _(e, st=st, hb=hb, ev=ev_y1):
                    if st == 0 and hb == 0:
                        e.wait_ge(dv.sem, ev)
                    else:
                        e.wait_ge(dv.sem, ev_x2w[-1])
                    for c in range(4 * hb, 4 * hb + 4):
                        mm = e.matmul(X2P[:, (c % 4) * 128:(c % 4) * 128 + 128],
                                      Y1B[:, c * p1mt + st * 128:c * p1mt + st * 128 + 128],
                                      DGA[:, c * 128:(c + 1) * 128], start=True, stop=True)
                    pe.inc(mm)

                @on("vector")
                def _(e, st=st, hb=hb):
                    e.wait_ge(pe.sem, pe.n)
                    if mt >= 1 and st == 0 and hb == 0:
                        e.wait_ge(sspill.sem, pv["x2b_sy"])
                    a = e.scalar_tensor_tensor(X2B[:, st * D + hb * 512:st * D + (hb + 1) * 512],
                                               X2P[:], 1.0,
                                               XB[:, sl + st * D + hb * 512:sl + st * D + (hb + 1) * 512],
                                               ALU.mult, ALU.add)
                    dv.inc(a)
                ev_x2w.append(dv.n)
        ev_x2[mt] = dv.n
        pv["o2_dv"] = dv.n

        @on("sync")
        def _(e, ev=ev_x2[mt]):
            e.wait_ge(dv.sem, ev)
            sspill.inc(e.dma_start(out=x2s.rearrange("(m s p) f -> m p s f", s=nst1, p=128)[mt],
                                   in_=X2B[:].rearrange("p (s f) -> p s f", s=nst1)), 16)
        ev_spill[mt] = sspill.n
        pv["x2b_sy"] = sspill.n

    if n1 >= 2:
        stage_a(0)
        stage_a(1)
        for m in range(2, n1):
            stage_b(m - 2)
            stage_a(m)
        stage_b(n1 - 2)
        stage_b(n1 - 1)
    else:
        stage_a(0)
        stage_b(0)

    ev_p1done = sspill.n
    if dbg:
        @on("gpsimd")
        def _(e):
            e.wait_ge(sspill.sem, ev_p1done)
            gdbg.inc(e.dma_start(out=dbg_t["d_h1t"][:], in_=XN1T[:]), 16)
            gdbg.inc(e.dma_start(out=dbg_t["d_q"][:], in_=QF[:]), 16)
            gdbg.inc(e.dma_start(out=dbg_t["d_kv"][:], in_=KV[:]), 16)
            gdbg.inc(e.dma_start(out=dbg_t["d_ao"][:], in_=AOUT[:]), 16)
            gdbg.inc(e.dma_start(out=dbg_t["d_x2"].rearrange("(m s p) f -> m p s f", s=nst1, p=128)[n1 - 1],
                               in_=X2B[:].rearrange("p (s f) -> p s f", s=nst1)), 16)

    ev_p1done = sspill.n
    ev_p1done_gq = gdbg.n
    es_p1.close()
    es_p1w.close()
    es_p2 = ExitStack()
    S2 = lambda nm, sh, dt: es_p2.enter_context(nc.sbuf_tensor(nm, sh, dt))
    X2L = S2("X2L", [128, mt_hg * nst2 * D], BF16)
    Z2P = [S2("Z2a", [128, nst2 * D], BF16), S2("Z2b", [128, nst2 * D], BF16)]
    H2P = [S2("H2a", [128, nst2 * D], BF16), S2("H2b", [128, nst2 * D], BF16)]
    XN3T = S2("XN3T", [128, 8 * p2mt], BF16)
    X2T = S2("X2T", [128, 8 * p2mt], BF16)
    XAT = S2("XAT", [128, mt_hg * 8 * p2mt], BF16)
    W1S = S2("W1S", [128, 8 * 1024], BF16)
    W2S = S2("W2S", [128, 8 * 1024], BF16)
    HQ = S2("HQ", [128, 8 * p2mt], BF16)
    EO = S2("EO", [128, mt_hg * 8 * p2mt], F32)
    EOB = S2("EOB", [128, mt_hg * 8 * p2mt], BF16)
    FB = Z2P[0]
    OUTB = S2("OUTB", [128, D], F32)
    P2_ = lambda nm, sh, dt: es_p2.enter_context(nc.psum_tensor(nm, sh, dt))
    MM2 = [MMP[0], MMP[1], P2_("MM2a", [128, 512], F32), P2_("MM2b", [128, 512], F32)]

    p2 = dict(x2l_dv=0, w1_pe=0, w2_pe=0, outb_sy=0, z2_dv=0, h2_ac=0,
              eo_last={}, eob_pe={}, statfree_ac=0)

    for hg in range(n_hg):
        eidx = (hg * mt_hg * p2mt * 2) // T
        ev_xat = [0] * mt_hg
        if "ch_ev" not in p2:
            p2["ch_ev"] = {}
        ch_ev = p2["ch_ev"]

        def pro_chain(mtg, hg=hg):
            mtglob = hg * mt_hg + mtg
            xsl = mtg * nst2 * D
            par = mtg % 2
            ST = STAT if par == 0 else STATB
            Z2 = Z2P[par]
            H2 = H2P[par]

            @on("sync")
            def _(e):
                if hg == 0 and mtg == 0:
                    e.wait_ge(sspill.sem, ev_p1done)
                    if dbg:
                        e.wait_ge(gdbg.sem, ev_p1done_gq)
                if hg >= 1:
                    e.wait_ge(dv.sem, p2["x2l_dv"])
                sx2l[mtg % 4].inc(e.dma_start(out=X2L[:, xsl:xsl + nst2 * D].rearrange("p (s f) -> p s f", s=nst2),
                                              in_=x2s.rearrange("(m s p) f -> m p s f", s=nst2, p=128)[mtglob]), 16)
            ev_x2l = sx2l[mtg % 4].n

            @on("vector")
            def _(e):
                e.wait_ge(sx2l[mtg % 4].sem, ev_x2l)
                e.wait_ge(gstb.sem, ev_stb)
                if ("st_free", par) in ch_ev:
                    e.wait_ge(ac.sem, ch_ev[("st_free", par)])
                ln_stats(e, X2L[:, xsl:xsl + nst2 * D], nst2, ST)
            ln_rest(nst2, ST)

            @on("scalar")
            def _(e):
                e.wait_ge(dv.sem, dv.n)
                if ("z2_free", par) in ch_ev:
                    e.wait_ge(pe.sem, ch_ev[("z2_free", par)])
                for st in range(nst2):
                    a = e.activation(Z2[:, st * D:(st + 1) * D], X2L[:, xsl + st * D:xsl + (st + 1) * D],
                                     AF.Identity, bias=ST[:, 3 * nst2 + st:3 * nst2 + st + 1],
                                     scale=ST[:, 2 * nst2 + st:2 * nst2 + st + 1])
                ac.inc(a)
            ev_z2 = ac.n

            @on("vector")
            def _(e):
                e.wait_ge(ac.sem, ev_z2)
                if ("h2_free", par) in ch_ev:
                    e.wait_ge(ac.sem, ch_ev[("h2_free", par)])
                for st in range(nst2):
                    h = e.scalar_tensor_tensor(H2[:, st * D:(st + 1) * D], Z2[:, st * D:(st + 1) * D],
                                               1.0, STB[:], ALU.mult, ALU.mult)
                dv.inc(h)
                e.wait_ge(dv.sem, dv.n)
                for st in range(nst2):
                    h = e.tensor_add(H2[:, st * D:(st + 1) * D], H2[:, st * D:(st + 1) * D], BTB[:])
                dv.inc(h)
                e.wait_ge(dv.sem, dv.n)
                ln_stats(e, H2[:, 0:nst2 * D], nst2, ST)
            ln_rest(nst2, ST)
            ev_nmr3 = dv.n

            @on("scalar")
            def _(e):
                e.wait_ge(dv.sem, ev_nmr3)
                for st in range(nst2):
                    a = e.activation(Z2[:, st * D:(st + 1) * D], H2[:, st * D:(st + 1) * D],
                                     AF.Identity, bias=ST[:, 3 * nst2 + st:3 * nst2 + st + 1],
                                     scale=ST[:, 2 * nst2 + st:2 * nst2 + st + 1])
                ac.inc(a)
            ch_ev[("xn3", mtg)] = ac.n
            ch_ev[("st_free", par)] = ac.n
            ch_ev[("h2_free", par)] = ac.n

        def pro_pe(mtg, hg=hg):
            xsl = mtg * nst2 * D
            par = mtg % 2
            XN3 = Z2P[par]
            ev_xn3 = ch_ev[("xn3", mtg)]
            for st in range(nst2):
                @on("tensor")
                def _(e, st=st):
                    e.wait_ge(ac.sem, ev_xn3 if st == 0 else ac.n)
                    for c in range(8):
                        t = e.transpose(TPP[:, c * 128:(c + 1) * 128],
                                        XN3[:, st * D + c * 128:st * D + (c + 1) * 128], IDN[:, 0:128])
                    pe.inc(t)

                @on("scalar")
                def _(e, st=st):
                    e.wait_ge(pe.sem, pe.n)
                    ac.inc(e.copy(XN3T[:].rearrange("p (c t) -> p c t", c=8)[:, :, st * 128:(st + 1) * 128],
                                  TPP[:].rearrange("p (c j) -> p c j", c=8)))
            ch_ev[("z2_free", par)] = pe.n
            for st in range(nst2):
                @on("tensor")
                def _(e, st=st):
                    e.wait_ge(ac.sem, ac.n)
                    for c in range(8):
                        t = e.transpose(TPP[:, c * 128:(c + 1) * 128],
                                        X2L[:, xsl + st * D + c * 128:xsl + st * D + (c + 1) * 128], IDN[:, 0:128])
                    pe.inc(t)

                @on("scalar")
                def _(e, st=st):
                    e.wait_ge(pe.sem, pe.n)
                    if mtg >= 1 and st == 0:
                        e.wait_ge(dv.sem, ev_xat[mtg - 1])
                    ac.inc(e.copy(X2T[:].rearrange("p (c t) -> p c t", c=8)[:, :, st * 128:(st + 1) * 128],
                                  TPP[:].rearrange("p (c j) -> p c j", c=8)))
            ev_x2t = ac.n

            ev_xaev = [dv.n + dc + 1 for dc in range(8)]
            @on("tensor")
            def _(e, ev=ev_x2t, dvb=dv.n):
                e.wait_ge(ac.sem, ev)
                e.wait_ge(dv.sem, dvb)
                for dc in range(8):
                    if dc >= 4:
                        e.wait_ge(dv.sem, ev_xaev[dc - 4])
                    for kc in range(8):
                        mm = e.matmul(MM2[dc % 4][:, 0:p2mt], AWT[:, kc * D + dc * 128:kc * D + dc * 128 + 128],
                                      XN3T[:, kc * p2mt:(kc + 1) * p2mt], start=(kc == 0), stop=(kc == 7))
                    pe.inc(mm)
            base_xa = pe.n - 8

            @on("vector")
            def _(e, mtg=mtg, base=base_xa):
                for dc in range(8):
                    e.wait_ge(pe.sem, base + dc + 1)
                    a = e.scalar_tensor_tensor(XAT[:, mtg * 8 * p2mt + dc * p2mt:mtg * 8 * p2mt + (dc + 1) * p2mt],
                                               MM2[dc % 4][:, 0:p2mt], col(34 + dc, 1),
                                               X2T[:, dc * p2mt:(dc + 1) * p2mt], ALU.add, ALU.add)
                    dv.inc(a)
            assert dv.n == ev_xaev[-1], ("xaev", dv.n, ev_xaev[-1])
            ev_xat[mtg] = dv.n

        sched = []
        for m in range(mt_hg):
            sched.append(("c", m))
        order = []
        if mt_hg >= 2:
            order = [("c", 0), ("c", 1)]
            for m in range(2, mt_hg):
                order.append(("p", m - 2))
                order.append(("c", m))
            order.append(("p", mt_hg - 2))
            order.append(("p", mt_hg - 1))
        else:
            order = [("c", 0), ("p", 0)]
        for kind, m in order:
            if kind == "c":
                pro_chain(m)
            else:
                pro_pe(m)

        for wq in range(4):
            @on("gpsimd")
            def _(e, wq=wq, eidx=eidx):
                e.wait_ge(sspill.sem, ev_p1done)
                if dbg:
                    e.wait_ge(gdbg.sem, ev_p1done_gq)
                e.wait_ge(pe.sem, p2["w1_pe"])
                for kc in range(8):
                    gw1.inc(e.dma_start(out=W1S[:, kc * 1024:(kc + 1) * 1024],
                                       in_=w1_in[eidx][128 * kc:128 * (kc + 1), 1024 * wq:1024 * (wq + 1)]), 16)
            ev_w1l = gw1.n

            @on("gpsimd")
            def _(e, wq=wq, eidx=eidx):
                e.wait_ge(pe.sem, p2["w2_pe"])
                for kc in range(8):
                    gw2.inc(e.dma_start(out=W2S[:, kc * 1024:(kc + 1) * 1024],
                                       in_=w2_in[eidx][1024 * wq + 128 * kc:1024 * wq + 128 * (kc + 1), :]), 16)
            ev_w2l = gw2.n

            for mtg in range(mt_hg):
                ev_hqev = [ac.n + hc + 1 for hc in range(8)]
                @on("tensor")
                def _(e, wq=wq, mtg=mtg, ev=ev_w1l, dvb=dv.n):
                    e.wait_ge(gw1.sem, ev)
                    e.wait_ge(dv.sem, ev_xat[mtg])
                    e.wait_ge(dv.sem, dvb)
                    for hc in range(8):
                        if hc >= 4:
                            e.wait_ge(ac.sem, ev_hqev[hc - 4])
                        for kc in range(8):
                            mm = e.matmul(MM2[hc % 4][:, 0:p2mt], W1S[:, kc * 1024 + hc * 128:kc * 1024 + hc * 128 + 128],
                                          XAT[:, mtg * 8 * p2mt + kc * p2mt:mtg * 8 * p2mt + (kc + 1) * p2mt],
                                          start=(kc == 0), stop=(kc == 7))
                        pe.inc(mm)
                base_h = pe.n - 8
                p2["w1_pe"] = pe.n if (mtg == mt_hg - 1) else p2["w1_pe"]

                @on("scalar")
                def _(e, wq=wq, mtg=mtg, eidx=eidx, base=base_h):
                    for hc in range(8):
                        e.wait_ge(pe.sem, base + hc + 1)
                        a = e.activation(HQ[:, hc * p2mt:(hc + 1) * p2mt],
                                         MM2[hc % 4][:, 0:p2mt], AF.Gelu,
                                         bias=B1C[:, eidx * 32 + wq * 8 + hc:eidx * 32 + wq * 8 + hc + 1])
                        ac.inc(a)
                assert ac.n == ev_hqev[-1], ("hqev", ac.n, ev_hqev[-1])
                ev_hq = ac.n

                ev_eoev = [dv.n + dc + 1 for dc in range(8)]
                @on("tensor")
                def _(e, wq=wq, mtg=mtg, ev2=ev_w2l, evh=ev_hq, dvb=dv.n):
                    e.wait_ge(gw2.sem, ev2)
                    e.wait_ge(ac.sem, evh)
                    e.wait_ge(dv.sem, dvb)
                    for dc in range(8):
                        if dc >= 4:
                            e.wait_ge(dv.sem, ev_eoev[dc - 4])
                        for kc in range(8):
                            mm = e.matmul(MM2[dc % 4][:, 0:p2mt], W2S[:, kc * 1024 + dc * 128:kc * 1024 + dc * 128 + 128],
                                          HQ[:, kc * p2mt:(kc + 1) * p2mt],
                                          start=(kc == 0), stop=(kc == 7))
                        pe.inc(mm)
                base_eo = pe.n - 8
                p2["w2_pe"] = pe.n if (mtg == mt_hg - 1) else p2["w2_pe"]

                @on("vector")
                def _(e, wq=wq, mtg=mtg, eidx=eidx, base=base_eo):
                    for dc in range(8):
                        e.wait_ge(pe.sem, base + dc + 1)
                        eo_sl = EO[:, mtg * 8 * p2mt + dc * p2mt:mtg * 8 * p2mt + (dc + 1) * p2mt]
                        key = (mtg, dc)
                        if wq == 0:
                            if key in p2["eo_last"]:
                                e.wait_ge(dv.sem, p2["eo_last"][key])
                            a = e.tensor_copy(eo_sl, MM2[dc % 4][:, 0:p2mt])
                        elif wq < 3:
                            e.wait_ge(dv.sem, p2["eo_last"][key])
                            a = e.tensor_add(eo_sl, eo_sl, MM2[dc % 4][:, 0:p2mt])
                        else:
                            e.wait_ge(dv.sem, p2["eo_last"][key])
                            if key in p2["eob_pe"]:
                                e.wait_ge(pe.sem, p2["eob_pe"][key])
                            eob_sl = EOB[:, mtg * 8 * p2mt + dc * p2mt:mtg * 8 * p2mt + (dc + 1) * p2mt]
                            a = e.scalar_tensor_tensor(eob_sl, MM2[dc % 4][:, 0:p2mt],
                                                       B2C[:, eidx * 8 + dc:eidx * 8 + dc + 1],
                                                       eo_sl, ALU.add, ALU.add)
                        dv.inc(a)
                        p2["eo_last"][key] = dv.n
                    assert dv.n == ev_eoev[-1], ("eoev", dv.n, ev_eoev[-1])

        for mtg in range(mt_hg):
            mtglob = hg * mt_hg + mtg
            xsl = mtg * nst2 * D
            ev_fbev = [ac.n + dc + 1 for dc in range(8)]
            @on("tensor")
            def _(e, mtg=mtg, dvb=dv.n, acb=ac.n):
                e.wait_ge(dv.sem, dvb)
                e.wait_ge(ac.sem, acb)
                for dcc in range(8):
                    e.wait_ge(dv.sem, p2["eo_last"][(mtg, dcc)])
                for dc in range(8):
                    if dc >= 2:
                        e.wait_ge(ac.sem, ev_fbev[dc - 2])
                    for kc in range(8):
                        mm = e.matmul(MMP[dc % 2][:, 0:p2mt], NWT[:, kc * D + dc * 128:kc * D + dc * 128 + 128],
                                      EOB[:, mtg * 8 * p2mt + kc * p2mt:mtg * 8 * p2mt + (kc + 1) * p2mt],
                                      start=(kc == 0), stop=(kc == 7))
                    pe.inc(mm)
            base_f = pe.n - 8
            for dcc in range(8):
                p2["eob_pe"][(mtg, dcc)] = pe.n

            @on("scalar")
            def _(e, base=base_f):
                for dc in range(8):
                    e.wait_ge(pe.sem, base + dc + 1)
                    a = e.activation(FB[:, dc * p2mt:(dc + 1) * p2mt], MMP[dc % 2][:, 0:p2mt],
                                     AF.Gelu, bias=col(42 + dc, 1))
                    ac.inc(a)
            assert ac.n == ev_fbev[-1], ("fbev", ac.n, ev_fbev[-1])
            ev_fb = ac.n

            ev_ow = []
            for st in range(nst2):
                for hb in range(2):
                    @on("tensor")
                    def _(e, st=st, hb=hb, ev=ev_fb):
                        if st == 0 and hb == 0:
                            e.wait_ge(ac.sem, ev)
                        else:
                            e.wait_ge(dv.sem, ev_ow[-1])
                        for c in range(4 * hb, 4 * hb + 4):
                            mm = e.matmul(X2P[:, (c % 4) * 128:(c % 4) * 128 + 128],
                                          FB[:, c * p2mt + st * 128:c * p2mt + st * 128 + 128],
                                          DGB[:, c * 128:(c + 1) * 128], start=True, stop=True)
                        pe.inc(mm)

                    @on("vector")
                    def _(e, st=st, hb=hb, xsl=xsl):
                        e.wait_ge(pe.sem, pe.n)
                        if p2["outb_sy"]:
                            e.wait_ge(sout.sem, p2["outb_sy"])
                        a = e.scalar_tensor_tensor(OUTB[:, hb * 512:(hb + 1) * 512],
                                                   X2P[:], 1.0,
                                                   X2L[:, xsl + st * D + hb * 512:xsl + st * D + (hb + 1) * 512],
                                                   ALU.mult, ALU.add)
                        dv.inc(a)
                    ev_ow.append(dv.n)

                @on("sync")
                def _(e, mtglob=mtglob, st=st, ev=dv.n):
                    e.wait_ge(dv.sem, ev)
                    sout.inc(e.dma_start(
                        out=y_out.rearrange("(m s p) f -> m s p f", s=nst2, p=128)[mtglob, st],
                        in_=OUTB[:]), 16)
                p2["outb_sy"] = sout.n
            ev_out = dv.n
        p2["x2l_dv"] = dv.n

        if dbg and hg == 0:
            @on("gpsimd")
            def _(e):
                e.wait_ge(dv.sem, dv.n)
                gdbg.inc(e.dma_start(out=dbg_t["d_xat"][:], in_=XAT[:, 0:8 * p2mt]), 16)
                gdbg.inc(e.dma_start(out=dbg_t["d_eo"][:], in_=EO[:, 0:8 * p2mt]), 16)
                gdbg.inc(e.dma_start(out=dbg_t["d_eob"][:], in_=EOB[:, 0:8 * p2mt]), 16)
                gdbg.inc(e.dma_start(out=dbg_t["d_fb"][:], in_=FB[:]), 16)
                gdbg.inc(e.dma_start(out=dbg_t["d_hq"][:], in_=HQ[:]), 16)

    @on("sync")
    def _(e):
        e.wait_ge(sout.sem, sout.n)
        e.wait_ge(sspill.sem, sspill.n)
    @on("gpsimd")
    def _(e):
        e.wait_ge(gw1.sem, gw1.n)
        e.wait_ge(gw2.sem, gw2.n)
        e.wait_ge(gdbg.sem, gdbg.n)
        e.wait_ge(gp_.sem, gp_.n)
    @on("vector")
    def _(e):
        e.wait_ge(sout.sem, sout.n)

    es_p2.close()
    es.close()
    return nc


_NC_CACHE = {}


def _get_nc():
    key = "real"
    if key not in _NC_CACHE:
        _NC_CACHE[key] = build(T=4096, p1mt=256, p2mt=256, mt_per_hg=4, dbg=False)
    return _NC_CACHE[key]


def kernel(**inputs):
    from concourse.bass_utils import run_bass_kernel_spmd

    consts = host_consts()
    nc = _get_nc()

    awt = np.ascontiguousarray(inputs["L0_ffn_attn_w"].T)
    nwt = np.ascontiguousarray(inputs["L0_ffn_next_w"].T)
    x = np.ascontiguousarray(inputs["L0_attn_x"])
    w1 = inputs["L0_ffn_w1"]
    b1 = inputs["L0_ffn_b1"]
    w2 = inputs["L0_ffn_w2"]
    b2 = inputs["L0_ffn_b2"]

    shared = {
        "wq": np.ascontiguousarray(inputs["L0_attn_Wq"]),
        "bq": np.ascontiguousarray(inputs["L0_attn_bq"]),
        "wdown": np.ascontiguousarray(inputs["L0_attn_Wdown"]),
        "bdown": np.ascontiguousarray(inputs["L0_attn_bdown"]),
        "wup": np.ascontiguousarray(inputs["L0_attn_Wup"]),
        "bup": np.ascontiguousarray(inputs["L0_attn_bup"]),
        "wout": np.ascontiguousarray(inputs["L0_attn_Wout"]),
        "bout": np.ascontiguousarray(inputs["L0_attn_bout"]),
        "ln_scale": np.ascontiguousarray(inputs["L0_ffn_ln_scale"]),
        "ln_bias": np.ascontiguousarray(inputs["L0_ffn_ln_bias"]),
        "attn_wT": awt,
        "attn_b": np.ascontiguousarray(inputs["L0_ffn_attn_b"]),
        "next_wT": nwt,
        "next_b": np.ascontiguousarray(inputs["L0_ffn_next_b"]),
        "wmod": np.ascontiguousarray(inputs["L0_wmod"]),
    }
    shared.update(consts)

    in_maps = []
    for cid in range(8):
        b, half = cid // 2, cid % 2
        e0 = 4 * (b % 2) + 2 * half
        m = dict(shared)
        m["x"] = np.ascontiguousarray(x[b, half * 4096:(half + 1) * 4096, :])
        m["cb_"] = np.ascontiguousarray(inputs["L0_c"][b])
        m["w1a"] = np.ascontiguousarray(w1[e0])
        m["w1b"] = np.ascontiguousarray(w1[e0 + 1])
        m["b1a"] = np.ascontiguousarray(b1[e0])
        m["b1b"] = np.ascontiguousarray(b1[e0 + 1])
        m["w2a"] = np.ascontiguousarray(w2[e0])
        m["w2b"] = np.ascontiguousarray(w2[e0 + 1])
        m["b2a"] = np.ascontiguousarray(b2[e0])
        m["b2b"] = np.ascontiguousarray(b2[e0 + 1])
        in_maps.append(m)

    res = run_bass_kernel_spmd(nc, in_maps, core_ids=list(range(8)))
    out = np.empty((4, 8192, 1024), np.float32)
    for cid in range(8):
        b, half = cid // 2, cid % 2
        out[b, half * 4096:(half + 1) * 4096, :] = res.results[cid]["y"]
    return out


# version 19
# speedup vs baseline: 1.0937x; 1.0221x over previous
import numpy as np
import concourse.bass as bass
import concourse.mybir as mybir
from contextlib import ExitStack

F32 = mybir.dt.float32
BF16 = mybir.dt.bfloat16
AF = mybir.ActivationFunctionType
ALU = mybir.AluOpType
AX = mybir.AxisListType

D = 1024
HID = 4096


def bcast_row(ap, nparts):
    dims = [list(d) for d in ap.ap]
    return bass.AP(ap.tensor, ap.offset, [dims[0], [0, nparts]] + dims[1:])


def host_consts():
    import ml_dtypes
    B = ml_dtypes.bfloat16
    idn = np.zeros((128, 192), np.float32)
    for i in range(128):
        idn[i, i] = 1.0
    for i in range(64):
        idn[64 + i, 128 + i] = 1.0
    mask = np.zeros((128, 512), np.float32)
    for p in range(128):
        t8 = p // 16
        for o in range(4):
            mask[p, o * 128 + t8 * 16: o * 128 + t8 * 16 + 16] = 1.0
    bones = np.zeros((128, 8), np.float32)
    for p in range(128):
        bones[p, p // 16] = 1.0
    return {"idn": idn.astype(B), "mask5": mask.astype(B), "bones": bones.astype(B)}


class Ctr:
    def __init__(self, es, nc, name):
        self.sem = es.enter_context(nc.semaphore(name))
        self.n = 0

    def inc(self, instr, k=1):
        instr.then_inc(self.sem, k)
        self.n += k
        return self.n


def build(T=4096, p1mt=512, p2mt=512, mt_per_hg=2, dbg=False):
    n1 = T // p1mt
    n2 = T // p2mt
    nst1 = p1mt // 128
    nst2 = p2mt // 128
    chtok = min(256, p1mt)
    nch_a = p1mt // chtok
    noct = chtok // 8
    npair = chtok // 2
    nw = noct // 8
    n_hg = max(1, n2 // mt_per_hg)
    mt_hg = n2 // n_hg

    nc = bass.Bass("TRN2", target_bir_lowering=False)

    def dram(nm, sh, dt, kind=None):
        t = nc.dram_tensor(nm, sh, dt, kind=kind) if kind else nc.dram_tensor(nm, sh, dt)
        return t.ap()

    x_in = dram("x", [T, D], F32, "ExternalInput")
    wq_in = dram("wq", [D, D], F32, "ExternalInput")
    bq_in = dram("bq", [D], F32, "ExternalInput")
    wdn_in = dram("wdown", [D, 256], F32, "ExternalInput")
    bdn_in = dram("bdown", [256], F32, "ExternalInput")
    wup_in = dram("wup", [256, 2048], F32, "ExternalInput")
    bup_in = dram("bup", [2048], F32, "ExternalInput")
    wo_in = dram("wout", [D, D], F32, "ExternalInput")
    bo_in = dram("bout", [D], F32, "ExternalInput")
    lns_in = dram("ln_scale", [D], F32, "ExternalInput")
    lnb_in = dram("ln_bias", [D], F32, "ExternalInput")
    awt_in = dram("attn_wT", [D, D], F32, "ExternalInput")
    ab_in = dram("attn_b", [D], F32, "ExternalInput")
    nwt_in = dram("next_wT", [D, D], F32, "ExternalInput")
    nb_in = dram("next_b", [D], F32, "ExternalInput")
    w1_in = [dram("w1a", [D, HID], F32, "ExternalInput"), dram("w1b", [D, HID], F32, "ExternalInput")]
    b1_in = [dram("b1a", [HID], F32, "ExternalInput"), dram("b1b", [HID], F32, "ExternalInput")]
    w2_in = [dram("w2a", [HID, D], F32, "ExternalInput"), dram("w2b", [HID, D], F32, "ExternalInput")]
    b2_in = [dram("b2a", [D], F32, "ExternalInput"), dram("b2b", [D], F32, "ExternalInput")]
    c_in = dram("cb_", [D], F32, "ExternalInput")
    wm_in = dram("wmod", [D, 6 * D], F32, "ExternalInput")
    idn_in = dram("idn", [128, 192], BF16, "ExternalInput")
    msk_in = dram("mask5", [128, 512], BF16, "ExternalInput")
    bon_in = dram("bones", [128, 8], BF16, "ExternalInput")
    x2s = dram("x2s", [T, D], BF16)
    y_out = dram("y", [T, D], F32, "ExternalOutput")
    dbg_t = {}
    if dbg:
        for nm, sh in [("d_h1t", [128, 8 * p1mt]), ("d_q", [128, 8 * p1mt]), ("d_kv", [128, 16 * p1mt]),
                       ("d_ao", [128, 8 * p1mt]), ("d_xat", [128, 8 * p2mt]), ("d_eo", [128, 8 * p2mt]),
                       ("d_eob", [128, 8 * p2mt]), ("d_fb", [128, 8 * p2mt]), ("d_hq", [128, 8 * p2mt]),
                       ("d_x2", [T, D])]:
            dbg_t[nm] = dram(nm, sh, F32, "ExternalOutput")

    es = ExitStack()
    S = lambda nm, sh, dt: es.enter_context(nc.sbuf_tensor(nm, sh, dt))
    P = lambda nm, sh, dt: es.enter_context(nc.psum_tensor(nm, sh, dt))

    IDN = S("IDN", [128, 192], BF16)
    MSK = S("MSK", [128, 128], BF16)
    BON = S("BON", [128, 8], BF16)
    EYEF = S("EYEF", [128, 128], F32)
    AWT = S("AWT", [128, 8 * D], BF16)
    NWT = S("NWT", [128, 8 * D], BF16)
    COLS = S("COLS", [128, 144], F32)
    B1C = S("B1C", [128, 64], F32)
    B2C = S("B2C", [128, 16], F32)
    DGA = S("DGA", [128, 8 * 128], BF16)
    DGB = S("DGB", [128, 8 * 128], BF16)
    STB = S("STB", [128, D], BF16)
    BTB = S("BTB", [128, D], BF16)
    SCR = S("SCR", [128, D], BF16)
    STAT = S("STAT", [128, 4 * max(nst1, nst2)], F32)
    STATB = S("STATB", [128, 4 * max(nst1, nst2)], F32)
    TPP = P("TPP", [128, 1024], BF16)
    MMP = [P("MMP0", [128, 512], F32), P("MMP1", [128, 512], F32)]
    X2P = P("X2P", [128, 512], F32)
    es_p1w = ExitStack()
    S1w = lambda nm, sh, dt: es_p1w.enter_context(nc.sbuf_tensor(nm, sh, dt))
    WQ = S1w("WQ", [128, 8 * D], BF16)
    WDN = S1w("WDN", [128, 8 * 256], BF16)
    WUP = S1w("WUP", [128, 2 * 2048], BF16)
    WO = S1w("WO", [128, 8 * D], BF16)
    es_pro = ExitStack()
    Spro = lambda nm, sh, dt: es_pro.enter_context(nc.sbuf_tensor(nm, sh, dt))
    Ppro = lambda nm, sh, dt: es_pro.enter_context(nc.psum_tensor(nm, sh, dt))
    WMK = Spro("WMK", [128, 2 * 6144], BF16)
    SCB = Spro("SCB", [128, 8], BF16)
    BAB = Spro("BAB", [128, 16], BF16)
    MODC = Spro("MODC", [128, 48], F32)
    MODR = Spro("MODR", [8, 256], F32)
    MODP = Ppro("MODP", [128, 8 * 48], F32)
    CRP = Ppro("CRP", [128, 24], F32)
    MRP = Ppro("MRP", [8, 256], F32)

    pe = Ctr(es, nc, "pe")
    dv = Ctr(es, nc, "dv")
    ac = Ctr(es, nc, "ac")
    gp_ = Ctr(es, nc, "gp")
    scon = Ctr(es, nc, "scon")
    sspill = Ctr(es, nc, "sspill")
    sswapp = [Ctr(es, nc, "sswap0"), Ctr(es, nc, "sswap1")]
    sx2l = [Ctr(es, nc, "sx2l0"), Ctr(es, nc, "sx2l1"), Ctr(es, nc, "sx2l2"), Ctr(es, nc, "sx2l3")]
    sout = Ctr(es, nc, "sout")
    gcols = Ctr(es, nc, "gcols")
    gwm = Ctr(es, nc, "gwm")
    gwk = [Ctr(es, nc, "gwk0"), Ctr(es, nc, "gwk1")]
    gx = [Ctr(es, nc, "gx0"), Ctr(es, nc, "gx1"), Ctr(es, nc, "gx2")]
    gmx = Ctr(es, nc, "gmx")
    gzf = Ctr(es, nc, "gzf")
    gstb = Ctr(es, nc, "gstb")
    gw1 = Ctr(es, nc, "gw1")
    gw2 = Ctr(es, nc, "gw2")
    gdbg = Ctr(es, nc, "gdbg")

    block = es.enter_context(nc.Block())
    on = lambda name: getattr(block, name)
    col = lambda i, n=1: COLS[:, i:i + n]

    @on("sync")
    def _(e):
        scon.inc(e.dma_start(out=IDN[:], in_=idn_in[:]), 16)
        scon.inc(e.dma_start(out=MSK[:], in_=msk_in[:, 0:128]), 16)
        scon.inc(e.dma_start(out=BON[:], in_=bon_in[:]), 16)
    ev_consts = scon.n

    def col_load(e, dst, src_vec, nchunks):
        src = bass.AP(src_vec.tensor, src_vec.offset, [[1, 128], [128, nchunks], [1, 1]])
        gcols.inc(e.dma_start(out=dst.unsqueeze(2), in_=src, allow_slow_non_contiguous=True), 16)

    @on("gpsimd")
    def _(e):
        col_load(e, col(0, 8), bq_in, 8)
        col_load(e, col(8, 2), bdn_in, 2)
        col_load(e, col(10, 16), bup_in, 16)
        col_load(e, col(26, 8), bo_in, 8)
        col_load(e, col(34, 8), ab_in, 8)
        col_load(e, col(42, 8), nb_in, 8)
        col_load(e, col(50, 8), lns_in, 8)
        col_load(e, col(58, 8), lnb_in, 8)
        col_load(e, col(114, 8), c_in, 8)
        col_load(e, B1C[:, 0:32], b1_in[0], 32)
        col_load(e, B1C[:, 32:64], b1_in[1], 32)
        col_load(e, B2C[:, 0:8], b2_in[0], 8)
        col_load(e, B2C[:, 8:16], b2_in[1], 8)
    ev_cols = gcols.n

    @on("gpsimd")
    def _(e):
        for kc in range(8):
            gwm.inc(e.dma_start(out=WQ[:, kc * D:(kc + 1) * D], in_=wq_in[128 * kc:128 * (kc + 1), :]), 16)
        for kc in range(8):
            gwm.inc(e.dma_start(out=WDN[:, kc * 256:(kc + 1) * 256], in_=wdn_in[128 * kc:128 * (kc + 1), :]), 16)
        for kc in range(2):
            gwm.inc(e.dma_start(out=WUP[:, kc * 2048:(kc + 1) * 2048], in_=wup_in[128 * kc:128 * (kc + 1), :]), 16)
        for kc in range(8):
            gwm.inc(e.dma_start(out=WO[:, kc * D:(kc + 1) * D], in_=wo_in[128 * kc:128 * (kc + 1), :]), 16)
        for kc in range(8):
            gwm.inc(e.dma_start(out=AWT[:, kc * D:(kc + 1) * D], in_=awt_in[128 * kc:128 * (kc + 1), :]), 16)
        for kc in range(8):
            gwm.inc(e.dma_start(out=NWT[:, kc * D:(kc + 1) * D], in_=nwt_in[128 * kc:128 * (kc + 1), :]), 16)
    ev_wmain = gwm.n

    @on("scalar")
    def _(e):
        e.wait_ge(gcols.sem, ev_cols)
        ac.inc(e.activation(col(122, 8), col(114, 8), AF.Silu))
    ev_silu = ac.n

    @on("vector")
    def _(e):
        e.wait_ge(ac.sem, ev_silu)
        e.wait_ge(scon.sem, ev_consts)
        dv.inc(e.tensor_copy(SCB[:], col(122, 8)))
        dv.inc(e.tensor_copy(EYEF[:], IDN[:, 0:128]))
    ev_scb = dv.n

    ev_wmk, ev_modk = [], []
    for kc in range(8):
        @on("gpsimd")
        def _(e, kc=kc):
            if kc >= 2:
                e.wait_ge(pe.sem, ev_modk[kc - 2])
            gwk[kc % 2].inc(e.dma_start(out=WMK[:, (kc % 2) * 6144:(kc % 2 + 1) * 6144],
                               in_=wm_in[128 * kc:128 * (kc + 1), :]), 16)
        ev_wmk.append(gwk[kc % 2].n)

        @on("tensor")
        def _(e, kc=kc):
            if kc == 0:
                e.wait_ge(dv.sem, ev_scb)
            e.wait_ge(gwk[kc % 2].sem, ev_wmk[kc])
            for m in range(48):
                mm = e.matmul(MODP[:, kc * 48 + m:kc * 48 + m + 1],
                              WMK[:, (kc % 2) * 6144 + 128 * m:(kc % 2) * 6144 + 128 * (m + 1)],
                              SCB[:, kc:kc + 1], start=True, stop=True)
            pe.inc(mm)
        ev_modk.append(pe.n)
    ev_mod = pe.n

    @on("vector")
    def _(e):
        e.wait_ge(pe.sem, ev_mod)
        step = lambda i: (dv.inc(i), e.wait_ge(dv.sem, dv.n))
        step(e.tensor_copy(MODC[:], MODP[:, 0:48]))
        for kc in range(1, 8):
            step(e.tensor_add(MODC[:], MODC[:], MODP[:, kc * 48:(kc + 1) * 48]))
        dv.inc(e.engine_nop())
    ev_modc_dv = dv.n

    @on("vector")
    def _(e):
        e.wait_ge(dv.sem, ev_modc_dv)
        step = lambda i: (dv.inc(i), e.wait_ge(dv.sem, dv.n))
        e.tensor_copy(col(66, 8), MODC[:, 0:8])
        e.tensor_scalar_add(col(74, 8), MODC[:, 8:16], 1.0)
        e.tensor_copy(col(90, 8), MODC[:, 24:32])
        e.tensor_scalar_add(col(98, 8), MODC[:, 32:40], 1.0)
        step(e.tensor_scalar(col(130, 8), MODC[:, 16:24], 0.0, None, ALU.is_ge))
        step(e.tensor_scalar(col(130, 8), col(130, 8), 2.0, -1.0, ALU.mult, ALU.add))
        dv.inc(e.engine_nop())
    ev_gsgn = dv.n

    @on("scalar")
    def _(e):
        e.wait_ge(dv.sem, ev_gsgn)
        ac.inc(e.activation(col(82, 8), MODC[:, 16:24], AF.Abs))
    ev_gab = ac.n

    @on("vector")
    def _(e):
        e.wait_ge(ac.sem, ev_gab)
        step = lambda i: (dv.inc(i), e.wait_ge(dv.sem, dv.n))
        step(e.tensor_scalar_max(col(82, 8), col(82, 8), 0.25))
        step(e.tensor_mul(col(82, 8), col(82, 8), col(130, 8)))
        step(e.tensor_scalar(col(130, 8), MODC[:, 40:48], 0.0, None, ALU.is_ge))
        step(e.tensor_scalar(col(130, 8), col(130, 8), 2.0, -1.0, ALU.mult, ALU.add))
    ev_gbs = dv.n

    @on("scalar")
    def _(e):
        e.wait_ge(dv.sem, ev_gbs)
        ac.inc(e.activation(col(106, 8), MODC[:, 40:48], AF.Abs))
    ev_gbab = ac.n

    @on("vector")
    def _(e):
        e.wait_ge(ac.sem, ev_gbab)
        step = lambda i: (dv.inc(i), e.wait_ge(dv.sem, dv.n))
        step(e.tensor_scalar_max(col(106, 8), col(106, 8), 0.25))
        step(e.tensor_mul(col(106, 8), col(106, 8), col(130, 8)))
        e.wait_ge(gwm.sem, ev_wmain)
        e.tensor_copy(BAB[:, 0:8], col(66, 8))
        dv.inc(e.tensor_copy(BAB[:, 8:16], col(58, 8)))
    ev_bab = dv.n

    @on("tensor")
    def _(e):
        e.wait_ge(dv.sem, ev_bab)
        for dc in range(8):
            for kc in range(8):
                mm = e.matmul(CRP[:, dc:dc + 1], WQ[:, kc * D + dc * 128:kc * D + dc * 128 + 128],
                              BAB[:, kc:kc + 1], start=(kc == 0), stop=(kc == 7))
        for dc in range(2):
            for kc in range(8):
                mm = e.matmul(CRP[:, 8 + dc:9 + dc], WDN[:, kc * 256 + dc * 128:kc * 256 + dc * 128 + 128],
                              BAB[:, kc:kc + 1], start=(kc == 0), stop=(kc == 7))
        for dc in range(8):
            for kc in range(8):
                mm = e.matmul(CRP[:, 16 + dc:17 + dc], AWT[:, kc * D + dc * 128:kc * D + dc * 128 + 128],
                              BAB[:, 8 + kc:9 + kc], start=(kc == 0), stop=(kc == 7))
        pe.inc(mm)
    ev_corr = pe.n

    @on("tensor")
    def _(e):
        pe.inc(e.transpose(MRP[0:8, 0:128], col(90, 8), EYEF[:]))
        pe.inc(e.transpose(MRP[0:8, 128:256], col(98, 8), EYEF[:]))
    ev_mrp = pe.n

    @on("vector")
    def _(e):
        e.wait_ge(pe.sem, ev_corr)
        step = lambda i: (dv.inc(i), e.wait_ge(dv.sem, dv.n))
        step(e.tensor_add(col(0, 8), col(0, 8), CRP[:, 0:8]))
        step(e.tensor_scalar_mul(col(0, 8), col(0, 8), 0.125))
        step(e.tensor_add(col(8, 2), col(8, 2), CRP[:, 8:10]))
        step(e.tensor_add(col(34, 8), col(34, 8), CRP[:, 16:24]))
        for kc in range(8):
            step(e.tensor_scalar(col(131, 1), col(74 + kc, 1), 0.125, None, ALU.mult))
            step(e.tensor_scalar(WQ[:, kc * D:(kc + 1) * D], WQ[:, kc * D:(kc + 1) * D], col(131, 1), None, ALU.mult))
            step(e.tensor_scalar(WDN[:, kc * 256:(kc + 1) * 256], WDN[:, kc * 256:(kc + 1) * 256],
                                 col(74 + kc, 1), None, ALU.mult))
            step(e.tensor_scalar(AWT[:, kc * D:(kc + 1) * D], AWT[:, kc * D:(kc + 1) * D],
                                 col(50 + kc, 1), None, ALU.mult))
        for c in range(8):
            e.tensor_scalar(DGA[:, c * 128:(c + 1) * 128], IDN[:, 0:128], col(82 + c, 1), None, ALU.mult)
            e.tensor_scalar(DGB[:, c * 128:(c + 1) * 128], IDN[:, 0:128], col(106 + c, 1), None, ALU.mult)
        dv.inc(e.engine_nop())
    ev_prescale = dv.n

    @on("scalar")
    def _(e):
        e.wait_ge(pe.sem, ev_mrp)
        ac.inc(e.copy(MODR[:, :], MRP[:, :]))
    ev_modr = ac.n

    @on("gpsimd")
    def _(e):
        e.wait_ge(ac.sem, ev_modr)
        for c in range(8):
            srcB = bass.AP(MODR[0:1, 0:1].tensor, c * 256, [[256, 1], [0, 128], [1, 128]])
            gstb.inc(e.dma_start(out=BTB[:, c * 128:(c + 1) * 128], in_=srcB), 16)
            srcS = bass.AP(MODR[0:1, 0:1].tensor, c * 256 + 128, [[256, 1], [0, 128], [1, 128]])
            gstb.inc(e.dma_start(out=STB[:, c * 128:(c + 1) * 128], in_=srcS), 16)
    ev_stb = gstb.n

    def ln_stats(e, src, n, ST=None):
        ST = ST if ST is not None else STAT
        for st in range(n):
            i1 = e.tensor_scalar(SCR[:], src[:, st * D:(st + 1) * D], 1.0, 0.0, ALU.mult, ALU.add,
                                 accum_out=ST[:, st:st + 1])
            dv.inc(i1)
            e.wait_ge(dv.sem, dv.n)
            s = e.scalar_tensor_tensor(SCR[:], src[:, st * D:(st + 1) * D], 1.0,
                                       src[:, st * D:(st + 1) * D], ALU.mult, ALU.mult,
                                       accum_out=ST[:, n + st:n + st + 1])
            dv.inc(s)
            e.wait_ge(dv.sem, dv.n)
        step = lambda i: (dv.inc(i), e.wait_ge(dv.sem, dv.n))
        step(e.tensor_scalar_mul(ST[:, 0:n], ST[:, 0:n], 1.0 / D))
        step(e.tensor_scalar_mul(ST[:, n:2 * n], ST[:, n:2 * n], 1.0 / D))
        step(e.tensor_mul(ST[:, 2 * n:3 * n], ST[:, 0:n], ST[:, 0:n]))
        step(e.tensor_sub(ST[:, n:2 * n], ST[:, n:2 * n], ST[:, 2 * n:3 * n]))
        step(e.tensor_scalar_add(ST[:, n:2 * n], ST[:, n:2 * n], 1e-5))
        step(e.reciprocal(ST[:, n:2 * n], ST[:, n:2 * n]))

    def ln_rest(n, ST=None):
        ST = ST if ST is not None else STAT
        @on("scalar")
        def _(e):
            e.wait_ge(dv.sem, dv.n)
            a = e.activation(ST[:, 3 * n:4 * n], ST[:, n:2 * n], AF.Ln)
            ac.inc(a)
            e.wait_ge(ac.sem, ac.n)
            ac.inc(e.activation(ST[:, 2 * n:3 * n], ST[:, 3 * n:4 * n], AF.Exp, scale=0.5))
        @on("vector")
        def _(e):
            e.wait_ge(ac.sem, ac.n)
            dv.inc(e.scalar_tensor_tensor(ST[:, 3 * n:4 * n], ST[:, 0:n], -1.0,
                                          ST[:, 2 * n:3 * n], ALU.mult, ALU.mult))
        return dv.n

    es_pro.close()
    es_p1 = ExitStack()
    S1 = lambda nm, sh, dt: es_p1.enter_context(nc.sbuf_tensor(nm, sh, dt))
    P1 = lambda nm, sh, dt: es_p1.enter_context(nc.psum_tensor(nm, sh, dt))
    XB = S1("XB", [128, 3 * nst1 * D], BF16)
    Z1 = S1("Z1", [128, nst1 * D], BF16)
    XN1T = S1("XN1T", [128, 8 * p1mt], BF16)
    QFp = [S1("QF0", [128, 8 * p1mt], BF16), S1("QF1", [128, 8 * p1mt], BF16)]
    QSp = [S1("QS0", [128, 8 * p1mt], BF16), S1("QS1", [128, 8 * p1mt], BF16)]
    KVp = [S1("KV0", [128, 16 * p1mt], BF16), S1("KV1", [128, 16 * p1mt], BF16)]
    KHIp = [S1("KHI0", [128, 16 * p1mt], BF16), S1("KHI1", [128, 16 * p1mt], BF16)]
    LATT = S1("LATT", [128, 2 * p1mt], BF16)
    LHS = S1("LHS", [128, 32 * npair], BF16)
    RHSQ = S1("RHSQ", [128, 16 * npair], BF16)
    EB = S1("EB", [128, 16 * noct], BF16)
    BD = S1("BD", [128, 128 * noct], BF16)
    VOC = S1("VOC", [128, 64 * noct], BF16)
    VSTG = S1("VSTG", [128, 128 * noct], BF16)
    ZTS = S1("ZTS", [16, 8 * noct], BF16)
    ZFM = S1("ZFM", [128, 8 * chtok], BF16)
    MXC = S1("MXC", [128, 4], F32)
    MXR = S1("MXR", [1, 128], F32)
    MXBIAS = MXC[:, 1:2]
    AOUT = S1("AOUT", [128, 8 * p1mt], BF16)
    Y1B = S1("Y1B", [128, 8 * p1mt], BF16)
    X2B = S1("X2B", [128, nst1 * D], BF16)
    SCP = [P1("SCP0", [128, max(16 * noct, 512)], F32), P1("SCP1", [128, max(16 * noct, 512)], F32)]
    VOP = P1("VOP", [128, 512], BF16)
    ZTP = P1("ZTP", [16, 8 * noct], F32)

    ev_x = [0] * n1
    ev_x2 = [0] * n1
    ev_spill = [0] * n1
    evm = {}
    pv = dict(scores_pe=[0, 0], exp_ac=[0, 0], ebcons=[(0, 0, 0)] * 2,
              bd_vcons_pe=0, voc_pe=0, zts_gq=0, ztp_ac=0, zfm_dv=0, o2_dv=0,
              mx_gq=0, mxb_ac=0, x2b_sy=0, mx_dv=0)

    def stage_a(mt):
        par = mt % 2
        xpar = mt % 3
        sl = xpar * nst1 * D
        QF, QS, KV, KHI = QFp[par], QSp[par], KVp[par], KHIp[par]

        @on("gpsimd")
        def _(e):
            if mt == 0:
                e.wait_ge(pe.sem, ev_mod)
                e.wait_ge(gstb.sem, ev_stb)
            if mt >= 3:
                e.wait_ge(dv.sem, ev_x2[mt - 3])
            gx[xpar].inc(e.dma_start(out=XB[:, sl:sl + nst1 * D].rearrange("p (s f) -> p s f", s=nst1),
                                     in_=x_in.rearrange("(m s p) f -> m p s f", s=nst1, p=128)[mt]), 16)
        ev_x[mt] = (xpar, gx[xpar].n)

        @on("vector")
        def _(e):
            e.wait_ge(gx[xpar].sem, ev_x[mt][1])
            if mt >= 1:
                e.wait_ge(ac.sem, evm[("z1", mt - 1)])
            ln_stats(e, XB[:, sl:sl + nst1 * D], nst1)
        ln_rest(nst1)
        ev_nmr = dv.n

        @on("scalar")
        def _(e):
            e.wait_ge(dv.sem, ev_nmr)
            if mt >= 1:
                e.wait_ge(pe.sem, evm[("z1busy", mt - 1)])
            for st in range(nst1):
                a = e.activation(Z1[:, st * D:(st + 1) * D], XB[:, sl + st * D:sl + (st + 1) * D],
                                 AF.Identity, bias=STAT[:, 3 * nst1 + st:3 * nst1 + st + 1],
                                 scale=STAT[:, 2 * nst1 + st:2 * nst1 + st + 1])
            ac.inc(a)
        ev_z1 = ac.n
        evm[("z1", mt)] = ac.n

        for st in range(nst1):
            @on("tensor")
            def _(e, st=st):
                e.wait_ge(ac.sem, ev_z1 if st == 0 else ac.n)
                for c in range(8):
                    t = e.transpose(TPP[:, c * 128:(c + 1) * 128],
                                    Z1[:, st * D + c * 128:st * D + (c + 1) * 128], IDN[:, 0:128])
                pe.inc(t)

            @on("scalar")
            def _(e, st=st):
                e.wait_ge(pe.sem, pe.n)
                ac.inc(e.copy(XN1T[:].rearrange("p (c t) -> p c t", c=8)[:, :, st * 128:(st + 1) * 128],
                              TPP[:].rearrange("p (c j) -> p c j", c=8)))
        evm[("z1busy", mt)] = pe.n
        ev_xn1t = ac.n

        ev_qev = [ac.n + dc + 1 for dc in range(8)]
        @on("tensor")
        def _(e, ev=ev_xn1t, dvb=dv.n):
            e.wait_ge(ac.sem, ev)
            e.wait_ge(dv.sem, ev_prescale)
            e.wait_ge(dv.sem, dvb)
            for dc in range(8):
                if dc >= 2:
                    e.wait_ge(ac.sem, ev_qev[dc - 2])
                for kc in range(8):
                    mm = e.matmul(MMP[dc % 2][:, 0:p1mt], WQ[:, kc * D + dc * 128:kc * D + dc * 128 + 128],
                                  XN1T[:, kc * p1mt:(kc + 1) * p1mt], start=(kc == 0), stop=(kc == 7))
                pe.inc(mm)
        base_q = pe.n - 8

        @on("scalar")
        def _(e, base=base_q):
            if mt >= 2:
                e.wait_ge(dv.sem, evm[("qfree_dv", mt - 2)])
            for dc in range(8):
                e.wait_ge(pe.sem, base + dc + 1)
                a = e.activation(QF[:, dc * p1mt:(dc + 1) * p1mt], MMP[dc % 2][:, 0:p1mt],
                                 AF.Identity, bias=col(dc, 1))
                ac.inc(a)
        assert ac.n == ev_qev[-1]
        ev_q = ac.n

        @on("tensor")
        def _(e, ev=ev_q, dvb=dv.n):
            e.wait_ge(ac.sem, ev)
            e.wait_ge(dv.sem, dvb)
            for dc in range(2):
                for kc in range(8):
                    mm = e.matmul(MMP[dc % 2][:, 0:p1mt], WDN[:, kc * 256 + dc * 128:kc * 256 + dc * 128 + 128],
                                  XN1T[:, kc * p1mt:(kc + 1) * p1mt], start=(kc == 0), stop=(kc == 7))
                pe.inc(mm)
        base_lat = pe.n - 2

        @on("scalar")
        def _(e, base=base_lat):
            for dc in range(2):
                e.wait_ge(pe.sem, base + dc + 1)
                a = e.activation(LATT[:, dc * p1mt:(dc + 1) * p1mt], MMP[dc % 2][:, 0:p1mt],
                                 AF.Identity, bias=col(8 + dc, 1))
                ac.inc(a)
        ev_lat = ac.n

        ev_kvev = [ac.n + gc + 1 for gc in range(16)]
        @on("tensor")
        def _(e, ev=ev_lat, dvb=dv.n):
            e.wait_ge(ac.sem, ev)
            e.wait_ge(dv.sem, dvb)
            for gc in range(16):
                if gc >= 2:
                    e.wait_ge(ac.sem, ev_kvev[gc - 2])
                for kc in range(2):
                    mm = e.matmul(MMP[gc % 2][:, 0:p1mt], WUP[:, kc * 2048 + gc * 128:kc * 2048 + gc * 128 + 128],
                                  LATT[:, kc * p1mt:(kc + 1) * p1mt], start=(kc == 0), stop=(kc == 1))
                pe.inc(mm)
        base_kv = pe.n - 16

        @on("scalar")
        def _(e, base=base_kv):
            if mt >= 2:
                e.wait_ge(dv.sem, evm[("kvfree_dv", mt - 2)])
            for gc in range(16):
                e.wait_ge(pe.sem, base + gc + 1)
                a = e.activation(KV[:, gc * p1mt:(gc + 1) * p1mt], MMP[gc % 2][:, 0:p1mt],
                                 AF.Identity, bias=col(10 + gc, 1))
                ac.inc(a)
        assert ac.n == ev_kvev[-1]
        ev_kv = ac.n
        evm[("kv", mt)] = ac.n
        evm[("q", mt)] = ev_q

        @on("sync")
        def _(e, evq=ev_q, evk=ev_kv):
            e.wait_ge(ac.sem, evq)
            if mt >= 2:
                e.wait_ge(dv.sem, evm[("qfree_dv", mt - 2)])
            sswapp[par].inc(e.dma_start(out=QS[0:64, :], in_=QF[64:128, :]), 16)
            sswapp[par].inc(e.dma_start(out=QS[64:128, :], in_=QF[0:64, :]), 16)
            e.wait_ge(ac.sem, evk)
            sswapp[par].inc(e.dma_start(out=KHI[64:128, :], in_=KV[0:64, :]), 16)
        evm[("sw", mt)] = sswapp[par].n

    def stage_b(mt):
        par = mt % 2
        sl = (mt % 3) * nst1 * D
        QF, QS, KV, KHI = QFp[par], QSp[par], KVp[par], KHIp[par]
        ev_sw = evm[("sw", mt)]
        ev_kv = evm[("kv", mt)]

        for ch in range(nch_a):
            t0 = ch * chtok
            bsl = 0
            rsl = 0
            esl = (ch % 2) * 16 * noct
            chi = mt * nch_a + ch
            cpar = chi % 2

            @on("vector")
            def _(e, t0=t0, chi=chi, ev=ev_kv):
                e.wait_ge(ac.sem, ev)
                if chi >= 1:
                    e.wait_ge(pe.sem, pv["voc_pe"])
                dv.inc(e.tensor_copy(
                    VSTG[64:128, :].rearrange("p (x g) -> p x g", g=16),
                    KV[64:128, :].rearrange("p (g t) -> p t g", g=16)[:, t0:t0 + chtok, :]))
            ev_vs_dv = dv.n

            @on("vector")
            def _(e, t0=t0, chi=chi, cpar=cpar):
                if chi == 0:
                    e.wait_ge(scon.sem, ev_consts)
                    e.wait_ge(pe.sem, ev_mod)
                    e.wait_ge(gstb.sem, ev_stb)
                    dv.inc(e.memset(LHS[:], 0.0))
                    e.wait_ge(dv.sem, dv.n)
                e.wait_ge(sswapp[par].sem, ev_sw)
                if chi >= 1:
                    e.wait_ge(pe.sem, pv["scores_pe"][(chi - 1) % 2])
                e.tensor_copy(LHS[0:64, bsl:bsl + 32 * npair].rearrange("p (x g) -> p x g", g=32)[:, :, 0:16],
                              KV[0:64, :].rearrange("p (g t) -> p t g", g=16)[:, t0:t0 + chtok:2, :])
                e.tensor_copy(LHS[64:128, bsl:bsl + 32 * npair].rearrange("p (x g) -> p x g", g=32)[:, :, 16:32],
                              KHI[64:128, :].rearrange("p (g t) -> p t g", g=16)[:, t0 + 1:t0 + chtok:2, :])
                e.tensor_copy(RHSQ[0:64, rsl:rsl + 16 * npair].rearrange("p (x h) -> p x h", h=16)[:, :, 0:16:2],
                              QF[0:64, :].rearrange("p (c t) -> p t c", c=8)[:, t0:t0 + chtok:2, :])
                e.tensor_copy(RHSQ[0:64, rsl:rsl + 16 * npair].rearrange("p (x h) -> p x h", h=16)[:, :, 1:16:2],
                              QS[0:64, :].rearrange("p (c t) -> p t c", c=8)[:, t0:t0 + chtok:2, :])
                e.tensor_copy(RHSQ[64:128, rsl:rsl + 16 * npair].rearrange("p (x h) -> p x h", h=16)[:, :, 0:16:2],
                              QS[64:128, :].rearrange("p (c t) -> p t c", c=8)[:, t0 + 1:t0 + chtok:2, :])
                dv.inc(e.tensor_copy(RHSQ[64:128, rsl:rsl + 16 * npair].rearrange("p (x h) -> p x h", h=16)[:, :, 1:16:2],
                                     QF[64:128, :].rearrange("p (c t) -> p t c", c=8)[:, t0 + 1:t0 + chtok:2, :]))
            ev_build = dv.n

            @on("tensor")
            def _(e, chi=chi, cpar=cpar, ev=ev_build):
                e.wait_ge(dv.sem, ev)
                if chi >= 2:
                    e.wait_ge(ac.sem, pv["exp_ac"][cpar])
                    e.wait_ge(dv.sem, pv["mx_dv"])
                for o in range(noct):
                    for j in range(4):
                        pr = 4 * o + j
                        mm = e.matmul(SCP[ch % 2][32 * j:32 * j + 32, 16 * o:16 * o + 16],
                                      LHS[:, bsl + 32 * pr:bsl + 32 * pr + 32],
                                      RHSQ[:, rsl + 16 * pr:rsl + 16 * pr + 16],
                                      start=True, stop=True, tile_position=(0, 32 * j))
                pe.inc(mm)
            ev_sc = pe.n
            pv["scores_pe"][cpar] = ev_sc

            @on("vector")
            def _(e, ch=ch, ev=ev_sc):
                e.wait_ge(pe.sem, ev)
                e.wait_ge(gmx.sem, pv["mx_gq"])
                dv.inc(e.reduce_max(MXC[:, 0:1], SCP[ch % 2][:, 0:16 * noct], axis=AX.X))
            ev_mx = dv.n
            pv["mx_dv"] = ev_mx

            @on("scalar")
            def _(e, ev=ev_mx):
                e.wait_ge(dv.sem, ev)
                gmx.inc(e.dma_start(out=MXR[0:1, 0:128], in_=MXC[:, 0:1]), 16)
            ev_mxr = gmx.n

            @on("vector")
            def _(e, ev=ev_mxr):
                e.wait_ge(gmx.sem, ev)
                dv.inc(e.tensor_reduce(MXR[0:1, 0:1], MXR[0:1, 0:128], AX.X, ALU.max, negate=True))
            ev_mneg = dv.n

            @on("scalar")
            def _(e, ev=ev_mneg):
                e.wait_ge(dv.sem, ev)
                e.wait_ge(ac.sem, pv["mxb_ac"])
                gmx.inc(e.dma_start(out=MXBIAS, in_=bcast_row(MXR[0:1, 0:1], 128)), 16)
            ev_nmx = gmx.n
            pv["mx_gq"] = gmx.n

            @on("scalar")
            def _(e, ch=ch, esl=esl, chi=chi, cpar=cpar, ev=ev_nmx):
                e.wait_ge(gmx.sem, ev)
                if chi >= 2:
                    zp, bdv, bgp = pv["ebcons"][cpar]
                    e.wait_ge(pe.sem, zp)
                    e.wait_ge(dv.sem, bdv)
                ac.inc(e.activation(EB[:, esl:esl + 16 * noct], SCP[ch % 2][:, 0:16 * noct], AF.Exp, bias=MXBIAS))
            ev_eb = ac.n
            pv["exp_ac"][cpar] = ev_eb
            pv["mxb_ac"] = ev_eb

            @on("vector")
            def _(e, esl=esl, chi=chi, ev=ev_eb):
                e.wait_ge(ac.sem, ev)
                if chi >= 1:
                    e.wait_ge(pe.sem, pv["bd_vcons_pe"]) 
                for par in range(2):
                    bb = e.tensor_tensor(
                        BD[:].rearrange("p (o a t h) -> p o a t h", o=noct, a=2, t=8)[:, :, par],
                        EB[:, esl:esl + 16 * noct].rearrange("p (o h) -> p o h", o=noct)[:, :, par::2].unsqueeze(2).broadcast_to([128, noct, 8, 8]),
                        MSK[:, 0:128].rearrange("p (t h) -> p t h", t=8)[:, :, par::2].unsqueeze(1).broadcast_to([128, noct, 8, 8]),
                        ALU.mult)
                dv.inc(bb)
            ev_bd_dv = dv.n
            ev_bd_ac = dv.n

            @on("tensor")
            def _(e, esl=esl, chi=chi, evb=ev_eb):
                e.wait_ge(ac.sem, evb)
                if chi >= 1:
                    e.wait_ge(ac.sem, pv["ztp_ac"])
                for o in range(noct):
                    mm = e.matmul(ZTP[:, 8 * o:8 * o + 8], EB[:, esl + 16 * o:esl + 16 * o + 16], BON[:],
                                  start=True, stop=True)
                pe.inc(mm)
            ev_zmm = pe.n

            @on("scalar")
            def _(e, chi=chi, ev=ev_zmm):
                e.wait_ge(pe.sem, ev)
                if chi >= 1:
                    e.wait_ge(gzf.sem, pv["zts_gq"])
                ac.inc(e.copy(ZTS[:], ZTP[:]))
            ev_zts = ac.n
            pv["ztp_ac"] = ev_zts

            @on("vector")
            def _(e, ev=ev_zts):
                e.wait_ge(ac.sem, ev)
                with nc.allow_low_precision(reason="softmax denominator reciprocal in bf16; tolerance-checked"):
                    dv.inc(e.reciprocal(ZTS[:], ZTS[:]))
            ev_zrec = dv.n

            @on("gpsimd")
            def _(e, chi=chi, ev=ev_zrec):
                e.wait_ge(dv.sem, ev)
                if chi >= 1:
                    e.wait_ge(dv.sem, pv["zfm_dv"])
                zp = 8 * noct
                for cc in range(8):
                    s2 = bass.AP(ZTS[0:1, 0:1].tensor, 2 * cc * zp, [[zp, 2], [0, 64], [1, zp]])
                    gzf.inc(e.dma_start(out=ZFM[:, cc * chtok:(cc + 1) * chtok], in_=s2), 16)
            ev_zfm = gzf.n
            pv["zts_gq"] = gzf.n

            ev_vev = []
            for w in range(nw):
                @on("tensor")
                def _(e, w=w, evd=ev_vs_dv):
                    if w == 0:
                        e.wait_ge(dv.sem, evd)
                    else:
                        e.wait_ge(ac.sem, ev_vev[w - 1])
                    for o in range(8 * w, 8 * w + 8):
                        t = e.transpose(VOP[:, (o % 8) * 64:(o % 8) * 64 + 64],
                                        VSTG[64:128, o * 128:(o + 1) * 128],
                                        IDN[64:128, 128:192])
                    pe.inc(t)

                @on("scalar")
                def _(e, w=w):
                    e.wait_ge(pe.sem, pe.n)
                    ac.inc(e.copy(VOC[:, w * 512:(w + 1) * 512], VOP[:]))
                ev_vev.append(ac.n)

            ev_o2w = []
            for w in range(nw):
                @on("tensor")
                def _(e, w=w, chi=chi, evd=ev_bd_ac):
                    O2P = X2P
                    e.wait_ge(ac.sem, ev_vev[w])
                    if w == 0:
                        e.wait_ge(dv.sem, evd)
                        if chi >= 1:
                            e.wait_ge(dv.sem, pv["o2_dv"])
                    else:
                        e.wait_ge(dv.sem, ev_o2w[w - 1])
                    for o in range(8 * w, 8 * w + 8):
                        off = (o % 8) * 64
                        e.matmul(O2P[0:64, off:off + 64], VOC[:, o * 64:o * 64 + 64],
                                 BD[:, o * 128:o * 128 + 64],
                                 start=True, stop=True)
                        mm = e.matmul(O2P[64:128, off:off + 64], VOC[:, o * 64:o * 64 + 64],
                                      BD[:, o * 128 + 64:o * 128 + 128],
                                      start=True, stop=True)
                    pe.inc(mm)

                @on("vector")
                def _(e, t0=t0, w=w, chi=chi, evz=ev_zfm):
                    O2P = X2P
                    e.wait_ge(gzf.sem, evz)
                    e.wait_ge(pe.sem, pe.n)
                    for half in range(2):
                        a = e.scalar_tensor_tensor(
                            AOUT[half * 64:(half + 1) * 64, :].rearrange("p (c t) -> p c t", c=8)[:, :, t0 + w * 64:t0 + (w + 1) * 64],
                            O2P[half * 64:(half + 1) * 64, 0:512].rearrange("p (x h) -> p h x", h=8),
                            1.0,
                            ZFM[half * 64:(half + 1) * 64, :].rearrange("p (c t) -> p c t", c=8)[:, :, w * 64:(w + 1) * 64],
                            ALU.mult, ALU.mult)
                    dv.inc(a)
                ev_o2w.append(dv.n)
            pv["zfm_dv"] = dv.n
            pv["voc_pe"] = pe.n
            pv["bd_vcons_pe"] = pe.n
            pv["ebcons"][cpar] = (ev_zmm, ev_bd_dv, 0)
        ev_aout = dv.n
        evm[("qfree_dv", mt)] = dv.n
        evm[("kvfree_dv", mt)] = dv.n

        ev_y1ev = [dv.n + dc + 1 for dc in range(8)]
        @on("tensor")
        def _(e, ev=ev_aout, acb=ac.n):
            e.wait_ge(dv.sem, ev)
            e.wait_ge(ac.sem, acb)
            for dc in range(8):
                if dc >= 2:
                    e.wait_ge(dv.sem, ev_y1ev[dc - 2])
                for kc in range(8):
                    mm = e.matmul(MMP[dc % 2][:, 0:p1mt], WO[:, kc * D + dc * 128:kc * D + dc * 128 + 128],
                                  AOUT[:, kc * p1mt:(kc + 1) * p1mt], start=(kc == 0), stop=(kc == 7))
                pe.inc(mm)
        base_wo = pe.n - 8

        @on("vector")
        def _(e, base=base_wo):
            for dc in range(8):
                e.wait_ge(pe.sem, base + dc + 1)
                a = e.tensor_scalar(Y1B[:, dc * p1mt:(dc + 1) * p1mt], MMP[dc % 2][:, 0:p1mt],
                                    col(26 + dc, 1), None, ALU.add)
                dv.inc(a)
        assert dv.n == ev_y1ev[-1]
        ev_y1 = dv.n

        ev_x2w = []
        for st in range(nst1):
            for hb in range(2):
                @on("tensor")
                def _(e, st=st, hb=hb, ev=ev_y1):
                    if st == 0 and hb == 0:
                        e.wait_ge(dv.sem, ev)
                    else:
                        e.wait_ge(dv.sem, ev_x2w[-1])
                    for c in range(4 * hb, 4 * hb + 4):
                        mm = e.matmul(X2P[:, (c % 4) * 128:(c % 4) * 128 + 128],
                                      Y1B[:, c * p1mt + st * 128:c * p1mt + st * 128 + 128],
                                      DGA[:, c * 128:(c + 1) * 128], start=True, stop=True)
                    pe.inc(mm)

                @on("vector")
                def _(e, st=st, hb=hb):
                    e.wait_ge(pe.sem, pe.n)
                    if mt >= 1 and st == 0 and hb == 0:
                        e.wait_ge(sspill.sem, pv["x2b_sy"])
                    a = e.scalar_tensor_tensor(X2B[:, st * D + hb * 512:st * D + (hb + 1) * 512],
                                               X2P[:], 1.0,
                                               XB[:, sl + st * D + hb * 512:sl + st * D + (hb + 1) * 512],
                                               ALU.mult, ALU.add)
                    dv.inc(a)
                ev_x2w.append(dv.n)
        ev_x2[mt] = dv.n
        pv["o2_dv"] = dv.n

        @on("sync")
        def _(e, ev=ev_x2[mt]):
            e.wait_ge(dv.sem, ev)
            sspill.inc(e.dma_start(out=x2s.rearrange("(m s p) f -> m p s f", s=nst1, p=128)[mt],
                                   in_=X2B[:].rearrange("p (s f) -> p s f", s=nst1)), 16)
        ev_spill[mt] = sspill.n
        pv["x2b_sy"] = sspill.n

    if n1 >= 2:
        stage_a(0)
        stage_a(1)
        for m in range(2, n1):
            stage_b(m - 2)
            stage_a(m)
        stage_b(n1 - 2)
        stage_b(n1 - 1)
    else:
        stage_a(0)
        stage_b(0)

    ev_p1done = sspill.n
    if dbg:
        @on("gpsimd")
        def _(e):
            e.wait_ge(sspill.sem, ev_p1done)
            gdbg.inc(e.dma_start(out=dbg_t["d_h1t"][:], in_=XN1T[:]), 16)
            gdbg.inc(e.dma_start(out=dbg_t["d_q"][:], in_=QF[:]), 16)
            gdbg.inc(e.dma_start(out=dbg_t["d_kv"][:], in_=KV[:]), 16)
            gdbg.inc(e.dma_start(out=dbg_t["d_ao"][:], in_=AOUT[:]), 16)
            gdbg.inc(e.dma_start(out=dbg_t["d_x2"].rearrange("(m s p) f -> m p s f", s=nst1, p=128)[n1 - 1],
                               in_=X2B[:].rearrange("p (s f) -> p s f", s=nst1)), 16)

    ev_p1done = sspill.n
    ev_p1done_gq = gdbg.n
    es_p1.close()
    es_p1w.close()
    es_p2 = ExitStack()
    S2 = lambda nm, sh, dt: es_p2.enter_context(nc.sbuf_tensor(nm, sh, dt))
    X2L = S2("X2L", [128, mt_hg * nst2 * D], BF16)
    Z2P = [S2("Z2a", [128, nst2 * D], BF16), S2("Z2b", [128, nst2 * D], BF16)]
    H2P = [S2("H2a", [128, nst2 * D], BF16), S2("H2b", [128, nst2 * D], BF16)]
    XN3T = S2("XN3T", [128, 8 * p2mt], BF16)
    X2T = S2("X2T", [128, 8 * p2mt], BF16)
    XAT = S2("XAT", [128, mt_hg * 8 * p2mt], BF16)
    W1S = S2("W1S", [128, 8 * 1024], BF16)
    W2S = S2("W2S", [128, 8 * 1024], BF16)
    HQ = S2("HQ", [128, 8 * p2mt], BF16)
    EO = S2("EO", [128, mt_hg * 8 * p2mt], F32)
    EOB = S2("EOB", [128, mt_hg * 8 * p2mt], BF16)
    FB = Z2P[0]
    OUTB = S2("OUTB", [128, D], F32)
    P2_ = lambda nm, sh, dt: es_p2.enter_context(nc.psum_tensor(nm, sh, dt))
    MM2 = [MMP[0], MMP[1], P2_("MM2a", [128, 512], F32), P2_("MM2b", [128, 512], F32)]

    p2 = dict(x2l_dv=0, w1_pe=0, w2_pe=0, outb_sy=0, z2_dv=0, h2_ac=0,
              eo_last={}, eob_pe={}, statfree_ac=0)

    for hg in range(n_hg):
        eidx = (hg * mt_hg * p2mt * 2) // T
        ev_xat = [0] * mt_hg
        if "ch_ev" not in p2:
            p2["ch_ev"] = {}
        ch_ev = p2["ch_ev"]

        def pro_chain(mtg, hg=hg):
            mtglob = hg * mt_hg + mtg
            xsl = mtg * nst2 * D
            par = mtg % 2
            ST = STAT if par == 0 else STATB
            Z2 = Z2P[par]
            H2 = H2P[par]

            @on("sync")
            def _(e):
                if hg == 0 and mtg == 0:
                    e.wait_ge(sspill.sem, ev_p1done)
                    if dbg:
                        e.wait_ge(gdbg.sem, ev_p1done_gq)
                if hg >= 1:
                    e.wait_ge(dv.sem, p2["x2l_dv"])
                sx2l[mtg % 4].inc(e.dma_start(out=X2L[:, xsl:xsl + nst2 * D].rearrange("p (s f) -> p s f", s=nst2),
                                              in_=x2s.rearrange("(m s p) f -> m p s f", s=nst2, p=128)[mtglob]), 16)
            ev_x2l = sx2l[mtg % 4].n

            @on("vector")
            def _(e):
                e.wait_ge(sx2l[mtg % 4].sem, ev_x2l)
                e.wait_ge(gstb.sem, ev_stb)
                if ("st_free", par) in ch_ev:
                    e.wait_ge(ac.sem, ch_ev[("st_free", par)])
                ln_stats(e, X2L[:, xsl:xsl + nst2 * D], nst2, ST)
            ln_rest(nst2, ST)

            @on("scalar")
            def _(e):
                e.wait_ge(dv.sem, dv.n)
                if ("z2_free", par) in ch_ev:
                    e.wait_ge(pe.sem, ch_ev[("z2_free", par)])
                for st in range(nst2):
                    a = e.activation(Z2[:, st * D:(st + 1) * D], X2L[:, xsl + st * D:xsl + (st + 1) * D],
                                     AF.Identity, bias=ST[:, 3 * nst2 + st:3 * nst2 + st + 1],
                                     scale=ST[:, 2 * nst2 + st:2 * nst2 + st + 1])
                ac.inc(a)
            ev_z2 = ac.n

            @on("vector")
            def _(e):
                e.wait_ge(ac.sem, ev_z2)
                if ("h2_free", par) in ch_ev:
                    e.wait_ge(ac.sem, ch_ev[("h2_free", par)])
                for st in range(nst2):
                    h = e.scalar_tensor_tensor(H2[:, st * D:(st + 1) * D], Z2[:, st * D:(st + 1) * D],
                                               1.0, STB[:], ALU.mult, ALU.mult)
                dv.inc(h)
                e.wait_ge(dv.sem, dv.n)
                for st in range(nst2):
                    h = e.tensor_add(H2[:, st * D:(st + 1) * D], H2[:, st * D:(st + 1) * D], BTB[:])
                dv.inc(h)
                e.wait_ge(dv.sem, dv.n)
                ln_stats(e, H2[:, 0:nst2 * D], nst2, ST)
            ln_rest(nst2, ST)
            ev_nmr3 = dv.n

            @on("scalar")
            def _(e):
                e.wait_ge(dv.sem, ev_nmr3)
                for st in range(nst2):
                    a = e.activation(Z2[:, st * D:(st + 1) * D], H2[:, st * D:(st + 1) * D],
                                     AF.Identity, bias=ST[:, 3 * nst2 + st:3 * nst2 + st + 1],
                                     scale=ST[:, 2 * nst2 + st:2 * nst2 + st + 1])
                ac.inc(a)
            ch_ev[("xn3", mtg)] = ac.n
            ch_ev[("st_free", par)] = ac.n
            ch_ev[("h2_free", par)] = ac.n

        def pro_pe(mtg, hg=hg):
            xsl = mtg * nst2 * D
            par = mtg % 2
            XN3 = Z2P[par]
            ev_xn3 = ch_ev[("xn3", mtg)]
            for st in range(nst2):
                @on("tensor")
                def _(e, st=st):
                    e.wait_ge(ac.sem, ev_xn3 if st == 0 else ac.n)
                    for c in range(8):
                        t = e.transpose(TPP[:, c * 128:(c + 1) * 128],
                                        XN3[:, st * D + c * 128:st * D + (c + 1) * 128], IDN[:, 0:128])
                    pe.inc(t)

                @on("scalar")
                def _(e, st=st):
                    e.wait_ge(pe.sem, pe.n)
                    ac.inc(e.copy(XN3T[:].rearrange("p (c t) -> p c t", c=8)[:, :, st * 128:(st + 1) * 128],
                                  TPP[:].rearrange("p (c j) -> p c j", c=8)))
            ch_ev[("z2_free", par)] = pe.n
            for st in range(nst2):
                @on("tensor")
                def _(e, st=st):
                    e.wait_ge(ac.sem, ac.n)
                    for c in range(8):
                        t = e.transpose(TPP[:, c * 128:(c + 1) * 128],
                                        X2L[:, xsl + st * D + c * 128:xsl + st * D + (c + 1) * 128], IDN[:, 0:128])
                    pe.inc(t)

                @on("scalar")
                def _(e, st=st):
                    e.wait_ge(pe.sem, pe.n)
                    if mtg >= 1 and st == 0:
                        e.wait_ge(dv.sem, ev_xat[mtg - 1])
                    ac.inc(e.copy(X2T[:].rearrange("p (c t) -> p c t", c=8)[:, :, st * 128:(st + 1) * 128],
                                  TPP[:].rearrange("p (c j) -> p c j", c=8)))
            ev_x2t = ac.n

            ev_xaev = [dv.n + dc + 1 for dc in range(8)]
            @on("tensor")
            def _(e, ev=ev_x2t, dvb=dv.n):
                e.wait_ge(ac.sem, ev)
                e.wait_ge(dv.sem, dvb)
                for dc in range(8):
                    if dc >= 4:
                        e.wait_ge(dv.sem, ev_xaev[dc - 4])
                    for kc in range(8):
                        mm = e.matmul(MM2[dc % 4][:, 0:p2mt], AWT[:, kc * D + dc * 128:kc * D + dc * 128 + 128],
                                      XN3T[:, kc * p2mt:(kc + 1) * p2mt], start=(kc == 0), stop=(kc == 7))
                    pe.inc(mm)
            base_xa = pe.n - 8

            @on("vector")
            def _(e, mtg=mtg, base=base_xa):
                for dc in range(8):
                    e.wait_ge(pe.sem, base + dc + 1)
                    a = e.scalar_tensor_tensor(XAT[:, mtg * 8 * p2mt + dc * p2mt:mtg * 8 * p2mt + (dc + 1) * p2mt],
                                               MM2[dc % 4][:, 0:p2mt], col(34 + dc, 1),
                                               X2T[:, dc * p2mt:(dc + 1) * p2mt], ALU.add, ALU.add)
                    dv.inc(a)
            assert dv.n == ev_xaev[-1], ("xaev", dv.n, ev_xaev[-1])
            ev_xat[mtg] = dv.n

        sched = []
        for m in range(mt_hg):
            sched.append(("c", m))
        order = []
        if mt_hg >= 2:
            order = [("c", 0), ("c", 1)]
            for m in range(2, mt_hg):
                order.append(("p", m - 2))
                order.append(("c", m))
            order.append(("p", mt_hg - 2))
            order.append(("p", mt_hg - 1))
        else:
            order = [("c", 0), ("p", 0)]
        for kind, m in order:
            if kind == "c":
                pro_chain(m)
            else:
                pro_pe(m)

        for wq in range(4):
            @on("gpsimd")
            def _(e, wq=wq, eidx=eidx):
                e.wait_ge(sspill.sem, ev_p1done)
                if dbg:
                    e.wait_ge(gdbg.sem, ev_p1done_gq)
                e.wait_ge(pe.sem, p2["w1_pe"])
                for kc in range(8):
                    gw1.inc(e.dma_start(out=W1S[:, kc * 1024:(kc + 1) * 1024],
                                       in_=w1_in[eidx][128 * kc:128 * (kc + 1), 1024 * wq:1024 * (wq + 1)]), 16)
            ev_w1l = gw1.n

            @on("gpsimd")
            def _(e, wq=wq, eidx=eidx):
                e.wait_ge(pe.sem, p2["w2_pe"])
                for kc in range(8):
                    gw2.inc(e.dma_start(out=W2S[:, kc * 1024:(kc + 1) * 1024],
                                       in_=w2_in[eidx][1024 * wq + 128 * kc:1024 * wq + 128 * (kc + 1), :]), 16)
            ev_w2l = gw2.n

            for mtg in range(mt_hg):
                ev_hqev = [ac.n + hc + 1 for hc in range(8)]
                @on("tensor")
                def _(e, wq=wq, mtg=mtg, ev=ev_w1l, dvb=dv.n):
                    e.wait_ge(gw1.sem, ev)
                    e.wait_ge(dv.sem, ev_xat[mtg])
                    e.wait_ge(dv.sem, dvb)
                    for hc in range(8):
                        if hc >= 4:
                            e.wait_ge(ac.sem, ev_hqev[hc - 4])
                        for kc in range(8):
                            mm = e.matmul(MM2[hc % 4][:, 0:p2mt], W1S[:, kc * 1024 + hc * 128:kc * 1024 + hc * 128 + 128],
                                          XAT[:, mtg * 8 * p2mt + kc * p2mt:mtg * 8 * p2mt + (kc + 1) * p2mt],
                                          start=(kc == 0), stop=(kc == 7))
                        pe.inc(mm)
                base_h = pe.n - 8
                p2["w1_pe"] = pe.n if (mtg == mt_hg - 1) else p2["w1_pe"]

                @on("scalar")
                def _(e, wq=wq, mtg=mtg, eidx=eidx, base=base_h):
                    for hc in range(8):
                        e.wait_ge(pe.sem, base + hc + 1)
                        a = e.activation(HQ[:, hc * p2mt:(hc + 1) * p2mt],
                                         MM2[hc % 4][:, 0:p2mt], AF.Gelu,
                                         bias=B1C[:, eidx * 32 + wq * 8 + hc:eidx * 32 + wq * 8 + hc + 1])
                        ac.inc(a)
                assert ac.n == ev_hqev[-1], ("hqev", ac.n, ev_hqev[-1])
                ev_hq = ac.n

                ev_eoev = [dv.n + dc + 1 for dc in range(8)]
                @on("tensor")
                def _(e, wq=wq, mtg=mtg, ev2=ev_w2l, evh=ev_hq, dvb=dv.n):
                    e.wait_ge(gw2.sem, ev2)
                    e.wait_ge(ac.sem, evh)
                    e.wait_ge(dv.sem, dvb)
                    for dc in range(8):
                        if dc >= 4:
                            e.wait_ge(dv.sem, ev_eoev[dc - 4])
                        for kc in range(8):
                            mm = e.matmul(MM2[dc % 4][:, 0:p2mt], W2S[:, kc * 1024 + dc * 128:kc * 1024 + dc * 128 + 128],
                                          HQ[:, kc * p2mt:(kc + 1) * p2mt],
                                          start=(kc == 0), stop=(kc == 7))
                        pe.inc(mm)
                base_eo = pe.n - 8
                p2["w2_pe"] = pe.n if (mtg == mt_hg - 1) else p2["w2_pe"]

                @on("vector")
                def _(e, wq=wq, mtg=mtg, eidx=eidx, base=base_eo):
                    for dc in range(8):
                        e.wait_ge(pe.sem, base + dc + 1)
                        eo_sl = EO[:, mtg * 8 * p2mt + dc * p2mt:mtg * 8 * p2mt + (dc + 1) * p2mt]
                        key = (mtg, dc)
                        if wq == 0:
                            if key in p2["eo_last"]:
                                e.wait_ge(dv.sem, p2["eo_last"][key])
                            a = e.tensor_copy(eo_sl, MM2[dc % 4][:, 0:p2mt])
                        elif wq < 3:
                            e.wait_ge(dv.sem, p2["eo_last"][key])
                            a = e.tensor_add(eo_sl, eo_sl, MM2[dc % 4][:, 0:p2mt])
                        else:
                            e.wait_ge(dv.sem, p2["eo_last"][key])
                            if key in p2["eob_pe"]:
                                e.wait_ge(pe.sem, p2["eob_pe"][key])
                            eob_sl = EOB[:, mtg * 8 * p2mt + dc * p2mt:mtg * 8 * p2mt + (dc + 1) * p2mt]
                            a = e.scalar_tensor_tensor(eob_sl, MM2[dc % 4][:, 0:p2mt],
                                                       B2C[:, eidx * 8 + dc:eidx * 8 + dc + 1],
                                                       eo_sl, ALU.add, ALU.add)
                        dv.inc(a)
                        p2["eo_last"][key] = dv.n
                    assert dv.n == ev_eoev[-1], ("eoev", dv.n, ev_eoev[-1])

        for mtg in range(mt_hg):
            mtglob = hg * mt_hg + mtg
            xsl = mtg * nst2 * D
            ev_fbev = [ac.n + dc + 1 for dc in range(8)]
            @on("tensor")
            def _(e, mtg=mtg, dvb=dv.n, acb=ac.n):
                e.wait_ge(dv.sem, dvb)
                e.wait_ge(ac.sem, acb)
                for dcc in range(8):
                    e.wait_ge(dv.sem, p2["eo_last"][(mtg, dcc)])
                for dc in range(8):
                    if dc >= 2:
                        e.wait_ge(ac.sem, ev_fbev[dc - 2])
                    for kc in range(8):
                        mm = e.matmul(MMP[dc % 2][:, 0:p2mt], NWT[:, kc * D + dc * 128:kc * D + dc * 128 + 128],
                                      EOB[:, mtg * 8 * p2mt + kc * p2mt:mtg * 8 * p2mt + (kc + 1) * p2mt],
                                      start=(kc == 0), stop=(kc == 7))
                    pe.inc(mm)
            base_f = pe.n - 8
            for dcc in range(8):
                p2["eob_pe"][(mtg, dcc)] = pe.n

            @on("scalar")
            def _(e, base=base_f):
                for dc in range(8):
                    e.wait_ge(pe.sem, base + dc + 1)
                    a = e.activation(FB[:, dc * p2mt:(dc + 1) * p2mt], MMP[dc % 2][:, 0:p2mt],
                                     AF.Gelu, bias=col(42 + dc, 1))
                    ac.inc(a)
            assert ac.n == ev_fbev[-1], ("fbev", ac.n, ev_fbev[-1])
            ev_fb = ac.n

            ev_ow = []
            for st in range(nst2):
                for hb in range(2):
                    @on("tensor")
                    def _(e, st=st, hb=hb, ev=ev_fb):
                        if st == 0 and hb == 0:
                            e.wait_ge(ac.sem, ev)
                        else:
                            e.wait_ge(dv.sem, ev_ow[-1])
                        for c in range(4 * hb, 4 * hb + 4):
                            mm = e.matmul(X2P[:, (c % 4) * 128:(c % 4) * 128 + 128],
                                          FB[:, c * p2mt + st * 128:c * p2mt + st * 128 + 128],
                                          DGB[:, c * 128:(c + 1) * 128], start=True, stop=True)
                        pe.inc(mm)

                    @on("vector")
                    def _(e, st=st, hb=hb, xsl=xsl):
                        e.wait_ge(pe.sem, pe.n)
                        if p2["outb_sy"]:
                            e.wait_ge(sout.sem, p2["outb_sy"])
                        a = e.scalar_tensor_tensor(OUTB[:, hb * 512:(hb + 1) * 512],
                                                   X2P[:], 1.0,
                                                   X2L[:, xsl + st * D + hb * 512:xsl + st * D + (hb + 1) * 512],
                                                   ALU.mult, ALU.add)
                        dv.inc(a)
                    ev_ow.append(dv.n)

                @on("sync")
                def _(e, mtglob=mtglob, st=st, ev=dv.n):
                    e.wait_ge(dv.sem, ev)
                    sout.inc(e.dma_start(
                        out=y_out.rearrange("(m s p) f -> m s p f", s=nst2, p=128)[mtglob, st],
                        in_=OUTB[:]), 16)
                p2["outb_sy"] = sout.n
            ev_out = dv.n
        p2["x2l_dv"] = dv.n

        if dbg and hg == 0:
            @on("gpsimd")
            def _(e):
                e.wait_ge(dv.sem, dv.n)
                gdbg.inc(e.dma_start(out=dbg_t["d_xat"][:], in_=XAT[:, 0:8 * p2mt]), 16)
                gdbg.inc(e.dma_start(out=dbg_t["d_eo"][:], in_=EO[:, 0:8 * p2mt]), 16)
                gdbg.inc(e.dma_start(out=dbg_t["d_eob"][:], in_=EOB[:, 0:8 * p2mt]), 16)
                gdbg.inc(e.dma_start(out=dbg_t["d_fb"][:], in_=FB[:]), 16)
                gdbg.inc(e.dma_start(out=dbg_t["d_hq"][:], in_=HQ[:]), 16)

    @on("sync")
    def _(e):
        e.wait_ge(sout.sem, sout.n)
        e.wait_ge(sspill.sem, sspill.n)
    @on("gpsimd")
    def _(e):
        e.wait_ge(gw1.sem, gw1.n)
        e.wait_ge(gw2.sem, gw2.n)
        e.wait_ge(gdbg.sem, gdbg.n)
        e.wait_ge(gp_.sem, gp_.n)
    @on("vector")
    def _(e):
        e.wait_ge(sout.sem, sout.n)

    es_p2.close()
    es.close()
    return nc


_NC_CACHE = {}


def _get_nc():
    key = "real"
    if key not in _NC_CACHE:
        _NC_CACHE[key] = build(T=4096, p1mt=256, p2mt=256, mt_per_hg=4, dbg=False)
    return _NC_CACHE[key]


def kernel(**inputs):
    from concourse.bass_utils import run_bass_kernel_spmd

    consts = host_consts()
    nc = _get_nc()

    awt = np.ascontiguousarray(inputs["L0_ffn_attn_w"].T)
    nwt = np.ascontiguousarray(inputs["L0_ffn_next_w"].T)
    x = np.ascontiguousarray(inputs["L0_attn_x"])
    w1 = inputs["L0_ffn_w1"]
    b1 = inputs["L0_ffn_b1"]
    w2 = inputs["L0_ffn_w2"]
    b2 = inputs["L0_ffn_b2"]

    shared = {
        "wq": np.ascontiguousarray(inputs["L0_attn_Wq"]),
        "bq": np.ascontiguousarray(inputs["L0_attn_bq"]),
        "wdown": np.ascontiguousarray(inputs["L0_attn_Wdown"]),
        "bdown": np.ascontiguousarray(inputs["L0_attn_bdown"]),
        "wup": np.ascontiguousarray(inputs["L0_attn_Wup"]),
        "bup": np.ascontiguousarray(inputs["L0_attn_bup"]),
        "wout": np.ascontiguousarray(inputs["L0_attn_Wout"]),
        "bout": np.ascontiguousarray(inputs["L0_attn_bout"]),
        "ln_scale": np.ascontiguousarray(inputs["L0_ffn_ln_scale"]),
        "ln_bias": np.ascontiguousarray(inputs["L0_ffn_ln_bias"]),
        "attn_wT": awt,
        "attn_b": np.ascontiguousarray(inputs["L0_ffn_attn_b"]),
        "next_wT": nwt,
        "next_b": np.ascontiguousarray(inputs["L0_ffn_next_b"]),
        "wmod": np.ascontiguousarray(inputs["L0_wmod"]),
    }
    shared.update(consts)

    in_maps = []
    for cid in range(8):
        b, half = cid // 2, cid % 2
        e0 = 4 * (b % 2) + 2 * half
        m = dict(shared)
        m["x"] = np.ascontiguousarray(x[b, half * 4096:(half + 1) * 4096, :])
        m["cb_"] = np.ascontiguousarray(inputs["L0_c"][b])
        m["w1a"] = np.ascontiguousarray(w1[e0])
        m["w1b"] = np.ascontiguousarray(w1[e0 + 1])
        m["b1a"] = np.ascontiguousarray(b1[e0])
        m["b1b"] = np.ascontiguousarray(b1[e0 + 1])
        m["w2a"] = np.ascontiguousarray(w2[e0])
        m["w2b"] = np.ascontiguousarray(w2[e0 + 1])
        m["b2a"] = np.ascontiguousarray(b2[e0])
        m["b2b"] = np.ascontiguousarray(b2[e0 + 1])
        in_maps.append(m)

    res = run_bass_kernel_spmd(nc, in_maps, core_ids=list(range(8)))
    out = np.empty((4, 8192, 1024), np.float32)
    for cid in range(8):
        b, half = cid // 2, cid % 2
        out[b, half * 4096:(half + 1) * 4096, :] = res.results[cid]["y"]
    return out
